# version 05
# baseline (speedup 1.0000x reference)
import numpy as np
import ml_dtypes
from contextlib import ExitStack
import concourse.bass as bass
import concourse.mybir as mybir

F32 = mybir.dt.float32
BF16 = mybir.dt.bfloat16
U8 = mybir.dt.uint8
AF = mybir.ActivationFunctionType
ALU = mybir.AluOpType
AX = mybir.AxisListType

ENGS = ("sync", "scalar", "vector", "tensor", "gpsimd")


class Prog:

    def __init__(self, name="prog"):
        self.nc = bass.Bass("TRN2", target_bir_lowering=False)
        self.st = ExitStack()
        self.ops = {e: [] for e in ENGS}
        self.semv = {}
        self.sems = {}
        self.name = name

    def sem(self, name):
        self.sems[name] = self.st.enter_context(self.nc.semaphore(name))
        self.semv[name] = 0
        return name

    def sbuf(self, name, shape, dt):
        return self.st.enter_context(self.nc.sbuf_tensor(name, shape, dt))

    def psum(self, name, shape, dt=F32):
        return self.st.enter_context(self.nc.psum_tensor(name, shape, dt))

    def wait(self, eng, sem, val=None):
        v = self.semv[sem] if val is None else val
        if v > 0:
            self.ops[eng].append(("w", sem, v))

    def op(self, eng, fn, inc=None, k=1):
        if inc is not None:
            self.semv[inc] += k
        self.ops[eng].append(("o", fn, inc, k,
                              self.semv[inc] if inc is not None else 0))

    def dma(self, eng, out, in_, sem, slow=False):
        if self.semv[sem] > 0:
            self.ops[eng].append(("w", sem, self.semv[sem]))
        self.semv[sem] += 16
        o, i = out, in_
        self.ops[eng].append(("o", lambda e: e.dma_start(
            out=o, in_=i, allow_slow_non_contiguous=slow), sem, 16, None))

    def build(self):
        with self.nc.Block() as block:
            for eng in ENGS:
                if not self.ops[eng]:
                    continue
                items = self.ops[eng]

                auto_self = eng in ("vector", "scalar", "gpsimd")
                own = None
                own_count = [0]
                if auto_self:
                    own = self.st.enter_context(self.nc.semaphore(f"own_{eng}"))

                def body(h, items=items, auto_self=auto_self, own=own,
                         own_count=own_count):
                    hwm = {}
                    pending = None
                    for it in items:
                        if it[0] == "w":
                            _, s, v = it
                            if hwm.get(s, -1) < v:
                                h.wait_ge(self.sems[s], v)
                                hwm[s] = v
                        else:
                            _, fn, inc, k, postv = it
                            if auto_self and pending is not None:
                                ps, pv = pending
                                key = ps if isinstance(ps, str) else "__own__"
                                if hwm.get(key, -1) < pv:
                                    hh = self.sems[ps] if isinstance(ps, str) else ps
                                    h.wait_ge(hh, pv)
                                    hwm[key] = pv
                                pending = None
                            instr = fn(h)
                            if inc is not None:
                                instr.then_inc(self.sems[inc], k)
                                if auto_self:
                                    pending = (inc, postv) if postv is not None else None
                            elif auto_self:
                                own_count[0] += 1
                                instr.then_inc(own, 1)
                                pending = (own, own_count[0])

                getattr(block, eng)(body)
        self.st.close()
        return self.nc


def np_bf16(x):
    return np.ascontiguousarray(x).astype(ml_dtypes.bfloat16)


class NS:
    pass



def emit_ln(p, s, pa, pb, T, nch, eps=1e-5, rms=False):
    D = nch * 128
    nth = T // 512
    for c in range(nch):
        sl = slice(c * T, (c + 1) * T)
        p.op("vector", lambda h, sl=sl: h.tensor_tensor(
            out=s.xn[:, sl], in0=s.xb[:, sl], in1=s.xb[:, sl], op=ALU.mult),
            inc="dv" if c == nch - 1 else None)
    for th in range(nth):
        p.wait("tensor", "dv")
        for c in range(nch):
            k0 = c * T + th * 512
            if not rms:
                p.op("tensor", lambda h, k0=k0, st=(c == 0), sp=(c == nch - 1): h.matmul(
                    pa[0:1, :], s.ones_l[:, 0:1], s.xb[:, k0:k0 + 512], start=st, stop=sp))
            p.op("tensor", lambda h, k0=k0, st=(c == 0), sp=(c == nch - 1): h.matmul(
                pb[0:1, :], s.ones_l[:, 0:1], s.xn[:, k0:k0 + 512], start=st, stop=sp),
                inc="pe" if c == nch - 1 else None)
        tsl = slice(th * 512, th * 512 + 512)
        p.wait("vector", "pe")
        if not rms:
            p.op("vector", lambda h, tsl=tsl: h.tensor_scalar(
                out=s.r0[0:1, tsl], in0=pa[0:1, :], scalar1=1.0 / D, scalar2=None,
                op0=ALU.mult))
        p.op("vector", lambda h, tsl=tsl: h.tensor_scalar(
            out=s.r1[0:1, tsl], in0=pb[0:1, :], scalar1=1.0 / D, scalar2=None,
            op0=ALU.mult), inc="dv")
    if not rms:
        p.op("vector", lambda h: h.tensor_tensor(
            out=s.r2[0:1, 0:T], in0=s.r0[0:1, 0:T], in1=s.r0[0:1, 0:T], op=ALU.mult))
        p.op("vector", lambda h: h.tensor_tensor(
            out=s.r1[0:1, 0:T], in0=s.r1[0:1, 0:T], in1=s.r2[0:1, 0:T],
            op=ALU.subtract), inc="dv")
    p.wait("scalar", "dv")
    p.op("scalar", lambda h: h.activation(out=s.r2[0:1, 0:T], in_=s.r1[0:1, 0:T],
                                          func=AF.Ln, bias=s.epsc[0:1, 0:1]))
    p.op("scalar", lambda h: h.activation(out=s.r3[0:1, 0:T], in_=s.r2[0:1, 0:T],
                                          func=AF.Exp, scale=-0.5), inc="sc")
    for th in range(nth):
        p.wait("tensor", "sc")
        p.wait("tensor", "dv")
        sl = slice(th * 512, th * 512 + 512)
        if not rms:
            p.op("tensor", lambda h, sl=sl: h.matmul(
                pa[:, 0:512], s.ones_c[0:1, :], s.r0[0:1, sl], start=True, stop=True))
        p.op("tensor", lambda h, sl=sl: h.matmul(
            pb[:, 0:512], s.ones_c[0:1, :], s.r3[0:1, sl], start=True, stop=True),
            inc="pe")
        p.wait("vector", "pe")
        if not rms:
            p.op("vector", lambda h, sl=sl: h.tensor_copy(out=s.mb[:, sl], in_=pa[:, 0:512]))
        p.op("vector", lambda h, sl=sl: h.tensor_copy(out=s.rb[:, sl], in_=pb[:, 0:512]),
             inc="dv")
    for c in range(nch):
        sl = slice(c * T, (c + 1) * T)
        if rms:
            p.op("vector", lambda h, sl=sl: h.tensor_tensor(
                out=s.xn[:, sl], in0=s.xb[:, sl], in1=s.rb[:, 0:T], op=ALU.mult),
                inc="dv" if c == nch - 1 else None)
        else:
            p.op("vector", lambda h, sl=sl: h.tensor_tensor(
                out=s.xn[:, sl], in0=s.xb[:, sl], in1=s.mb[:, 0:T], op=ALU.subtract))
            p.op("vector", lambda h, sl=sl: h.tensor_tensor(
                out=s.xn[:, sl], in0=s.xn[:, sl], in1=s.rb[:, 0:T], op=ALU.mult),
                inc="dv" if c == nch - 1 else None)


def emit_load_x_as_bf16(p, s, xT_ap, T, nch):
    conv_dv = {}
    for c in range(nch):
        slot = c % 2
        if c >= 2:
            p.wait("sync", "dv", conv_dv[c - 2])
        p.dma("sync", s.xtmp[:, slot * T:(slot + 1) * T],
              xT_ap[c * 128:(c + 1) * 128, :], f"ld{slot}")
        p.wait("vector", f"ld{slot}")
        p.op("vector", lambda h, c=c, slot=slot: h.tensor_copy(
            out=s.xb[:, c * T:(c + 1) * T], in_=s.xtmp[:, slot * T:(slot + 1) * T]),
            inc="dv")
        conv_dv[c] = p.semv["dv"]



def build_ghost(TT=1024, debug=False):
    D, I5, NCH, IC = 2048, 5504, 16, 43
    p = Prog("ghost")
    nc = p.nc

    xT = nc.dram_tensor("xT", [D, TT], F32, kind="ExternalInput").ap()
    gidxT = nc.dram_tensor("gidxT", [D, I5], U8, kind="ExternalInput").ap()
    uidxT = nc.dram_tensor("uidxT", [D, I5], U8, kind="ExternalInput").ap()
    didxT = nc.dram_tensor("didxT", [I5, D], U8, kind="ExternalInput").ap()
    lut = nc.dram_tensor("lut", [1, 16], F32, kind="ExternalInput").ap()
    yT = nc.dram_tensor("yT", [D, TT], F32, kind="ExternalOutput").ap()

    if debug:
        dbg_xn = nc.dram_tensor("dbg_xn", [128, 16 * TT], BF16, kind="ExternalOutput").ap()
        dbg_wg = nc.dram_tensor("dbg_wg", [256, I5], BF16, kind="ExternalOutput").ap()
        dbg_wd = nc.dram_tensor("dbg_wd", [256, D], BF16, kind="ExternalOutput").ap()
        dbg_h = nc.dram_tensor("dbg_h", [128, 43 * 512], BF16, kind="ExternalOutput").ap()
    WgS = nc.dram_tensor("WgS", [D, I5], BF16).ap()
    WuS = nc.dram_tensor("WuS", [D, I5], BF16).ap()
    WdS = nc.dram_tensor("WdS", [I5, D], BF16).ap()

    for sn in ("ld0", "ld1", "ldc", "wd0", "wd1", "wdc0", "wdc1", "wde0", "wde1",
               "cv", "dq", "so0", "so1", "pg", "ev", "dv", "pe", "sc", "yo0", "yo1"):
        p.sem(sn)

    s = NS()
    AB = 70 * 1024
    s.A = p.sbuf("A", [128, AB], BF16)
    s.xb = s.A[:, 0:NCH * TT]
    s.xn = s.A[:, NCH * TT:2 * NCH * TT]
    o_deq = 2 * NCH * TT
    idxb = [s.A[:, o_deq + k * I5: o_deq + (k + 1) * I5] for k in range(2)]
    accb = [s.A[:, o_deq + (2 + k) * I5: o_deq + (3 + k) * I5] for k in range(2)]
    tmpb = s.A[:, o_deq + 4 * I5: o_deq + 5 * I5]
    hb = s.A[:, o_deq: o_deq + IC * 512]
    o_w = o_deq + 5 * I5
    wg_l = [s.A[:, o_w + k * 2048: o_w + (k + 1) * 2048] for k in range(2)]
    wu_l = [s.A[:, o_w + (2 + k) * 2048: o_w + (3 + k) * 2048] for k in range(2)]
    o_wd = o_w + 4 * 2048
    wd_l = [s.A[:, o_wd + k * 512: o_wd + (k + 1) * 512] for k in range(2)]
    assert o_wd + 2 * 512 <= AB

    idx_u8 = p.sbuf("idx_u8", [128, 2 * I5], U8)
    s.xtmp = p.sbuf("xtmp", [128, 2 * TT], F32)
    s.ones_l = p.sbuf("ones_l", [128, 1], BF16)
    s.ones_c = p.sbuf("ones_c", [1, 128], F32)
    s.r0 = p.sbuf("r0", [1, TT], F32)
    s.r1 = p.sbuf("r1", [1, TT], F32)
    s.r2 = p.sbuf("r2", [1, TT], F32)
    s.r3 = p.sbuf("r3", [1, TT], F32)
    s.mb = p.sbuf("mb", [128, TT], F32)
    s.rb = p.sbuf("rb", [128, TT], F32)
    s.epsc = p.sbuf("epsc", [128, 1], F32)
    lutr = p.sbuf("lutr", [1, 16], F32)
    lut128 = p.sbuf("lut128", [128, 16], F32)
    gsil = p.sbuf("gsil", [128, 512], F32)
    xq = p.sbuf("xq", [128, 2 * 512], F32)
    oute = p.sbuf("oute", [128, 2 * 512], F32)

    P = [p.psum(f"P{i}", [128, 512]) for i in range(8)]
    pa, pb = P[0], P[1]

    p.dma("sync", lutr[:], lut, "ldc")
    p.op("vector", lambda h: h.memset(s.ones_l[:], 1.0))
    p.op("vector", lambda h: h.memset(s.epsc[:], 1e-5))
    p.op("vector", lambda h: h.memset(s.ones_c[:], 1.0), inc="dv")
    p.wait("tensor", "ldc")
    p.wait("tensor", "dv")
    p.op("tensor", lambda h: h.matmul(pa[:, 0:16], s.ones_c[0:1, :], lutr[0:1, :],
                                      start=True, stop=True), inc="pe")
    p.wait("vector", "pe")
    p.op("vector", lambda h: h.tensor_copy(out=lut128[:], in_=pa[:, 0:16]), inc="dv")
    emit_load_x_as_bf16(p, s, xT, TT, NCH)

    emit_ln(p, s, pa, pb, TT, NCH)

    jobs = []
    for i in range(16):
        jobs.append((gidxT[i * 128:(i + 1) * 128, :], I5, WgS[i * 128:(i + 1) * 128, :]))
    for i in range(16):
        jobs.append((uidxT[i * 128:(i + 1) * 128, :], I5, WuS[i * 128:(i + 1) * 128, :]))
    for i in range(IC):
        jobs.append((didxT[i * 128:(i + 1) * 128, :], D, WdS[i * 128:(i + 1) * 128, :]))
    n_jobs = len(jobs)
    for sidx, (src, cols, dst) in enumerate(jobs):
        sl = sidx % 2
        if sidx >= 2:
            p.wait("sync", "dq", sidx - 1)
        p.dma("sync", idx_u8[:, sl * I5: sl * I5 + cols], src, f"wdc{sl}")
        p.wait("gpsimd", f"wdc{sl}")
        p.op("gpsimd", lambda h, sl=sl, cols=cols: h.tensor_copy(
            out=idxb[sl][:, 0:cols], in_=idx_u8[:, sl * I5: sl * I5 + cols]), inc="cv")
        p.wait("vector", "cv")
        if sidx >= 2:
            p.wait("vector", f"so{sl}")
        p.op("vector", lambda h, sl=sl, cols=cols: h.tensor_scalar(
            out=accb[sl][:, 0:cols], in0=idxb[sl][:, 0:cols], scalar1=0.0,
            scalar2=lut128[:, 0:1], op0=ALU.is_equal, op1=ALU.mult))
        for v in range(1, 16):
            p.op("vector", lambda h, sl=sl, cols=cols, v=v: h.tensor_scalar(
                out=tmpb[:, 0:cols], in0=idxb[sl][:, 0:cols], scalar1=float(v),
                scalar2=lut128[:, v:v + 1], op0=ALU.is_equal, op1=ALU.mult))
            p.op("vector", lambda h, sl=sl, cols=cols: h.tensor_tensor(
                out=accb[sl][:, 0:cols], in0=accb[sl][:, 0:cols], in1=tmpb[:, 0:cols],
                op=ALU.add), inc="dq" if v == 15 else None)
        p.wait("gpsimd", "dq")
        p.dma("gpsimd", dst, accb[sl][:, 0:cols], f"so{sl}")

    if debug:
        p.wait("gpsimd", "dv")
        p.dma("gpsimd", dbg_xn, s.xn, "yo0")
        p.wait("gpsimd", "so", 75 * 16)
        p.dma("gpsimd", dbg_wg, WgS[0:256, :], "yo0")
        p.dma("gpsimd", dbg_wd, WdS[0:256, :], "yo0")

    guard = {}

    for th in range(2):
        t0 = th * 512
        if th == 1:
            p.wait("vector", "pg")
        p.wait("sync", "so0")
        p.wait("sync", "so1")
        for ic in range(IC):
            sl = ic % 2
            if ("wslot", sl) in guard:
                p.wait("sync", "pg", guard[("wslot", sl)])
            p.dma("sync", wg_l[sl].rearrange("p (k i) -> p k i", k=16),
                  WgS[:, ic * 128:(ic + 1) * 128].rearrange("(k p) i -> p k i", p=128),
                  f"wd{sl}")
            p.dma("sync", wu_l[sl].rearrange("p (k i) -> p k i", k=16),
                  WuS[:, ic * 128:(ic + 1) * 128].rearrange("(k p) i -> p k i", p=128),
                  f"wd{sl}")
            bg, bu = 2 + sl, 4 + sl
            p.wait("tensor", f"wd{sl}")
            p.wait("tensor", "dv")
            if ("evac", bg) in guard:
                p.wait("tensor", "ev", guard[("evac", bg)])
            for k in range(NCH):
                p.op("tensor", lambda h, k=k, sl=sl, bg=bg, t0=t0,
                     st=(k == 0), sp=(k == NCH - 1): h.matmul(
                         P[bg][:, 0:512], wg_l[sl][:, k * 128:(k + 1) * 128],
                         s.xn[:, k * TT + t0: k * TT + t0 + 512], start=st, stop=sp))
            for k in range(NCH):
                p.op("tensor", lambda h, k=k, sl=sl, bu=bu, t0=t0,
                     st=(k == 0), sp=(k == NCH - 1): h.matmul(
                         P[bu][:, 0:512], wu_l[sl][:, k * 128:(k + 1) * 128],
                         s.xn[:, k * TT + t0: k * TT + t0 + 512], start=st, stop=sp),
                     inc="pg" if k == NCH - 1 else None)
            guard[("wslot", sl)] = p.semv["pg"]
            p.wait("scalar", "pg")
            if "gsil" in guard:
                p.wait("scalar", "ev", guard["gsil"])
            p.op("scalar", lambda h, bg=bg: h.activation(
                out=gsil[:, 0:512], in_=P[bg][:, 0:512], func=AF.Silu), inc="sc")
            p.wait("vector", "sc")
            p.op("vector", lambda h, bu=bu, ic=ic: h.tensor_tensor(
                out=hb[:, ic * 512:(ic + 1) * 512], in0=gsil[:, 0:512],
                in1=P[bu][:, 0:512], op=ALU.mult), inc="ev")
            guard[("evac", bg)] = p.semv["ev"]
            guard[("evac", bu)] = p.semv["ev"]
            guard["gsil"] = p.semv["ev"]
        if debug and th == 0:
            p.wait("gpsimd", "ev")
            p.dma("gpsimd", dbg_h, hb, "yo0")
        p.wait("tensor", "ev")
        banks = [6, 7, 0, 1]
        for og in range(4):
            for b in banks:
                if ("evac", b) in guard:
                    p.wait("tensor", "ev", guard[("evac", b)])
            for ic in range(IC):
                sl = ic % 2
                if ("wdslot", sl) in guard:
                    p.wait("sync", "pg", guard[("wdslot", sl)])
                p.dma("sync", wd_l[sl],
                      WdS[ic * 128:(ic + 1) * 128, og * 512:(og + 1) * 512], f"wde{sl}")
                p.wait("tensor", f"wde{sl}")
                for j in range(4):
                    p.op("tensor", lambda h, b=banks[j], j=j, sl=sl, ic=ic: h.matmul(
                        P[b][:, 0:512], wd_l[sl][:, j * 128:(j + 1) * 128],
                        hb[:, ic * 512:(ic + 1) * 512],
                        start=(ic == 0), stop=(ic == IC - 1)),
                        inc="pg" if j == 3 else None)
                guard[("wdslot", sl)] = p.semv["pg"]
            chains_done = p.semv["pg"]
            for j in range(4):
                oc = og * 4 + j
                xs = slice((j % 2) * 512, (j % 2) * 512 + 512)
                if ("xq", j % 2) in guard:
                    p.wait("sync", "ev", guard[("xq", j % 2)])
                p.dma("sync", xq[:, xs], xT[oc * 128:(oc + 1) * 128, t0:t0 + 512],
                      f"ld{j % 2}")
                p.wait("vector", "pg", chains_done)
                p.wait("vector", f"ld{j % 2}")
                p.wait("vector", f"yo{j % 2}")
                p.op("vector", lambda h, b=banks[j], xs=xs: h.scalar_tensor_tensor(
                    out=oute[:, xs], in0=P[b][:, 0:512], scalar=0.5, in1=xq[:, xs],
                    op0=ALU.mult, op1=ALU.add), inc="ev")
                guard[("evac", banks[j])] = p.semv["ev"]
                guard[("xq", j % 2)] = p.semv["ev"]
                p.wait("gpsimd", "ev")
                p.dma("gpsimd", yT[oc * 128:(oc + 1) * 128, t0:t0 + 512], oute[:, xs],
                      f"yo{j % 2}")
    p.wait("gpsimd", "yo0")
    p.wait("gpsimd", "yo1")
    return p.build()


def stage_ghost_inputs(x_slice_T, gate_idx, up_idx, down_idx, lut):
    return {
        "xT": np.ascontiguousarray(x_slice_T, dtype=np.float32),
        "gidxT": np.ascontiguousarray(np.asarray(gate_idx).T).astype(np.uint8),
        "uidxT": np.ascontiguousarray(np.asarray(up_idx).T).astype(np.uint8),
        "didxT": np.ascontiguousarray(np.asarray(down_idx).T).astype(np.uint8),
        "lut": np.ascontiguousarray(np.asarray(lut), dtype=np.float32).reshape(1, 16),
    }



def build_mamba_front(TT=1024):
    D, I4, NCH = 2048, 4096, 16
    p = Prog("mfront")
    nc = p.nc
    xT = nc.dram_tensor("xT", [D, TT], F32, kind="ExternalInput").ap()
    wipT = nc.dram_tensor("wipT", [D, 2 * I4], F32, kind="ExternalInput").ap()
    hinT = nc.dram_tensor("hinT", [I4, TT], BF16, kind="ExternalOutput").ap()
    sgT = nc.dram_tensor("sgT", [I4, TT], BF16, kind="ExternalOutput").ap()

    for sn in ("ld0", "ld1", "wd0", "wd1", "cv", "dv", "pe", "sc", "ev", "pg",
               "yo0", "yo1", "yo2", "yo3"):
        p.sem(sn)

    s = NS()
    s.A = p.sbuf("A", [128, 40 * 1024], BF16)
    s.xb = s.A[:, 0:NCH * TT]
    s.xn = s.A[:, NCH * TT:2 * NCH * TT]
    wslab_b = [s.A[:, 32 * TT + k * 2048: 32 * TT + (k + 1) * 2048] for k in range(2)]
    outb = [s.A[:, 32 * TT + 2 * 2048 + k * 512: 32 * TT + 2 * 2048 + (k + 1) * 512]
            for k in range(4)]
    s.xtmp = p.sbuf("xtmp", [128, 2 * TT], F32)
    wslab_f = [p.sbuf(f"wsf{k}", [128, 2048], F32) for k in range(2)]
    s.ones_l = p.sbuf("ones_l", [128, 1], BF16)
    s.ones_c = p.sbuf("ones_c", [1, 128], F32)
    s.epsc = p.sbuf("epsc", [128, 1], F32)
    s.r0 = p.sbuf("r0", [1, TT], F32)
    s.r1 = p.sbuf("r1", [1, TT], F32)
    s.r2 = p.sbuf("r2", [1, TT], F32)
    s.r3 = p.sbuf("r3", [1, TT], F32)
    s.mb = p.sbuf("mb", [128, TT], F32)
    s.rb = p.sbuf("rb", [128, TT], F32)
    P = [p.psum(f"P{i}", [128, 512]) for i in range(6)]
    pa, pb = P[0], P[1]

    p.op("vector", lambda h: h.memset(s.ones_l[:], 1.0))
    p.op("vector", lambda h: h.memset(s.epsc[:], 1e-5))
    p.op("vector", lambda h: h.memset(s.ones_c[:], 1.0), inc="dv")
    emit_load_x_as_bf16(p, s, xT, TT, NCH)
    emit_ln(p, s, pa, pb, TT, NCH)

    guard = {}
    for oc in range(2 * I4 // 128):
        sl = oc % 2
        if ("wslot", sl) in guard:
            p.wait("sync", "cv", guard[("wslot", sl)])
        p.dma("sync", wslab_f[sl].rearrange("p (k i) -> p k i", k=16),
              wipT[:, oc * 128:(oc + 1) * 128].rearrange("(k p) i -> p k i", p=128),
              f"wd{sl}")
        p.wait("vector", f"wd{sl}")
        if ("wbslot", sl) in guard:
            p.wait("vector", "pg", guard[("wbslot", sl)])
        p.op("vector", lambda h, sl=sl: h.tensor_copy(out=wslab_b[sl], in_=wslab_f[sl][:]),
             inc="cv")
        guard[("wslot", sl)] = p.semv["cv"]
        p.wait("tensor", "cv")
        p.wait("tensor", "dv")
        for th in range(2):
            bk = 2 + 2 * (oc % 2) + th
            if ("evac", bk) in guard:
                p.wait("tensor", "ev", guard[("evac", bk)])
            for k in range(NCH):
                p.op("tensor", lambda h, k=k, sl=sl, bk=bk, th=th,
                     st=(k == 0), sp=(k == NCH - 1): h.matmul(
                         P[bk][:, 0:512], wslab_b[sl][:, k * 128:(k + 1) * 128],
                         s.xn[:, k * TT + th * 512: k * TT + th * 512 + 512],
                         start=st, stop=sp), inc="pg" if k == NCH - 1 else None)
        guard[("wbslot", sl)] = p.semv["pg"]
        p.wait("scalar", "pg")
        for th in range(2):
            bk = 2 + 2 * (oc % 2) + th
            slot = 2 * (oc % 2) + th
            ob = outb[slot]
            p.wait("scalar", f"yo{slot}")
            if oc < 32:
                p.op("scalar", lambda h, bk=bk, ob=ob: h.copy(out=ob, in_=P[bk][:, 0:512]),
                     inc="ev")
            else:
                p.op("scalar", lambda h, bk=bk, ob=ob: h.activation(
                    out=ob, in_=P[bk][:, 0:512], func=AF.Silu), inc="ev")
            guard[("evac", bk)] = p.semv["ev"]
            p.wait("gpsimd", "ev")
            dst = (hinT if oc < 32 else sgT)
            r0 = (oc if oc < 32 else oc - 32) * 128
            p.dma("gpsimd", dst[r0:r0 + 128, th * 512:th * 512 + 512], ob, f"yo{slot}")
    for sn in ("yo0", "yo1", "yo2", "yo3"):
        p.wait("gpsimd", sn)
    return p.build()



def build_conv(CH=512, T=8192):
    p = Prog("conv")
    nc = p.nc
    hinT = nc.dram_tensor("hinT", [CH, T], BF16, kind="ExternalInput").ap()
    convw = nc.dram_tensor("convw", [CH, 4], F32, kind="ExternalInput").ap()
    convb = nc.dram_tensor("convb", [CH, 1], F32, kind="ExternalInput").ap()
    uT = nc.dram_tensor("uT", [CH, T], BF16, kind="ExternalOutput").ap()
    for sn in ("ld0", "ld1", "ldc", "dv", "sc", "yo0", "yo1"):
        p.sem(sn)
    nic = CH // 128
    hsb = p.sbuf("hsb", [128, 2 * (T + 3)], BF16)
    acc = p.sbuf("acc", [128, T], F32)
    ub = p.sbuf("ub", [128, 2 * T], BF16)
    wc = p.sbuf("wc", [128, 4 * nic], F32)
    bc = p.sbuf("bc", [128, nic], F32)
    p.dma("sync", wc.rearrange("p (k f) -> p k f", k=nic),
          convw.rearrange("(k p) f -> p k f", p=128), "ldc")
    p.dma("sync", bc[:], convb.rearrange("(k p) f -> p (k f)", p=128), "ldc", slow=True)
    guard = {}
    for ic in range(nic):
        sl = ic % 2
        o = sl * (T + 3)
        if ("h", sl) in guard:
            p.wait("sync", "sc", guard[("h", sl)])
        p.dma("sync", hsb[:, o + 3: o + 3 + T], hinT[ic * 128:(ic + 1) * 128, :],
              f"ld{sl}")
        p.wait("vector", f"ld{sl}")
        p.wait("vector", "ldc")
        if ic == 0:
            p.op("vector", lambda h: h.memset(hsb[:, 0:3], 0.0))
            p.op("vector", lambda h: h.memset(hsb[:, T + 3: T + 6], 0.0))
        if ic >= 1:
            p.wait("vector", "sc", guard[("acc", 0)])
        p.op("vector", lambda h, o=o, ic=ic: h.tensor_scalar(
            out=acc[:], in0=hsb[:, o: o + T], scalar1=wc[:, 4 * ic: 4 * ic + 1],
            scalar2=None, op0=ALU.mult))
        for k in range(1, 4):
            p.op("vector", lambda h, o=o, ic=ic, k=k: h.scalar_tensor_tensor(
                out=acc[:], in0=hsb[:, o + k: o + k + T],
                scalar=wc[:, 4 * ic + k: 4 * ic + k + 1], in1=acc[:],
                op0=ALU.mult, op1=ALU.add), inc="dv" if k == 3 else None)
        guard[("h", sl)] = p.semv["dv"]
        p.wait("scalar", "dv")
        p.wait("scalar", f"yo{sl}")
        p.op("scalar", lambda h, sl=sl, ic=ic: h.activation(
            out=ub[:, sl * T:(sl + 1) * T], in_=acc[:], func=AF.Silu,
            bias=bc[:, ic:ic + 1]), inc="sc")
        guard[("h", sl)] = p.semv["sc"]
        guard[("acc", 0)] = p.semv["sc"]
        p.wait("gpsimd", "sc")
        p.dma("gpsimd", uT[ic * 128:(ic + 1) * 128, :], ub[:, sl * T:(sl + 1) * T],
              f"yo{sl}")
    p.wait("gpsimd", "yo0")
    p.wait("gpsimd", "yo1")
    return p.build()



def build_xproj_dt(TT=1024, do_dt=True, do_bc=True, dt_mode='full'):
    I4, R = 4096, 128
    p = Prog("xproj")
    nc = p.nc
    uT = nc.dram_tensor("uT", [I4, TT], BF16, kind="ExternalInput").ap()
    xpT = nc.dram_tensor("xpT", [I4, 160], F32, kind="ExternalInput").ap()
    dtpT = nc.dram_tensor("dtpT", [R, I4], F32, kind="ExternalInput").ap()
    dtb = nc.dram_tensor("dtb", [I4, 1], F32, kind="ExternalInput").ap()
    dtT = nc.dram_tensor("dtT", [I4, TT], F32, kind="ExternalOutput").ap()
    Bo = nc.dram_tensor("Bo", [16, TT], F32, kind="ExternalOutput").ap()
    Co = nc.dram_tensor("Co", [16, TT], F32, kind="ExternalOutput").ap()
    for sn in ("ld", "cv", "dv", "pe", "sc", "ev", "pg", "yo0", "yo1", "yob"):
        p.sem(sn)
    nk = I4 // 128
    ub = p.sbuf("ub", [128, nk * TT], BF16)
    xpf = p.sbuf("xpf", [128, nk * 160], F32)
    xpb = p.sbuf("xpb", [128, nk * 160], BF16)
    dtpf = p.sbuf("dtpf", [128, I4], F32)
    dtpb = p.sbuf("dtpb", [128, I4], BF16)
    dtbc = p.sbuf("dtbc", [128, nk], F32)
    tsb = p.sbuf("tsb", [128, TT], BF16)
    t1 = p.sbuf("t1", [128, 512], F32)
    a1 = p.sbuf("a1", [128, 512], F32)
    dto = p.sbuf("dto", [128, 2 * 512], F32)
    bco = p.sbuf("bco", [32, TT], F32)
    P = [p.psum(f"P{i}", [128, 512]) for i in range(6)]

    p.dma("sync", ub.rearrange("p (k t) -> p k t", k=nk),
          uT.rearrange("(k p) t -> p k t", p=128), "ld")
    p.dma("sync", xpf.rearrange("p (k r) -> p k r", k=nk),
          xpT.rearrange("(k p) r -> p k r", p=128), "ld")
    p.dma("sync", dtpf[:], dtpT, "ld")
    p.dma("sync", dtbc[:], dtb.rearrange("(k p) f -> p (k f)", p=128), "ld", slow=True)
    p.wait("vector", "ld")
    p.op("vector", lambda h: h.tensor_copy(out=xpb[:], in_=xpf[:]))
    p.op("vector", lambda h: h.tensor_copy(out=dtpb[:], in_=dtpf[:]), inc="cv")
    p.wait("tensor", "cv")
    for th in range(2):
        for k in range(nk):
            p.op("tensor", lambda h, k=k, th=th, st=(k == 0), sp=(k == nk - 1): h.matmul(
                P[0 + th][:, 0:512], xpb[:, k * 160: k * 160 + 128],
                ub[:, k * TT + th * 512: k * TT + th * 512 + 512], start=st, stop=sp))
            p.op("tensor", lambda h, k=k, th=th, st=(k == 0), sp=(k == nk - 1): h.matmul(
                P[2 + th][0:32, 0:512], xpb[:, k * 160 + 128: (k + 1) * 160],
                ub[:, k * TT + th * 512: k * TT + th * 512 + 512], start=st, stop=sp),
                inc="pg" if k == nk - 1 else None)
        p.wait("vector", "pg")
        p.op("vector", lambda h, th=th: h.tensor_copy(
            out=tsb[:, th * 512:(th + 1) * 512], in_=P[0 + th][:, 0:512]))
        p.op("vector", lambda h, th=th: h.tensor_copy(
            out=bco[0:32, th * 512:(th + 1) * 512], in_=P[2 + th][0:32, 0:512]),
            inc="dv")
    p.wait("gpsimd", "dv")
    if do_bc:
        p.dma("gpsimd", Bo, bco[0:16, :], "yob")
        p.dma("gpsimd", Co, bco[16:32, :], "yob")
    else:
        p.dma("gpsimd", Bo, bco[0:16, :], "yob")
        p.dma("gpsimd", Co, bco[0:16, :], "yob")
    guard = {}
    for oc in range(nk if do_dt else 0):
        for th in range(2):
            bk = 4 + (oc * 2 + th) % 2
            if ("e", bk) in guard:
                p.wait("tensor", "ev", guard[("e", bk)])
            if ("ed", bk) in guard:
                p.wait("tensor", "dv", guard[("ed", bk)])
            else:
                p.wait("tensor", "dv")
            p.op("tensor", lambda h, oc=oc, th=th, bk=bk: h.matmul(
                P[bk][:, 0:512], dtpb[:, oc * 128:(oc + 1) * 128],
                tsb[:, th * 512:(th + 1) * 512], start=True, stop=True), inc="pg")
            v = p.semv["pg"]
            p.wait("vector", "pg", v)
            p.wait("scalar", "pg", v)
            p.wait("vector", f"yo{th}")
            p.wait("scalar", f"yo{th}")
            if dt_mode in ('abs', 'absexp', 'absexpln'):
                p.wait("scalar", "dv")
                p.op("scalar", lambda h, bk=bk, oc=oc: h.activation(
                    out=a1[:], in_=P[bk][:, 0:512], func=AF.Abs, bias=dtbc[:, oc:oc + 1]),
                    inc="ev")
                guard[("e", bk)] = p.semv["ev"]
                if dt_mode in ('absexp', 'absexpln'):
                    p.op("scalar", lambda h: h.activation(out=a1[:], in_=a1[:],
                                                          func=AF.Exp, scale=-1.0),
                         inc="sc")
                if dt_mode == 'absexpln':
                    p.op("scalar", lambda h: h.activation(out=a1[:], in_=a1[:],
                                                          func=AF.Ln, bias=1.0),
                         inc="sc")
                p.op("vector", lambda h, th=th: h.tensor_copy(
                    out=dto[:, th * 512:(th + 1) * 512], in_=a1[:]), inc="dv")
            elif dt_mode == 'copy':
                p.wait("scalar", "dv")
                p.op("scalar", lambda h, bk=bk: h.copy(out=a1[:], in_=P[bk][:, 0:512]),
                     inc="ev")
                guard[("e", bk)] = p.semv["ev"]
                p.op("vector", lambda h, th=th: h.tensor_copy(
                    out=dto[:, th * 512:(th + 1) * 512], in_=a1[:]), inc="dv")
            elif dt_mode == 'ts':
                p.op("vector", lambda h, bk=bk, oc=oc: h.tensor_scalar(
                    out=t1[:], in0=P[bk][:, 0:512], scalar1=dtbc[:, oc:oc + 1],
                    scalar2=0.0, op0=ALU.add, op1=ALU.max), inc="ev")
                guard[("e", bk)] = p.semv["ev"]
                p.op("vector", lambda h, th=th: h.tensor_copy(
                    out=dto[:, th * 512:(th + 1) * 512], in_=t1[:]), inc="dv")
            else:
                p.op("vector", lambda h, bk=bk, oc=oc: h.tensor_scalar(
                    out=t1[:], in0=P[bk][:, 0:512], scalar1=dtbc[:, oc:oc + 1],
                    scalar2=0.0, op0=ALU.add, op1=ALU.max), inc="dv")
                p.wait("scalar", "dv")
                p.op("scalar", lambda h, bk=bk, oc=oc: h.activation(
                    out=a1[:], in_=P[bk][:, 0:512], func=AF.Abs, bias=dtbc[:, oc:oc + 1]),
                    inc="ev")
                guard[("e", bk)] = p.semv["ev"]
                p.op("scalar", lambda h: h.activation(out=a1[:], in_=a1[:], func=AF.Exp,
                                                      scale=-1.0))
                p.op("scalar", lambda h: h.activation(out=a1[:], in_=a1[:], func=AF.Ln,
                                                      bias=1.0), inc="sc")
                p.wait("vector", "sc")
                p.op("vector", lambda h, th=th: h.tensor_tensor(
                    out=dto[:, th * 512:(th + 1) * 512], in0=t1[:], in1=a1[:], op=ALU.add),
                    inc="dv")
            guard[("ed", bk)] = p.semv["dv"]
            p.wait("gpsimd", "dv")
            p.dma("gpsimd", dtT[oc * 128:(oc + 1) * 128, th * 512:(th + 1) * 512],
                  dto[:, th * 512:(th + 1) * 512], f"yo{th}")
    if not do_dt:
        p.wait("gpsimd", "ld")
        p.op("gpsimd", lambda h: h.memset(dto[:, 0:4], 0.0))
        p.dma("gpsimd", dtT[0:128, 0:4], dto[:, 0:4], "yo0")
    p.wait("gpsimd", "yo0")
    p.wait("gpsimd", "yo1")
    p.wait("gpsimd", "yob")
    return p.build()



def build_scan(CH=512, T=8192, TC=2048):
    N = 16
    p = Prog("scan")
    nc = p.nc
    uT = nc.dram_tensor("uT", [CH, T], BF16, kind="ExternalInput").ap()
    sgT = nc.dram_tensor("sgT", [CH, T], BF16, kind="ExternalInput").ap()
    dtT = nc.dram_tensor("dtT", [CH, T], F32, kind="ExternalInput").ap()
    Bf = nc.dram_tensor("Bf", [16, T], F32, kind="ExternalInput").ap()
    Cf = nc.dram_tensor("Cf", [16, T], F32, kind="ExternalInput").ap()
    Alog = nc.dram_tensor("Alog", [CH, N], F32, kind="ExternalInput").ap()
    sel = nc.dram_tensor("sel", [16, 16 * 128], BF16, kind="ExternalInput").ap()
    Dv = nc.dram_tensor("Dv", [CH, 1], F32, kind="ExternalInput").ap()
    ygT = nc.dram_tensor("ygT", [CH, T], BF16, kind="ExternalOutput").ap()

    for sn in ("ld", "ldc", "wd", "dv", "pe", "sc", "ev", "pg", "yo0", "yo1"):
        p.sem(sn)
    nic = CH // 128
    ntc = T // TC
    dt_t = [p.sbuf(f"dt{ic}", [128, TC], F32) for ic in range(nic)]
    u_t = [p.sbuf(f"u{ic}", [128, TC], BF16) for ic in range(nic)]
    sg_t = [p.sbuf(f"sg{ic}", [128, TC], BF16) for ic in range(nic)]
    dtu_t = [p.sbuf(f"dtu{ic}", [128, TC], F32) for ic in range(nic)]
    y_t = [p.sbuf(f"y{ic}", [128, TC], F32) for ic in range(nic)]
    dA = p.sbuf("dA", [128, TC], F32)
    dBx = p.sbuf("dBx", [128, TC], F32)
    hh = p.sbuf("hh", [128, TC], F32)
    Bb = p.sbuf("Bb", [128, TC], F32)
    Cb = p.sbuf("Cb", [128, TC], F32)
    Brow = p.sbuf("Brow", [16, TC], BF16)
    Crow = p.sbuf("Crow", [16, TC], BF16)
    Brf = p.sbuf("Brf", [16, TC], F32)
    Crf = p.sbuf("Crf", [16, TC], F32)
    Acol = p.sbuf("Acol", [128, nic * N], F32)
    Dcol = p.sbuf("Dcol", [128, nic], F32)
    state = p.sbuf("state", [128, nic * N], F32)
    ygb = p.sbuf("ygb", [128, 2 * TC], BF16)
    ones_c = p.sbuf("ones_c", [1, 128], BF16)
    selt = p.sbuf("selt", [16, 16 * 128], BF16)
    pbc = p.psum("pbc", [128, 512])
    pbc2 = p.psum("pbc2", [128, 512])

    p.dma("sync", selt[:], sel, "ldc")
    p.dma("sync", Acol.rearrange("p (k n) -> p k n", k=nic),
          Alog.rearrange("(k p) n -> p k n", p=128), "ldc")
    p.dma("sync", Dcol[:], Dv.rearrange("(k p) f -> p (k f)", p=128), "ldc", slow=True)
    p.wait("vector", "ldc")
    p.op("vector", lambda h: h.memset(ones_c[:], 1.0))
    p.op("vector", lambda h: h.memset(state[:], 0.0), inc="dv")
    p.wait("scalar", "ldc")
    p.op("scalar", lambda h: h.activation(out=Acol[:], in_=Acol[:], func=AF.Exp),
         inc="sc")
    p.wait("vector", "sc")
    p.op("vector", lambda h: h.tensor_scalar(
        out=Acol[:], in0=Acol[:], scalar1=-1.0, scalar2=None, op0=ALU.mult), inc="dv")

    guard = {("dAfree",): p.semv["dv"]}
    for tc in range(ntc):
        t0 = tc * TC
        if tc > 0:
            p.wait("sync", "dv", guard[("bcrows",)])
        p.dma("sync", Brf[:], Bf[:, t0:t0 + TC], "wd")
        p.dma("sync", Crf[:], Cf[:, t0:t0 + TC], "wd")
        p.wait("vector", "wd")
        if tc > 0:
            p.wait("vector", "dv", guard[("bcrows",)])
        p.op("vector", lambda h: h.tensor_copy(out=Brow[:], in_=Brf[:]))
        p.op("vector", lambda h: h.tensor_copy(out=Crow[:], in_=Crf[:]), inc="dv")
        guard[("bcrows",)] = p.semv["dv"]
        for ic in range(nic):
            if tc > 0:
                p.wait("sync", "dv", guard[("done", ic)])
            p.dma("sync", dt_t[ic][:], dtT[ic * 128:(ic + 1) * 128, t0:t0 + TC], "ld")
            p.dma("sync", u_t[ic][:], uT[ic * 128:(ic + 1) * 128, t0:t0 + TC], "ld")
            p.dma("sync", sg_t[ic][:], sgT[ic * 128:(ic + 1) * 128, t0:t0 + TC], "ld")
        p.wait("vector", "ld")
        p.wait("scalar", "ld")
        for ic in range(nic):
            p.op("vector", lambda h, ic=ic: h.tensor_scalar(
                out=y_t[ic][:], in0=u_t[ic][:], scalar1=Dcol[:, ic:ic + 1],
                scalar2=None, op0=ALU.mult))
            p.op("vector", lambda h, ic=ic: h.tensor_tensor(
                out=dtu_t[ic][:], in0=dt_t[ic][:], in1=u_t[ic][:], op=ALU.mult),
                inc="dv")
        for n in range(N):
            nbt = TC // 512
            for j in range(nbt):
                p.wait("tensor", "dv")
                if ("bb", 0) in guard:
                    p.wait("tensor", "dv", guard[("bb", 0)])
                p.op("tensor", lambda h, n=n, j=j: h.matmul(
                    pbc[:, 0:512], selt[:, n * 128:(n + 1) * 128],
                    Brow[:, j * 512: (j + 1) * 512],
                    start=True, stop=True))
                p.op("tensor", lambda h, n=n, j=j: h.matmul(
                    pbc2[:, 0:512], selt[:, n * 128:(n + 1) * 128],
                    Crow[:, j * 512: (j + 1) * 512],
                    start=True, stop=True), inc="pe")
                p.wait("vector", "pe")
                p.op("vector", lambda h, j=j: h.tensor_copy(
                    out=Bb[:, j * 512:(j + 1) * 512], in_=pbc[:, 0:512]))
                p.op("vector", lambda h, j=j: h.tensor_copy(
                    out=Cb[:, j * 512:(j + 1) * 512], in_=pbc2[:, 0:512]), inc="dv")
            guard[("bb", 0)] = p.semv["dv"]
            for ic in range(nic):
                p.wait("scalar", "dv", guard.get(("dAfree",), 0))
                p.op("scalar", lambda h, ic=ic, n=n: h.activation(
                    out=dA[:], in_=dt_t[ic][:], func=AF.Exp,
                    scale=Acol[:, ic * N + n: ic * N + n + 1]), inc="sc")
                p.wait("vector", "sc")
                p.op("vector", lambda h, ic=ic: h.tensor_tensor(
                    out=dBx[:], in0=dtu_t[ic][:], in1=Bb[:], op=ALU.mult))
                p.op("vector", lambda h, ic=ic, n=n: h.tensor_tensor_scan(
                    out=hh[:], data0=dA[:], data1=dBx[:],
                    initial=state[:, ic * N + n: ic * N + n + 1],
                    op0=ALU.mult, op1=ALU.add))
                guard[("dAfree",)] = p.semv["dv"] + 2
                p.op("vector", lambda h, ic=ic, n=n: h.tensor_copy(
                    out=state[:, ic * N + n: ic * N + n + 1], in_=hh[:, TC - 1: TC]))
                p.op("vector", lambda h, ic=ic: h.tensor_tensor(
                    out=hh[:], in0=hh[:], in1=Cb[:], op=ALU.mult))
                p.op("vector", lambda h, ic=ic: h.tensor_tensor(
                    out=y_t[ic][:], in0=y_t[ic][:], in1=hh[:], op=ALU.add), inc="dv")
                guard[("dAfree",)] = p.semv["dv"]
        for ic in range(nic):
            sl = ic % 2
            p.wait("vector", f"yo{sl}")
            p.op("vector", lambda h, ic=ic, sl=sl: h.tensor_tensor(
                out=ygb[:, sl * TC:(sl + 1) * TC], in0=y_t[ic][:], in1=sg_t[ic][:],
                op=ALU.mult), inc="dv")
            guard[("done", ic)] = p.semv["dv"]
            p.wait("gpsimd", "dv")
            p.dma("gpsimd", ygT[ic * 128:(ic + 1) * 128, t0:t0 + TC],
                  ygb[:, sl * TC:(sl + 1) * TC], f"yo{sl}")
    p.wait("gpsimd", "yo0")
    p.wait("gpsimd", "yo1")
    return p.build()



def build_outproj(TT=1024, k_dim=4096):
    I4, D = k_dim, 2048
    p = Prog("outp")
    nc = p.nc
    ygT = nc.dram_tensor("ygT", [I4, TT], BF16, kind="ExternalInput").ap()
    wopT = nc.dram_tensor("wopT", [I4, D], F32, kind="ExternalInput").ap()
    xT = nc.dram_tensor("xT", [D, TT], F32, kind="ExternalInput").ap()
    yT = nc.dram_tensor("yT", [D, TT], F32, kind="ExternalOutput").ap()
    for sn in ("ldy", "ld0", "ld1", "wd0", "wd1", "cv", "dv", "pe", "ev", "pg",
               "yo0", "yo1"):
        p.sem(sn)
    nk = I4 // 128
    ygb = p.sbuf("ygb", [128, nk * TT], BF16)
    wf = [p.sbuf(f"wf{k}", [128, nk * 128], F32) for k in range(2)]
    wb = [p.sbuf(f"wb{k}", [128, nk * 128], BF16) for k in range(2)]
    xq = p.sbuf("xq", [128, 2 * 512], F32)
    oute = p.sbuf("oute", [128, 2 * 512], F32)
    P = [p.psum(f"P{i}", [128, 512]) for i in range(4)]

    p.dma("sync", ygb.rearrange("p (k t) -> p k t", k=nk),
          ygT.rearrange("(k p) t -> p k t", p=128), "ldy")
    guard = {}
    for oc in range(D // 128):
        sl = oc % 2
        if ("w", sl) in guard:
            p.wait("sync", "cv", guard[("w", sl)])
        p.dma("sync", wf[sl].rearrange("p (k i) -> p k i", k=nk),
              wopT[:, oc * 128:(oc + 1) * 128].rearrange("(k p) i -> p k i", p=128),
              f"wd{sl}")
        p.wait("vector", f"wd{sl}")
        if ("wb", sl) in guard:
            p.wait("vector", "pg", guard[("wb", sl)])
        p.op("vector", lambda h, sl=sl: h.tensor_copy(out=wb[sl][:], in_=wf[sl][:]),
             inc="cv")
        guard[("w", sl)] = p.semv["cv"]
        p.wait("tensor", "cv")
        p.wait("tensor", "ldy")
        for th in range(2):
            bk = 2 * sl + th
            if ("e", bk) in guard:
                p.wait("tensor", "ev", guard[("e", bk)])
            for k in range(nk):
                p.op("tensor", lambda h, k=k, th=th, sl=sl, bk=bk,
                     st=(k == 0), sp=(k == nk - 1): h.matmul(
                         P[bk][:, 0:512], wb[sl][:, k * 128:(k + 1) * 128],
                         ygb[:, k * TT + th * 512: k * TT + th * 512 + 512],
                         start=st, stop=sp), inc="pg" if k == nk - 1 else None)
        guard[("wb", sl)] = p.semv["pg"]
        for th in range(2):
            bk = 2 * sl + th
            xs = slice(th * 512, (th + 1) * 512)
            if ("xq", th) in guard:
                p.wait("sync", "ev", guard[("xq", th)])
            p.dma("sync", xq[:, xs], xT[oc * 128:(oc + 1) * 128, th * 512:(th + 1) * 512],
                  f"ld{th}")
            p.wait("vector", "pg")
            p.wait("vector", f"ld{th}")
            p.wait("vector", f"yo{th}")
            p.op("vector", lambda h, bk=bk, xs=xs: h.tensor_tensor(
                out=oute[:, xs], in0=P[bk][:, 0:512], in1=xq[:, xs], op=ALU.add),
                inc="ev")
            guard[("e", bk)] = p.semv["ev"]
            guard[("xq", th)] = p.semv["ev"]
            p.wait("gpsimd", "ev")
            p.dma("gpsimd", yT[oc * 128:(oc + 1) * 128, th * 512:(th + 1) * 512],
                  oute[:, xs], f"yo{th}")
    p.wait("gpsimd", "yo0")
    p.wait("gpsimd", "yo1")
    return p.build()


def make_sel():
    sel = np.zeros((16, 16 * 128), dtype=np.float32)
    for n in range(16):
        sel[n, n * 128:(n + 1) * 128] = 1.0
    return np_bf16(sel)



def build_l1ffn(TT=1024):
    D, I8, NCH = 2048, 8192, 16
    NIC = I8 // 128
    p = Prog("l1ffn")
    nc = p.nc
    xT = nc.dram_tensor("xT", [D, TT], F32, kind="ExternalInput").ap()
    gwT = nc.dram_tensor("gwT", [D, I8], U8, kind="ExternalInput").ap()
    uwT = nc.dram_tensor("uwT", [D, I8], U8, kind="ExternalInput").ap()
    dwT = nc.dram_tensor("dwT", [I8, D], U8, kind="ExternalInput").ap()
    gsc = nc.dram_tensor("gsc", [I8, 1], F32, kind="ExternalInput").ap()
    usc = nc.dram_tensor("usc", [I8, 1], F32, kind="ExternalInput").ap()
    dsc = nc.dram_tensor("dsc", [D, 1], F32, kind="ExternalInput").ap()
    yT = nc.dram_tensor("yT", [D, TT], F32, kind="ExternalOutput").ap()

    for sn in ("ld0", "ld1", "ldc", "wd0", "wd1", "wde0", "wde1", "cv", "dv", "pe",
               "sc", "ev", "pg", "yo0", "yo1", "hs"):
        p.sem(sn)

    s = NS()
    AB = 74 * 1024
    s.A = p.sbuf("A", [128, AB], BF16)
    s.xb = s.A[:, 0:NCH * TT]
    s.xn = s.A[:, NCH * TT:2 * NCH * TT]
    hb = s.A[:, 2 * NCH * TT: 2 * NCH * TT + NIC * 512]
    o_w = 2 * NCH * TT + NIC * 512
    wgb = [s.A[:, o_w + k * 2048: o_w + (k + 1) * 2048] for k in range(2)]
    wub = [s.A[:, o_w + (2 + k) * 2048: o_w + (3 + k) * 2048] for k in range(2)]
    o_wd = o_w + 4 * 2048
    wdb = [s.A[:, o_wd + k * 512: o_wd + (k + 1) * 512] for k in range(2)]
    assert o_wd + 1024 <= AB

    wg8 = [p.sbuf(f"wg8{k}", [128, 2048], U8) for k in range(2)]
    wu8 = [p.sbuf(f"wu8{k}", [128, 2048], U8) for k in range(2)]
    wd8 = [p.sbuf(f"wd8{k}", [128, 512], U8) for k in range(2)]
    s.xtmp = p.sbuf("xtmp", [128, 2 * TT], F32)
    s.ones_l = p.sbuf("ones_l", [128, 1], BF16)
    s.ones_c = p.sbuf("ones_c", [1, 128], F32)
    s.epsc = p.sbuf("epsc", [128, 1], F32)
    s.r0 = p.sbuf("r0", [1, TT], F32)
    s.r1 = p.sbuf("r1", [1, TT], F32)
    s.r2 = p.sbuf("r2", [1, TT], F32)
    s.r3 = p.sbuf("r3", [1, TT], F32)
    s.mb = p.sbuf("mb", [128, TT], F32)
    s.rb = p.sbuf("rb", [128, TT], F32)
    m128 = p.sbuf("m128", [1, 128], BF16)
    xsb = p.sbuf("xsb", [1, TT], BF16)
    hsb = p.sbuf("hsb", [1, TT], BF16)
    gcol = p.sbuf("gcol", [128, NIC], F32)
    ucol = p.sbuf("ucol", [128, NIC], F32)
    dcol = p.sbuf("dcol", [128, NCH], F32)
    dcolh = p.sbuf("dcolh", [128, NCH], F32)
    gs = p.sbuf("gs", [128, 512], F32)
    xq = p.sbuf("xq", [128, 2 * 512], F32)
    oute = p.sbuf("oute", [128, 2 * 512], F32)
    P = [p.psum(f"P{i}", [128, 512]) for i in range(8)]
    pa, pb = P[0], P[1]

    p.op("vector", lambda h: h.memset(s.ones_l[:], 1.0))
    p.op("vector", lambda h: h.memset(s.epsc[:], 1e-5))
    p.op("vector", lambda h: h.memset(m128[:], -128.0))
    p.op("vector", lambda h: h.memset(s.ones_c[:], 1.0), inc="dv")
    p.dma("sync", gcol[:], gsc.rearrange("(k p) f -> p (k f)", p=128), "ldc", slow=True)
    p.dma("sync", ucol[:], usc.rearrange("(k p) f -> p (k f)", p=128), "ldc", slow=True)
    p.dma("sync", dcol[:], dsc.rearrange("(k p) f -> p (k f)", p=128), "ldc", slow=True)
    p.wait("vector", "ldc")
    p.wait("scalar", "ldc")
    p.op("vector", lambda h: h.tensor_scalar(
        out=dcolh[:], in0=dcol[:], scalar1=0.5, scalar2=None, op0=ALU.mult), inc="dv")
    emit_load_x_as_bf16(p, s, xT, TT, NCH)
    emit_ln(p, s, pa, pb, TT, NCH)

    for th in range(2):
        p.wait("tensor", "dv")
        for c in range(NCH):
            k0 = c * TT + th * 512
            p.op("tensor", lambda h, k0=k0, st=(c == 0), sp=(c == NCH - 1): h.matmul(
                pa[0:1, :], s.ones_l[:, 0:1], s.xn[:, k0:k0 + 512], start=st, stop=sp),
                inc="pe" if c == NCH - 1 else None)
        p.wait("vector", "pe")
        p.op("vector", lambda h, th=th: h.tensor_copy(
            out=xsb[0:1, th * 512:(th + 1) * 512], in_=pa[0:1, :]), inc="dv")

    guard = {}
    for th in range(2):
        t0 = th * 512
        if th == 1:
            p.wait("vector", "pg")
        for ic in range(NIC):
            sl = ic % 2
            if ("w8", sl) in guard:
                p.wait("sync", "cv", guard[("w8", sl)])
            p.dma("sync", wg8[sl][:].rearrange("p (k i) -> p k i", k=16),
                  gwT[:, ic * 128:(ic + 1) * 128].rearrange("(k p) i -> p k i", p=128),
                  f"wd{sl}")
            p.dma("sync", wu8[sl][:].rearrange("p (k i) -> p k i", k=16),
                  uwT[:, ic * 128:(ic + 1) * 128].rearrange("(k p) i -> p k i", p=128),
                  f"wd{sl}")
            p.wait("gpsimd", f"wd{sl}")
            if ("wb", sl) in guard:
                p.wait("gpsimd", "pg", guard[("wb", sl)])
            p.op("gpsimd", lambda h, sl=sl: h.tensor_copy(out=wgb[sl], in_=wg8[sl][:]))
            p.op("gpsimd", lambda h, sl=sl: h.tensor_copy(out=wub[sl], in_=wu8[sl][:]),
                 inc="cv")
            guard[("w8", sl)] = p.semv["cv"]
            bg, bu = 2 + sl, 4 + sl
            p.wait("tensor", "cv")
            p.wait("tensor", "dv")
            for (bk, wb) in ((bg, wgb[sl]), (bu, wub[sl])):
                if ("evac", bk) in guard:
                    p.wait("tensor", "ev", guard[("evac", bk)])
                for k in range(NCH):
                    p.op("tensor", lambda h, k=k, bk=bk, wb=wb, t0=t0,
                         st=(k == 0): h.matmul(
                             P[bk][:, 0:512], wb[:, k * 128:(k + 1) * 128],
                             s.xn[:, k * TT + t0: k * TT + t0 + 512],
                             start=st, stop=False))
                p.op("tensor", lambda h, bk=bk, t0=t0: h.matmul(
                    P[bk][:, 0:512], m128[0:1, :], xsb[0:1, t0:t0 + 512],
                    start=False, stop=True), inc="pg")
            guard[("wb", sl)] = p.semv["pg"]
            p.wait("scalar", "pg")
            if "gs" in guard:
                p.wait("scalar", "ev", guard["gs"])
            p.op("scalar", lambda h, bg=bg, ic=ic: h.activation(
                out=gs[:, 0:512], in_=P[bg][:, 0:512], func=AF.Silu,
                scale=gcol[:, ic:ic + 1]), inc="sc")
            p.wait("vector", "sc")
            p.op("vector", lambda h, bu=bu, ic=ic: h.scalar_tensor_tensor(
                out=hb[:, ic * 512:(ic + 1) * 512], in0=P[bu][:, 0:512],
                scalar=ucol[:, ic:ic + 1], in1=gs[:, 0:512],
                op0=ALU.mult, op1=ALU.mult), inc="ev")
            guard[("evac", bg)] = p.semv["ev"]
            guard[("evac", bu)] = p.semv["ev"]
            guard["gs"] = p.semv["ev"]
        p.wait("tensor", "ev")
        if ("hs",) in guard:
            p.wait("tensor", "dv", guard[("hs",)])
        for c in range(NIC):
            p.op("tensor", lambda h, c=c, st=(c == 0), sp=(c == NIC - 1): h.matmul(
                pb[0:1, :], s.ones_l[:, 0:1], hb[:, c * 512:(c + 1) * 512],
                start=st, stop=sp), inc="pe" if c == NIC - 1 else None)
        p.wait("vector", "pe")
        p.op("vector", lambda h, t0=t0: h.tensor_copy(out=hsb[0:1, 0:512],
                                                      in_=pb[0:1, :]), inc="dv")
        guard[("hs",)] = p.semv["dv"]
        banks = [6, 7, 0, 1]
        for og in range(4):
            for b in banks:
                if ("evac", b) in guard:
                    p.wait("tensor", "ev", guard[("evac", b)])
            p.wait("tensor", "dv", guard[("hs",)])
            for ic in range(NIC):
                sl = ic % 2
                if ("wd8", sl) in guard:
                    p.wait("sync", "cv", guard[("wd8", sl)])
                p.dma("sync", wd8[sl][:],
                      dwT[ic * 128:(ic + 1) * 128, og * 512:(og + 1) * 512], f"wde{sl}")
                p.wait("gpsimd", f"wde{sl}")
                if ("wdb", sl) in guard:
                    p.wait("gpsimd", "pg", guard[("wdb", sl)])
                p.op("gpsimd", lambda h, sl=sl: h.tensor_copy(out=wdb[sl], in_=wd8[sl][:]),
                     inc="cv")
                guard[("wd8", sl)] = p.semv["cv"]
                p.wait("tensor", "cv")
                for j in range(4):
                    p.op("tensor", lambda h, b=banks[j], j=j, sl=sl, ic=ic: h.matmul(
                        P[b][:, 0:512], wdb[sl][:, j * 128:(j + 1) * 128],
                        hb[:, ic * 512:(ic + 1) * 512],
                        start=(ic == 0), stop=False),
                        inc="pg" if j == 3 else None)
                guard[("wdb", sl)] = p.semv["pg"]
            for j in range(4):
                p.op("tensor", lambda h, b=banks[j]: h.matmul(
                    P[b][:, 0:512], m128[0:1, :], hsb[0:1, 0:512],
                    start=False, stop=True), inc="pg" if j == 3 else None)
            chains_done = p.semv["pg"]
            for j in range(4):
                oc = og * 4 + j
                xs = slice((j % 2) * 512, (j % 2) * 512 + 512)
                if ("xq", j % 2) in guard:
                    p.wait("sync", "ev", guard[("xq", j % 2)])
                p.dma("sync", xq[:, xs], xT[oc * 128:(oc + 1) * 128, t0:t0 + 512],
                      f"ld{j % 2}")
                p.wait("vector", "pg", chains_done)
                p.wait("vector", f"ld{j % 2}")
                p.wait("vector", f"yo{j % 2}")
                p.op("vector", lambda h, b=banks[j], xs=xs, oc=oc: h.scalar_tensor_tensor(
                    out=oute[:, xs], in0=P[b][:, 0:512], scalar=dcolh[:, oc:oc + 1],
                    in1=xq[:, xs], op0=ALU.mult, op1=ALU.add), inc="ev")
                guard[("evac", banks[j])] = p.semv["ev"]
                guard[("xq", j % 2)] = p.semv["ev"]
                p.wait("gpsimd", "ev")
                p.dma("gpsimd", yT[oc * 128:(oc + 1) * 128, t0:t0 + 512],
                      oute[:, xs], f"yo{j % 2}")
    p.wait("gpsimd", "yo0")
    p.wait("gpsimd", "yo1")
    return p.build()


def stage_l1ffn_inputs(x_slice_T, gate_w, gate_scale, up_w, up_scale, down_w, down_scale):
    return {
        "xT": np.ascontiguousarray(x_slice_T, dtype=np.float32),
        "gwT": np.ascontiguousarray(np.asarray(gate_w).T).astype(np.uint8),
        "uwT": np.ascontiguousarray(np.asarray(up_w).T).astype(np.uint8),
        "dwT": np.ascontiguousarray(np.asarray(down_w).T).astype(np.uint8),
        "gsc": np.ascontiguousarray(np.asarray(gate_scale), dtype=np.float32).reshape(-1, 1),
        "usc": np.ascontiguousarray(np.asarray(up_scale), dtype=np.float32).reshape(-1, 1),
        "dsc": np.ascontiguousarray(np.asarray(down_scale), dtype=np.float32).reshape(-1, 1),
    }



def build_attn_front(TT=1024):
    D, NCH = 2048, 16
    p = Prog("afront")
    nc = p.nc
    xT = nc.dram_tensor("xT", [D, TT], F32, kind="ExternalInput").ap()
    wqkvT = nc.dram_tensor("wqkvT", [D, 3 * D], F32, kind="ExternalInput").ap()
    qnw = nc.dram_tensor("qnw", [D, 1], F32, kind="ExternalInput").ap()
    knw = nc.dram_tensor("knw", [D, 1], F32, kind="ExternalInput").ap()
    cosb = nc.dram_tensor("cosb", [128, TT], F32, kind="ExternalInput").ap()
    sinb = nc.dram_tensor("sinb", [128, TT], F32, kind="ExternalInput").ap()
    permT = nc.dram_tensor("permT", [128, 128], BF16, kind="ExternalInput").ap()
    qT = nc.dram_tensor("qT", [D, TT], BF16, kind="ExternalOutput").ap()
    kT = nc.dram_tensor("kT", [D, TT], BF16, kind="ExternalOutput").ap()
    vT = nc.dram_tensor("vT", [D, TT], BF16, kind="ExternalOutput").ap()

    for sn in ("ld0", "ld1", "ldc", "wd0", "wd1", "cv", "dv", "pe", "sc", "ev",
               "pg", "yo0", "yo1", "yo2", "yo3"):
        p.sem(sn)

    s = NS()
    AB = 58 * 1024
    s.A = p.sbuf("A", [128, AB], BF16)
    s.xb = s.A[:, 0:NCH * TT]
    s.xn = s.A[:, NCH * TT:2 * NCH * TT]
    q2 = s.A[:, 2 * NCH * TT: 2 * NCH * TT + TT]
    qnb = s.A[:, 2 * NCH * TT + TT: 2 * NCH * TT + 2 * TT]
    o_q = 2 * NCH * TT + 2 * TT
    qf = s.A[:, o_q: o_q + NCH * TT]
    o_w = o_q + NCH * TT
    wslab_b = [s.A[:, o_w + k * 2048: o_w + (k + 1) * 2048] for k in range(2)]
    outb = [s.A[:, o_w + 2 * 2048 + k * 512: o_w + 2 * 2048 + (k + 1) * 512]
            for k in range(4)]
    permb = s.A[:, o_w + 2 * 2048 + 4 * 512: o_w + 2 * 2048 + 4 * 512 + 128]
    assert o_w + 2 * 2048 + 4 * 512 + 128 <= AB
    wslab_f = [p.sbuf(f"wsf{k}", [128, 2048], F32) for k in range(2)]
    s.xtmp = p.sbuf("xtmp", [128, 2 * TT], F32)
    s.ones_l = p.sbuf("ones_l", [128, 1], BF16)
    s.ones_c = p.sbuf("ones_c", [1, 128], F32)
    s.epsc = p.sbuf("epsc", [128, 1], F32)
    s.eps6 = p.sbuf("eps6", [128, 1], F32)
    s.r0 = p.sbuf("r0", [1, TT], F32)
    s.r1 = p.sbuf("r1", [1, TT], F32)
    s.r2 = p.sbuf("r2", [1, TT], F32)
    s.r3 = p.sbuf("r3", [1, TT], F32)
    s.mb = p.sbuf("mb", [128, TT], F32)
    s.rb = p.sbuf("rb", [128, TT], F32)
    cosf = p.sbuf("cosf", [128, TT], F32)
    sinf = p.sbuf("sinf", [128, TT], F32)
    nwcol = p.sbuf("nwcol", [128, 2 * NCH], F32)
    t1 = p.sbuf("t1", [128, TT], F32)
    t2 = p.sbuf("t2", [128, TT], F32)
    qsw = p.sbuf("qsw", [128, TT], F32)
    P = [p.psum(f"P{i}", [128, 512]) for i in range(8)]
    pa, pb = P[0], P[1]

    p.op("vector", lambda h: h.memset(s.ones_l[:], 1.0))
    p.op("vector", lambda h: h.memset(s.epsc[:], 1e-5))
    p.op("vector", lambda h: h.memset(s.eps6[:], 1e-6))
    p.op("vector", lambda h: h.memset(s.ones_c[:], 1.0), inc="dv")
    p.dma("sync", cosf[:], cosb, "ldc")
    p.dma("sync", sinf[:], sinb, "ldc")
    p.dma("sync", permb, permT, "ldc")
    p.dma("sync", nwcol[:, 0:NCH], qnw.rearrange("(k p) f -> p (k f)", p=128),
          "ldc", slow=True)
    p.dma("sync", nwcol[:, NCH:2 * NCH], knw.rearrange("(k p) f -> p (k f)", p=128),
          "ldc", slow=True)
    p.wait("vector", "ldc")
    p.wait("tensor", "ldc")
    emit_load_x_as_bf16(p, s, xT, TT, NCH)
    emit_ln(p, s, pa, pb, TT, NCH)

    guard = {}
    def qkv_chunk(oc, bk_pair, out_cb):
        sl = oc % 2
        if ("w", sl) in guard:
            p.wait("sync", "cv", guard[("w", sl)])
        p.dma("sync", wslab_f[sl].rearrange("p (k i) -> p k i", k=16),
              wqkvT[:, oc * 128:(oc + 1) * 128].rearrange("(k p) i -> p k i", p=128),
              f"wd{sl}")
        p.wait("vector", f"wd{sl}")
        if ("wb", sl) in guard:
            p.wait("vector", "pg", guard[("wb", sl)])
        p.op("vector", lambda h, sl=sl: h.tensor_copy(out=wslab_b[sl], in_=wslab_f[sl][:]),
             inc="cv")
        guard[("w", sl)] = p.semv["cv"]
        p.wait("tensor", "cv")
        p.wait("tensor", "dv")
        for th in range(2):
            bk = bk_pair[th]
            if ("evac", bk) in guard:
                p.wait("tensor", "ev", guard[("evac", bk)])
            for k in range(NCH):
                p.op("tensor", lambda h, k=k, sl=sl, bk=bk, th=th,
                     st=(k == 0), sp=(k == NCH - 1): h.matmul(
                         P[bk][:, 0:512], wslab_b[sl][:, k * 128:(k + 1) * 128],
                         s.xn[:, k * TT + th * 512: k * TT + th * 512 + 512],
                         start=st, stop=sp), inc="pg" if k == NCH - 1 else None)
        guard[("wb", sl)] = p.semv["pg"]
        for th in range(2):
            out_cb(bk_pair[th], th)

    for oc in range(32, 48):
        def v_out(bk, th, oc=oc):
            slot = 2 * (oc % 2) + th
            p.wait("scalar", "pg")
            p.wait("scalar", f"yo{slot}")
            p.op("scalar", lambda h, bk=bk, slot=slot: h.copy(
                out=outb[slot], in_=P[bk][:, 0:512]), inc="ev")
            guard[("evac", bk)] = p.semv["ev"]
            p.wait("gpsimd", "ev")
            p.dma("gpsimd", vT[(oc - 32) * 128:(oc - 31) * 128, th * 512:th * 512 + 512],
                  outb[slot], f"yo{slot}")
        qkv_chunk(oc, (2 + 2 * (oc % 2), 3 + 2 * (oc % 2) - 1 + 1), v_out)

    p.wait("vector", "yo2")
    p.wait("vector", "yo3")
    for which, base_oc, wofs, outT in (("q", 0, 0, qT), ("k", 16, NCH, kT)):
        for oc in range(base_oc, base_oc + 16):
            def f_out(bk, th, oc=oc):
                c = oc - base_oc
                p.wait("vector", "pg")
                p.op("vector", lambda h, bk=bk, th=th, c=c: h.tensor_copy(
                    out=qf[:, c * TT + th * 512: c * TT + th * 512 + 512],
                    in_=P[bk][:, 0:512]), inc="ev")
                guard[("evac", bk)] = p.semv["ev"]
            qkv_chunk(oc, (2 + 2 * (oc % 2), 3 + 2 * (oc % 2) - 1 + 1), f_out)
        for th in range(2):
            p.wait("vector", "ev")
            for c in range(NCH):
                sl = slice(c * TT + th * 512, c * TT + th * 512 + 512)
                q2s = slice((c % 2) * 512, (c % 2) * 512 + 512)
                if c >= 2:
                    p.wait("vector", "pe", guard[("q2", c % 2)])
                p.op("vector", lambda h, sl=sl, q2s=q2s: h.tensor_tensor(
                    out=q2[:, q2s], in0=qf[:, sl], in1=qf[:, sl], op=ALU.mult),
                    inc="dv")
                p.wait("tensor", "dv")
                p.op("tensor", lambda h, q2s=q2s, st=(c == 0), sp=(c == NCH - 1): h.matmul(
                    pa[0:1, :], s.ones_l[:, 0:1], q2[:, q2s],
                    start=st, stop=sp), inc="pe")
                guard[("q2", c % 2)] = p.semv["pe"]
            p.wait("vector", "pe")
            p.op("vector", lambda h, th=th: h.tensor_scalar(
                out=s.r1[0:1, th * 512:(th + 1) * 512], in0=pa[0:1, :],
                scalar1=1.0 / D, scalar2=None, op0=ALU.mult), inc="dv")
        p.wait("scalar", "dv")
        p.op("scalar", lambda h: h.activation(out=s.r2[0:1, 0:TT], in_=s.r1[0:1, 0:TT],
                                              func=AF.Ln, bias=s.eps6[0:1, 0:1]))
        p.op("scalar", lambda h: h.activation(out=s.r3[0:1, 0:TT], in_=s.r2[0:1, 0:TT],
                                              func=AF.Exp, scale=-0.5), inc="sc")
        for th in range(2):
            p.wait("tensor", "sc")
            p.wait("tensor", "dv")
            p.op("tensor", lambda h, th=th: h.matmul(
                pb[:, 0:512], s.ones_c[0:1, :], s.r3[0:1, th * 512:(th + 1) * 512],
                start=True, stop=True), inc="pe")
            p.wait("vector", "pe")
            p.op("vector", lambda h, th=th: h.tensor_copy(
                out=s.rb[:, th * 512:(th + 1) * 512], in_=pb[:, 0:512]), inc="dv")
        for c in range(NCH):
            csl = slice(c * TT, (c + 1) * TT)
            p.op("vector", lambda h, csl=csl, c=c, wofs=wofs: h.scalar_tensor_tensor(
                out=qf[:, csl], in0=qf[:, csl], scalar=nwcol[:, wofs + c: wofs + c + 1],
                in1=s.rb[:, 0:TT], op0=ALU.mult, op1=ALU.mult))
            p.op("vector", lambda h, csl=csl: h.tensor_copy(out=qnb, in_=qf[:, csl]),
                 inc="dv")
            for th in range(2):
                p.wait("tensor", "dv")
                if ("psw", th) in guard:
                    p.wait("tensor", "ev", guard[("psw", th)])
                p.op("tensor", lambda h, th=th: h.matmul(
                    P[6 + th][:, 0:512], permb, qnb[th * 512:(th + 1) * 512]
                    if False else qnb[:, th * 512:(th + 1) * 512],
                    start=True, stop=True), inc="pe")
                p.wait("vector", "pe")
                p.op("vector", lambda h, th=th: h.tensor_copy(
                    out=qsw[:, th * 512:(th + 1) * 512], in_=P[6 + th][:, 0:512]),
                    inc="ev")
                guard[("psw", th)] = p.semv["ev"]
            p.op("vector", lambda h, csl=csl: h.tensor_tensor(
                out=t1[:], in0=qf[:, csl], in1=cosf[:], op=ALU.mult))
            p.op("vector", lambda h: h.tensor_tensor(
                out=t2[:], in0=qsw[:], in1=sinf[:], op=ALU.mult))
            slot = c % 2
            p.wait("vector", f"yo{slot}")
            p.op("vector", lambda h, slot=slot: h.tensor_tensor(
                out=outb[slot], in_=None, in0=t1[:, 0:512], in1=t2[:, 0:512],
                op=ALU.add) if False else h.tensor_tensor(
                out=outb[slot], in0=t1[:, 0:512], in1=t2[:, 0:512], op=ALU.add))
            p.op("vector", lambda h, slot=slot: h.tensor_tensor(
                out=outb[2 + slot], in0=t1[:, 512:1024], in1=t2[:, 512:1024],
                op=ALU.add), inc="dv")
            p.wait("gpsimd", "dv")
            p.dma("gpsimd", outT[c * 128:(c + 1) * 128, 0:512], outb[slot], f"yo{slot}")
            p.dma("gpsimd", outT[c * 128:(c + 1) * 128, 512:1024], outb[2 + slot],
                  f"yo{slot}")
    for sn in ("yo0", "yo1", "yo2", "yo3"):
        p.wait("gpsimd", sn)
    return p.build()



def build_attn_core(NH=2, T=8192, W=1024):
    KB = T + W
    NQ = T // 128
    p = Prog("acore")
    nc = p.nc
    qh = nc.dram_tensor("qh", [NH * 128, T], BF16, kind="ExternalInput").ap()
    kh = nc.dram_tensor("kh", [NH * 128, T], BF16, kind="ExternalInput").ap()
    vh = nc.dram_tensor("vh", [NH * 128, T], BF16, kind="ExternalInput").ap()
    maskb = nc.dram_tensor("maskb", [128, 9 * 1152], F32, kind="ExternalInput").ap()
    ident = nc.dram_tensor("ident", [128, 128], BF16, kind="ExternalInput").ap()
    oh = nc.dram_tensor("oh", [NH * 128, T], BF16, kind="ExternalOutput").ap()

    for sn in ("ld", "dv", "pe", "sc", "ev", "pg", "yo0", "yo1"):
        p.sem(sn)

    qb_t = p.sbuf("qb_t", [128, T], BF16)
    kbuf = p.sbuf("kbuf", [128, KB], BF16)
    vbuf = p.sbuf("vbuf", [128, KB], BF16)
    VT = p.sbuf("VT", [128, KB], BF16)
    maskt = p.sbuf("maskt", [128, 9 * 1152], F32)
    identt = p.sbuf("identt", [128, 128], BF16)
    S = p.sbuf("S", [128, 1152], F32)
    Pb = p.sbuf("Pb", [128, 1152], BF16)
    PT = p.sbuf("PT", [128, 9 * 128], BF16)
    mx = p.sbuf("mx", [128, 4], F32)
    ob = p.sbuf("ob", [128, 2 * 128], BF16)
    scol = p.sbuf("scol", [128, 1], F32)
    P = [p.psum(f"P{i}", [128, 512]) for i in range(5)]
    ptr = p.psum("ptr", [128, 128], BF16)
    pvt = p.psum("pvt", [128, 128], BF16)

    p.dma("sync", maskt[:], maskb, "ld")
    p.dma("sync", identt[:], ident, "ld")
    p.op("vector", lambda h: h.memset(scol[:], 2 ** -3.5), inc="dv")

    guard = {}
    for head in range(NH):
        if head > 0:
            p.wait("sync", "pg")
            p.wait("sync", "ev")
        r0 = head * 128
        p.dma("sync", qb_t[:], qh[r0:r0 + 128, :], "ld")
        p.dma("sync", kbuf[:, W:KB], kh[r0:r0 + 128, :], "ld")
        p.dma("sync", vbuf[:, W:KB], vh[r0:r0 + 128, :], "ld")
        p.wait("vector", "ld")
        if head == 0:
            p.op("vector", lambda h: h.memset(kbuf[:, 0:W], 0.0))
            p.op("vector", lambda h: h.memset(vbuf[:, 0:W], 0.0), inc="dv")
        for kc in range(KB // 128):
            if ("vt",) in guard:
                p.wait("tensor", "ev", guard[("vt",)])
            p.wait("tensor", "ld")
            p.wait("tensor", "dv")
            p.op("tensor", lambda h, kc=kc: h.transpose(
                pvt[:], vbuf[:, kc * 128:(kc + 1) * 128], identt[:]), inc="pe")
            p.wait("vector", "pe")
            p.op("vector", lambda h, kc=kc: h.tensor_copy(
                out=VT[:, kc * 128:(kc + 1) * 128], in_=pvt[:]), inc="ev")
            guard[("vt",)] = p.semv["ev"]
        for qb in range(NQ):
            var = min(qb, 8)
            k0 = qb * 128
            segs = [(0, 512, 0), (512, 512, 1), (1024, 128, 2)]
            for (o, n, bk) in segs:
                if ("sb", bk) in guard:
                    p.wait("tensor", "ev", guard[("sb", bk)])
                p.op("tensor", lambda h, o=o, n=n, bk=bk, k0=k0, qb=qb: h.matmul(
                    P[bk][:, 0:n], qb_t[:, qb * 128:(qb + 1) * 128],
                    kbuf[:, k0 + o: k0 + o + n], start=True, stop=True),
                    inc="pg" if bk == 2 else None)
            p.wait("vector", "pg")
            for (o, n, bk) in segs:
                p.op("vector", lambda h, o=o, n=n, bk=bk, var=var: h.scalar_tensor_tensor(
                    out=S[:, o:o + n], in0=P[bk][:, 0:n], scalar=scol[:, 0:1],
                    in1=maskt[:, var * 1152 + o: var * 1152 + o + n],
                    op0=ALU.mult, op1=ALU.add), inc="ev" if bk == 2 else None)
            guard[("sb", 0)] = guard[("sb", 1)] = guard[("sb", 2)] = p.semv["ev"]
            p.op("vector", lambda h: h.reduce_max(out=mx[:, 0:1], in_=S[:], axis=AX.X))
            p.op("vector", lambda h: h.tensor_scalar(
                out=mx[:, 1:2], in0=mx[:, 0:1], scalar1=-1.0, scalar2=None,
                op0=ALU.mult), inc="dv")
            p.wait("scalar", "dv")
            p.op("scalar", lambda h: h.activation(
                out=Pb[:], in_=S[:], func=AF.Exp, bias=mx[:, 1:2]), inc="sc")
            p.wait("vector", "sc")
            p.op("vector", lambda h: h.reduce_sum(out=mx[:, 2:3], in_=Pb[:], axis=AX.X))
            p.op("vector", lambda h: h.reciprocal(out=mx[:, 3:4], in_=mx[:, 2:3]))
            p.op("vector", lambda h: h.tensor_scalar(
                out=Pb[:], in0=Pb[:], scalar1=mx[:, 3:4], scalar2=None,
                op0=ALU.mult), inc="dv")
            for kc in range(9):
                p.wait("tensor", "dv")
                if ("pt",) in guard:
                    p.wait("tensor", "ev", guard[("pt",)])
                p.op("tensor", lambda h, kc=kc: h.transpose(
                    ptr[:], Pb[:, kc * 128:(kc + 1) * 128], identt[:]), inc="pe")
                p.wait("vector", "pe")
                p.op("vector", lambda h, kc=kc: h.tensor_copy(
                    out=PT[:, kc * 128:(kc + 1) * 128], in_=ptr[:]), inc="ev")
                guard[("pt",)] = p.semv["ev"]
            if ("ob",) in guard:
                p.wait("tensor", "ev", guard[("ob",)])
            p.wait("tensor", "ev", guard[("pt",)])
            for kc in range(9):
                p.op("tensor", lambda h, kc=kc, qb=qb: h.matmul(
                    P[4][:, 0:128], VT[:, (qb + kc) * 128:(qb + kc + 1) * 128],
                    PT[:, kc * 128:(kc + 1) * 128],
                    start=(kc == 0), stop=(kc == 8)), inc="pg" if kc == 8 else None)
            sl = qb % 2
            p.wait("vector", "pg")
            p.wait("vector", f"yo{sl}")
            p.op("vector", lambda h, sl=sl: h.tensor_copy(
                out=ob[:, sl * 128:(sl + 1) * 128], in_=P[4][:, 0:128]), inc="ev")
            guard[("ob",)] = p.semv["ev"]
            p.wait("gpsimd", "ev")
            p.dma("gpsimd", oh[r0:r0 + 128, qb * 128:(qb + 1) * 128],
                  ob[:, sl * 128:(sl + 1) * 128], f"yo{sl}")
    p.wait("gpsimd", "yo0")
    p.wait("gpsimd", "yo1")
    return p.build()


def make_attn_masks():
    m = np.full((9, 128, 1152), -1e30, dtype=np.float32)
    for v in range(9):
        for ql in range(128):
            lo = ql + 1
            if v < 8:
                lo = max(lo, 1024 - v * 128)
            m[v, ql, lo:ql + 1025] = 0.0
    return np.ascontiguousarray(m.transpose(1, 0, 2).reshape(128, 9 * 1152))


def make_rope_tables(t0, TT):
    pos = np.arange(t0, t0 + TT, dtype=np.float64)
    inv = 1.0 / (10000.0 ** (np.arange(0, 128, 2, dtype=np.float64) / 128.0))
    fr = pos[None, :] * inv[:, None]
    cos = np.cos(fr).astype(np.float32)
    sin = np.sin(fr).astype(np.float32)
    cosb = np.concatenate([cos, cos], axis=0)
    sinb = np.concatenate([sin, sin], axis=0)
    return np.ascontiguousarray(cosb), np.ascontiguousarray(sinb)


def make_perm():
    Pm = np.zeros((128, 128), dtype=np.float32)
    for d in range(64):
        Pm[d, 64 + d] = -1.0
        Pm[64 + d, d] = 1.0
    return np_bf16(Pm.T)
"""Single-launch megakernel for mk_macaron_d2 (token-sharded, x resident in SBUF).

Exchanges (gpsimd collective AllGather on internal DRAM):
  AG-W: ghost bf16 weight shares (each core dequants 1/8, straightened to WgS/WuS/WdS)
  AG-T: mamba conv h_in 4-token tails (neighbor selected by per-core mask)
  AG-S: mamba scan chunk summaries (P_chunk=exp(A*sum dt), h_end)
  AG-KV: roped K | V (attention halo; neighbor slice read via register-offset DMA)
Per-core data keeps the program uniform: rope tables (absolute positions), band
masks (variant pre-selected per core), rank prefix masks, neighbor masks/offsets.
"""
import numpy as np
import concourse.bass as bass

D, TT, NCH = 2048, 1024, 16
I5, IC5 = 5504, 43
I4, I8 = 4096, 8192

DMA_SEMS = ("ld0", "ld1", "ldc", "wd0", "wd1", "wdc0", "wdc1", "wde0", "wde1",
            "so0", "so1", "yo0", "yo1", "st0", "st1")


def build_mega(phases=("g1", "mamba", "g2", "f1", "attn", "f2"), dbg=None):
    p = Prog("mega")
    nc = p.nc

    xT = nc.dram_tensor("xT", [D, TT], F32, kind="ExternalInput").ap()
    yT = nc.dram_tensor("yT", [D, TT], F32, kind="ExternalOutput").ap()
    gidxT_s = nc.dram_tensor("gidxT_s", [256, I5], U8, kind="ExternalInput").ap()
    uidxT_s = nc.dram_tensor("uidxT_s", [256, I5], U8, kind="ExternalInput").ap()
    didxT_s = nc.dram_tensor("didxT_s", [I5, 256], U8, kind="ExternalInput").ap()
    lut = nc.dram_tensor("lut", [1, 16], F32, kind="ExternalInput").ap()
    wipT = nc.dram_tensor("wipT", [D, 2 * I4], F32, kind="ExternalInput").ap()
    convw = nc.dram_tensor("convw", [I4, 4], F32, kind="ExternalInput").ap()
    convb = nc.dram_tensor("convb", [I4, 1], F32, kind="ExternalInput").ap()
    xpT = nc.dram_tensor("xpT", [I4, 160], F32, kind="ExternalInput").ap()
    dtpT = nc.dram_tensor("dtpT", [128, I4], F32, kind="ExternalInput").ap()
    dtb = nc.dram_tensor("dtb", [I4, 1], F32, kind="ExternalInput").ap()
    Alog = nc.dram_tensor("Alog", [I4, 16], F32, kind="ExternalInput").ap()
    Dvec = nc.dram_tensor("Dvec", [I4, 1], F32, kind="ExternalInput").ap()
    wopT = nc.dram_tensor("wopT", [I4, D], F32, kind="ExternalInput").ap()
    sel = nc.dram_tensor("sel", [64, 16 * 128], BF16, kind="ExternalInput").ap()
    rmask = nc.dram_tensor("rmask", [128, 8], F32, kind="ExternalInput").ap()
    nbrm = nc.dram_tensor("nbrm", [128, 8], F32, kind="ExternalInput").ap()
    import concourse.mybir as mybir
    nbro = nc.dram_tensor("nbro", [128, 32], mybir.dt.uint32, kind="ExternalInput").ap()
    gwT = nc.dram_tensor("gwT", [D, I8], U8, kind="ExternalInput").ap()
    uwT = nc.dram_tensor("uwT", [D, I8], U8, kind="ExternalInput").ap()
    dwT = nc.dram_tensor("dwT", [I8, D], U8, kind="ExternalInput").ap()
    gsc = nc.dram_tensor("gsc", [I8, 1], F32, kind="ExternalInput").ap()
    usc = nc.dram_tensor("usc", [I8, 1], F32, kind="ExternalInput").ap()
    dsc = nc.dram_tensor("dsc", [D, 1], F32, kind="ExternalInput").ap()
    wqkvT = nc.dram_tensor("wqkvT", [D, 3 * D], F32, kind="ExternalInput").ap()
    qnw = nc.dram_tensor("qnw", [D, 1], F32, kind="ExternalInput").ap()
    knw = nc.dram_tensor("knw", [D, 1], F32, kind="ExternalInput").ap()
    woT = nc.dram_tensor("woT", [D, D], F32, kind="ExternalInput").ap()
    cosb = nc.dram_tensor("cosb", [128, TT], F32, kind="ExternalInput").ap()
    sinb = nc.dram_tensor("sinb", [128, TT], F32, kind="ExternalInput").ap()
    permT = nc.dram_tensor("permT", [128, 128], BF16, kind="ExternalInput").ap()
    maskb = nc.dram_tensor("maskb", [128, 8 * 1152], BF16, kind="ExternalInput").ap()
    identb_i = nc.dram_tensor("identb_i", [128, 128], BF16, kind="ExternalInput").ap()
    hflag = nc.dram_tensor("hflag", [128, 1], F32, kind="ExternalInput").ap()

    if dbg == "gw":
        dbg_wg = nc.dram_tensor("dbg_wg", [D, 512], BF16, kind="ExternalOutput").ap()
        dbg_wd = nc.dram_tensor("dbg_wd", [512, D], BF16, kind="ExternalOutput").ap()
    if dbg == "xn":
        dbg_xn = nc.dram_tensor("dbg_xn", [128, 16 * TT], BF16, kind="ExternalOutput").ap()
    if dbg == "mamba":
        dbg_u = nc.dram_tensor("dbg_u", [I4, TT], BF16, kind="ExternalOutput").ap()
        dbg_sg = nc.dram_tensor("dbg_sg", [I4, TT], BF16, kind="ExternalOutput").ap()
        dbg_yg = nc.dram_tensor("dbg_yg", [I4, TT], BF16, kind="ExternalOutput").ap()
        dbg_bc = nc.dram_tensor("dbg_bc", [64, TT], BF16, kind="ExternalOutput").ap()
        dbg_ts = nc.dram_tensor("dbg_ts", [128, TT], BF16, kind="ExternalOutput").ap()
        dbg_xp = nc.dram_tensor("dbg_xp", [128, 5120], BF16, kind="ExternalOutput").ap()
    if dbg == "xr":
        dbg_x1 = nc.dram_tensor("dbg_x1", [128, 16 * TT], F32, kind="ExternalOutput").ap()
        dbg_x2 = nc.dram_tensor("dbg_x2", [128, 16 * TT], F32, kind="ExternalOutput").ap()
        dbg_x3 = nc.dram_tensor("dbg_x3", [128, 16 * TT], F32, kind="ExternalOutput").ap()
    if dbg == "at":
        dbg_q = nc.dram_tensor("dbg_q", [D, TT], BF16, kind="ExternalOutput").ap()
        dbg_k = nc.dram_tensor("dbg_k", [D, TT], BF16, kind="ExternalOutput").ap()
        dbg_v = nc.dram_tensor("dbg_v", [D, TT], BF16, kind="ExternalOutput").ap()
        dbg_ao = nc.dram_tensor("dbg_ao", [128, 16 * TT], BF16, kind="ExternalOutput").ap()
    if dbg == "l1":
        dbg_h1 = nc.dram_tensor("dbg_h1", [I8, TT], BF16, kind="ExternalOutput").ap()
        dbg_xs = nc.dram_tensor("dbg_xs", [1, TT], F32, kind="ExternalOutput").ap()
    if dbg == "hb":
        dbg_hb = nc.dram_tensor("dbg_hb", [128, IC5 * 512], BF16, kind="ExternalOutput").ap()
    NGW = 2 * 256 * I5 + I5 * 256
    g_cc_in = nc.dram_tensor("g_cc_in", [NGW], BF16).ap()
    g_cc_out = nc.dram_tensor("g_cc_out", [8 * NGW], BF16).ap()
    WgS = nc.dram_tensor("WgS", [D, I5], BF16).ap()
    WuS = nc.dram_tensor("WuS", [D, I5], BF16).ap()
    WdS = nc.dram_tensor("WdS", [I5, D], BF16).ap()
    t_cc_in = nc.dram_tensor("t_cc_in", [I4, 4], BF16).ap()
    t_cc_out = nc.dram_tensor("t_cc_out", [8 * I4, 4], BF16).ap()
    s_cc_in = nc.dram_tensor("s_cc_in", [I4, 32], F32).ap()
    s_cc_out = nc.dram_tensor("s_cc_out", [8 * I4, 32], F32).ap()
    kv_cc_in = nc.dram_tensor("kv_cc_in", [2 * D, TT], BF16).ap()
    kv_cc_out_t = nc.dram_tensor("kv_cc_out", [8 * 2 * D, TT], BF16)
    kv_cc_out = kv_cc_out_t.ap()
    u_sp = nc.dram_tensor("u_sp", [I4, TT], BF16).ap()
    sg_sp = nc.dram_tensor("sg_sp", [I4, TT], BF16).ap()
    y1_sp = nc.dram_tensor("y1_sp", [I4, TT], F32).ap()
    yg_sp = nc.dram_tensor("yg_sp", [I4, TT], BF16).ap()
    kraw_sp = nc.dram_tensor("kraw_sp", [D, TT], BF16).ap()
    qraw_sp = nc.dram_tensor("qraw_sp", [D, TT], BF16).ap()
    q_sp = nc.dram_tensor("q_sp", [D, TT], BF16).ap()
    h_sp = nc.dram_tensor("h_sp", [I8, TT], BF16).ap()

    for sn in DMA_SEMS:
        p.sem(sn)
    for sn in ("cv", "dq", "pg", "ev", "dv", "pe", "sc", "cc", "hs"):
        p.sem(sn)

    s = NS()
    XR = p.sbuf("XR", [128, NCH * TT], F32)
    AE = 47 * 1024
    A = p.sbuf("A", [128, AE], BF16)
    FE = 8 * 1024
    Fa = p.sbuf("Fa", [128, FE], F32)
    s.xn = A[:, 0:16 * 1024]
    s.xb = A[:, 16 * 1024:32 * 1024]
    ARE = 32 * 1024
    s.mb = Fa[:, 0:TT]
    s.rb = Fa[:, TT:2 * TT]
    s.r0 = Fa[0:1, 2 * TT:3 * TT]
    s.r1 = Fa[0:1, 3 * TT:4 * TT]
    s.r2 = Fa[0:1, 4 * TT:5 * TT]
    s.r3 = Fa[0:1, 5 * TT:6 * TT]
    s.xtmp = Fa[:, 6 * TT:8 * TT]
    s.ones_l = p.sbuf("ones_l", [128, 1], BF16)
    s.ones_c = p.sbuf("ones_c", [1, 128], F32)
    s.epsc = p.sbuf("epsc", [128, 1], F32)
    s.eps6 = p.sbuf("eps6", [128, 1], F32)
    lutr = p.sbuf("lutr", [1, 16], F32)
    lut128 = p.sbuf("lut128", [128, 16], F32)
    m128 = p.sbuf("m128", [1, 128], BF16)
    gsil = A[:, 47616:48128]
    P = [p.psum(f"P{i}", [128, 512]) for i in range(7)]
    ptx = p.psum("ptx", [128, 1024], BF16)
    ptr = ptx[:, 0:128]
    pvt = ptx[:, 128:256]
    pa, pb = P[0], P[1]
    guard = {}

    def bankw(eng, *banks):
        for b in banks:
            if ("evac", b) in guard:
                p.wait(eng, "ev", guard[("evac", b)])
            if ("evacdv", b) in guard:
                p.wait(eng, "dv", guard[("evacdv", b)])

    def dma_barrier():
        for sn in DMA_SEMS:
            if p.semv[sn] > 0:
                for e_ in ("sync", "gpsimd", "vector", "scalar", "tensor"):
                    p.wait(e_, sn)
        for sn in ("pg", "ev", "dv", "sc", "cv"):
            if p.semv[sn] > 0:
                p.wait("sync", sn)
                p.wait("gpsimd", sn)

    def ln_into_xn(eps=1e-5):
        bankw("tensor", 0, 1)
        p.wait("vector", "ev")
        p.wait("vector", "cv")
        p.wait("vector", "dv")
        for sn_ in ("so0", "so1", "yo0", "yo1", "st0", "st1"):
            if p.semv[sn_] > 0:
                p.wait("vector", sn_)
        for c in range(NCH):
            slq = slice(c * TT, (c + 1) * TT)
            p.op("vector", lambda h, slq=slq: h.tensor_copy(out=s.xb[:, slq],
                                                            in_=XR[:, slq]),
                 inc="dv" if c == NCH - 1 else None)
        emit_ln(p, s, pa, pb, TT, NCH, eps=eps)
        guard[("evacdv", 0)] = guard[("evacdv", 1)] = p.semv["dv"]

    def residual_from_psum(bank, oc, t0, scale_col=None, imm=1.0):
        xs = slice(oc * TT + t0, oc * TT + t0 + 512)
        if scale_col is not None:
            p.op("vector", lambda h, bank=bank, xs=xs, sc=scale_col:
                 h.scalar_tensor_tensor(out=XR[:, xs], in0=P[bank][:, 0:512],
                                        scalar=sc, in1=XR[:, xs],
                                        op0=ALU.mult, op1=ALU.add), inc="ev")
        else:
            p.op("vector", lambda h, bank=bank, xs=xs, imm=imm:
                 h.scalar_tensor_tensor(out=XR[:, xs], in0=P[bank][:, 0:512],
                                        scalar=imm, in1=XR[:, xs],
                                        op0=ALU.mult, op1=ALU.add), inc="ev")
        guard[("evac", bank)] = p.semv["ev"]

    p.dma("sync", lutr[:], lut, "ldc")
    p.op("vector", lambda h: h.memset(s.ones_l[:], 1.0))
    p.op("vector", lambda h: h.memset(s.epsc[:], 1e-5))
    p.op("vector", lambda h: h.memset(s.eps6[:], 1e-6))
    p.op("vector", lambda h: h.memset(m128[:], -128.0))
    p.op("vector", lambda h: h.memset(s.ones_c[:], 1.0), inc="dv")
    p.wait("tensor", "ldc")
    p.wait("tensor", "dv")
    p.op("tensor", lambda h: h.matmul(pa[:, 0:16], s.ones_c[0:1, :], lutr[0:1, :],
                                      start=True, stop=True), inc="pe")
    p.wait("vector", "pe")
    p.op("vector", lambda h: h.tensor_copy(out=lut128[:], in_=pa[:, 0:16]), inc="dv")
    for c in range(NCH):
        slot = c % 2
        if c >= 2:
            p.wait("sync", "dv", guard[("xload", c - 2)])
        p.dma("sync", s.xtmp[:, slot * TT:(slot + 1) * TT],
              xT[c * 128:(c + 1) * 128, :], f"ld{slot}")
        p.wait("vector", f"ld{slot}")
        p.op("vector", lambda h, c=c, slot=slot: h.tensor_copy(
            out=XR[:, c * TT:(c + 1) * TT], in_=s.xtmp[:, slot * TT:(slot + 1) * TT]),
            inc="dv")
        guard[("xload", c)] = p.semv["dv"]

    if "g1" in phases or "g2" in phases:
        HW = 2752
        idx_u8 = p.sbuf("idx_u8", [128, 2 * HW], U8)
        idxb = [A[:, k * HW: (k + 1) * HW] for k in range(2)]
        accb = [A[:, (2 + k) * HW: (3 + k) * HW] for k in range(2)]
        tmpb = A[:, 4 * HW: 5 * HW]
        jobs = []
        gv = g_cc_in[0:256 * I5].rearrange("(q t) -> q t", q=256)
        uv = g_cc_in[256 * I5:2 * 256 * I5].rearrange("(q t) -> q t", q=256)
        dv_ = g_cc_in[2 * 256 * I5:NGW].rearrange("(q t) -> q t", q=I5)
        for i in range(2):
            for ch in range(2):
                jobs.append((gidxT_s[i * 128:(i + 1) * 128, ch * HW:(ch + 1) * HW], HW,
                             gv[i * 128:(i + 1) * 128, ch * HW:(ch + 1) * HW]))
        for i in range(2):
            for ch in range(2):
                jobs.append((uidxT_s[i * 128:(i + 1) * 128, ch * HW:(ch + 1) * HW], HW,
                             uv[i * 128:(i + 1) * 128, ch * HW:(ch + 1) * HW]))
        for i in range(IC5):
            jobs.append((didxT_s[i * 128:(i + 1) * 128, :], 256,
                         dv_[i * 128:(i + 1) * 128, :]))
        for sidx, (src, cols, dst) in enumerate(jobs):
            sl = sidx % 2
            if sidx >= 2:
                p.wait("sync", "dq", sidx - 1)
            p.dma("sync", idx_u8[:, sl * HW: sl * HW + cols], src, f"wdc{sl}")
            p.wait("gpsimd", f"wdc{sl}")
            p.op("gpsimd", lambda h, sl=sl, cols=cols: h.tensor_copy(
                out=idxb[sl][:, 0:cols], in_=idx_u8[:, sl * HW: sl * HW + cols]),
                inc="cv")
            p.wait("vector", "cv")
            p.wait("vector", "dv")
            if sidx >= 2:
                p.wait("vector", f"so{sl}")
            p.op("vector", lambda h, sl=sl, cols=cols: h.tensor_scalar(
                out=accb[sl][:, 0:cols], in0=idxb[sl][:, 0:cols], scalar1=0.0,
                scalar2=lut128[:, 0:1], op0=ALU.is_equal, op1=ALU.mult))
            for v in range(1, 16):
                p.op("vector", lambda h, sl=sl, cols=cols, v=v: h.tensor_scalar(
                    out=tmpb[:, 0:cols], in0=idxb[sl][:, 0:cols], scalar1=float(v),
                    scalar2=lut128[:, v:v + 1], op0=ALU.is_equal, op1=ALU.mult))
                p.op("vector", lambda h, sl=sl, cols=cols: h.tensor_tensor(
                    out=accb[sl][:, 0:cols], in0=accb[sl][:, 0:cols],
                    in1=tmpb[:, 0:cols], op=ALU.add), inc="dq" if v == 15 else None)
            p.wait("gpsimd", "dq")
            p.dma("gpsimd", dst, accb[sl][:, 0:cols], f"so{sl}")
        p.wait("gpsimd", "so0")
        p.wait("gpsimd", "so1")
        p.op("gpsimd", lambda h: h.collective_compute(
            "AllGather", ALU.bypass, replica_groups=[list(range(8))],
            ins=[g_cc_in[:]], outs=[g_cc_out[:]]), inc="cc")
        p.wait("gpsimd", "cc", p.semv["cc"])
        for r in range(8):
            base = r * NGW
            p.dma("gpsimd", WgS[r * 256:(r + 1) * 256, :],
                  g_cc_out[base: base + 256 * I5].rearrange("(q t) -> q t", q=256),
                  "st0")
            p.dma("gpsimd", WuS[r * 256:(r + 1) * 256, :],
                  g_cc_out[base + 256 * I5: base + 2 * 256 * I5].rearrange(
                      "(q t) -> q t", q=256), "st0")
            p.dma("gpsimd", WdS[:, r * 256:(r + 1) * 256],
                  g_cc_out[base + 2 * 256 * I5: base + NGW].rearrange(
                      "(q t) -> q t", q=I5), "st0")
        p.wait("sync", "st0")
        if dbg == "gw":
            p.wait("gpsimd", "st0")
            for cc in range(4):
                p.dma("gpsimd", dbg_wg[:, cc * 128:(cc + 1) * 128],
                      WgS[:, cc * 128:(cc + 1) * 128], "st1")
            p.dma("gpsimd", dbg_wd[:, :],
                  WdS[1000:1512, :], "st1")
            p.wait("gpsimd", "st1")

    def ghost_sublayer():
        dma_barrier()
        hb = A[:, 16 * 1024: 16 * 1024 + IC5 * 512]
        o_w = 16 * 1024 + IC5 * 512
        wg_l = [A[:, o_w + k * 2048: o_w + (k + 1) * 2048] for k in range(2)]
        wu_l = [A[:, o_w + (2 + k) * 2048: o_w + (3 + k) * 2048] for k in range(2)]
        o_wd = o_w + 4 * 2048
        wd_l = [A[:, o_wd + k * 512: o_wd + (k + 1) * 512] for k in range(2)]
        assert o_wd + 1024 <= AE
        ln_into_xn()
        if dbg == "xn":
            p.wait("gpsimd", "dv")
            p.dma("gpsimd", dbg_xn, s.xn, "st1")
        for th in range(2):
            t0 = th * 512
            if th == 1:
                p.wait("vector", "pg")
            for ic in range(IC5):
                sl = ic % 2
                if ("gwslot", sl) in guard:
                    p.wait("sync", "pg", guard[("gwslot", sl)])
                p.dma("sync", wg_l[sl].rearrange("p (k i) -> p k i", k=16),
                      WgS[:, ic * 128:(ic + 1) * 128].rearrange(
                          "(k p) i -> p k i", p=128), f"wd{sl}")
                p.dma("sync", wu_l[sl].rearrange("p (k i) -> p k i", k=16),
                      WuS[:, ic * 128:(ic + 1) * 128].rearrange(
                          "(k p) i -> p k i", p=128), f"wd{sl}")
                bg, bu = 2 + sl, 4 + sl
                p.wait("tensor", f"wd{sl}")
                p.wait("tensor", "dv")
                bankw("tensor", bg, bu)
                wl = {bg: wg_l[sl], bu: wu_l[sl]}
                for bk in (bg, bu):
                    for k in range(NCH):
                        p.op("tensor", lambda h, k=k, bk=bk, w=wl[bk], t0=t0,
                             st=(k == 0), sp=(k == NCH - 1): h.matmul(
                                 P[bk][:, 0:512], w[:, k * 128:(k + 1) * 128],
                                 s.xn[:, k * TT + t0: k * TT + t0 + 512],
                                 start=st, stop=sp),
                             inc="pg" if k == NCH - 1 else None)
                guard[("gwslot", sl)] = p.semv["pg"]
                p.wait("scalar", "pg")
                if "gsil" in guard:
                    p.wait("scalar", "ev", guard["gsil"])
                p.op("scalar", lambda h, bg=bg: h.activation(
                    out=gsil[:, 0:512], in_=P[bg][:, 0:512], func=AF.Silu), inc="sc")
                p.wait("vector", "sc")
                p.op("vector", lambda h, bu=bu, ic=ic: h.tensor_tensor(
                    out=hb[:, ic * 512:(ic + 1) * 512], in0=gsil[:, 0:512],
                    in1=P[bu][:, 0:512], op=ALU.mult), inc="ev")
                guard[("evac", bg)] = p.semv["ev"]
                guard[("evac", bu)] = p.semv["ev"]
                guard["gsil"] = p.semv["ev"]
            p.wait("tensor", "ev")
            if dbg == "hb" and th == 0:
                p.wait("gpsimd", "ev")
                p.dma("gpsimd", dbg_hb, hb, "st1")
            banks = [6, 0, 1, 2]
            for og in range(4):
                bankw("tensor", *banks)
                for ic in range(IC5):
                    sl = ic % 2
                    if ("gwdslot", sl) in guard:
                        p.wait("sync", "pg", guard[("gwdslot", sl)])
                    p.dma("sync", wd_l[sl],
                          WdS[ic * 128:(ic + 1) * 128, og * 512:(og + 1) * 512],
                          f"wde{sl}")
                    p.wait("tensor", f"wde{sl}")
                    for j in range(4):
                        p.op("tensor", lambda h, b=banks[j], j=j, sl=sl, ic=ic:
                             h.matmul(P[b][:, 0:512],
                                      wd_l[sl][:, j * 128:(j + 1) * 128],
                                      hb[:, ic * 512:(ic + 1) * 512],
                                      start=(ic == 0), stop=(ic == IC5 - 1)),
                             inc="pg" if j == 3 else None)
                    guard[("gwdslot", sl)] = p.semv["pg"]
                chains_done = p.semv["pg"]
                p.wait("vector", "pg", chains_done)
                for j in range(4):
                    residual_from_psum(banks[j], og * 4 + j, t0, imm=0.5)

    if "g1" in phases:
        ghost_sublayer()

    if "mamba" in phases:
        dma_barrier()
        dtbc = p.sbuf("dtbc", [128, 32], F32)
        convwc = p.sbuf("convwc", [128, 128], F32)
        convbc = p.sbuf("convbc", [128, 32], F32)
        Acol = p.sbuf("Acol", [128, 512], F32)
        Dcol = p.sbuf("Dcol", [128, 32], F32)
        rmaskc = p.sbuf("rmaskc", [128, 8], F32)
        nbrmc = p.sbuf("nbrmc", [128, 8], F32)
        selt = p.sbuf("selt", [64, 2048], BF16)
        bcrow_b = p.sbuf("bcrow_b", [64, TT], BF16)
        Bb_all = A[:, 0:16384]
        Cb_all = A[:, 16384:32768]
        wslab_b = [A[:, 32768 + k * 2048: 32768 + (k + 1) * 2048] for k in range(2)]
        u_sl = [A[:, 36864 + k * 1024: 36864 + (k + 1) * 1024] for k in range(2)]
        tsb = A[:, 38912:39936]
        xpb = A[:, 39936:45056]
        ntb = A[:, 45056:45184]
        tailb = A[:, 45184:45312]
        obf = [A[:, 45312 + k * 1024: 45312 + (k + 1) * 1024] for k in range(2)]
        dtsl_b = [A[:, 47360 + k * 128: 47360 + (k + 1) * 128] for k in range(2)]
        assert 47360 + 256 <= AE
        xpf = Fa[:, 0:5120]
        wf_sl = [Fa[:, 0 + k * 2048: (k + 1) * 2048] for k in range(2)]
        hpad = Fa[:, 4096:5123]
        acc = Fa[:, 5123:6147]
        dtq = Fa[:, 0:1024]
        dtu = Fa[:, 1024:2048]
        dAt = Fa[:, 2048:3072]
        dBxt = Fa[:, 3072:4096]
        hsct = Fa[:, 4096:5120]
        yvt = Fa[:, 5120:6144]
        zrt = Fa[:, 3072:4096]
        summ = Fa[:, 6144:7168]
        hinb = Fa[:, 7168:7680]
        accsave = Fa[:, 7680:7776]
        sdtc = Fa[:, 7776:7777]
        dtpf = [Fa[0:128, 7792 + k * 128: 7792 + (k + 1) * 128] for k in range(2)]
        ntf = Fa[:, 8048:8176]
        p.dma("sync", xpf.rearrange("p (k r) -> p k r", k=32),
              xpT.rearrange("(k p) r -> p k r", p=128), "ldc")
        p.dma("sync", dtbc[:], dtb.rearrange("(k p) f -> p (k f)", p=128), "ldc",
              slow=True)
        p.dma("sync", convwc.rearrange("p (k f) -> p k f", k=32),
              convw.rearrange("(k p) f -> p k f", p=128), "ldc")
        p.dma("sync", convbc[:], convb.rearrange("(k p) f -> p (k f)", p=128), "ldc",
              slow=True)
        p.dma("sync", Acol.rearrange("p (k n) -> p k n", k=32),
              Alog.rearrange("(k p) n -> p k n", p=128), "ldc")
        p.dma("sync", Dcol[:], Dvec.rearrange("(k p) f -> p (k f)", p=128), "ldc",
              slow=True)
        p.dma("sync", rmaskc[:], rmask, "ldc")
        p.dma("sync", nbrmc[:], nbrm, "ldc")
        p.dma("sync", selt[:], sel, "ldc")
        p.wait("vector", "ldc")
        p.wait("vector", "pg")
        p.op("vector", lambda h: h.tensor_copy(out=xpb, in_=xpf), inc="dv")
        p.wait("scalar", "ldc")
        p.op("scalar", lambda h: h.activation(out=Acol[:], in_=Acol[:], func=AF.Exp),
             inc="sc")
        p.wait("vector", "sc")
        p.op("vector", lambda h: h.tensor_scalar(
            out=Acol[:], in0=Acol[:], scalar1=-1.0, scalar2=None, op0=ALU.mult),
            inc="dv")
        guard[("acoln",)] = p.semv["dv"]
        ln_into_xn()
        p.wait("sync", "dv")
        for oc in range(64):
            sl = oc % 2
            if ("mws", sl) in guard:
                p.wait("sync", "cv", guard[("mws", sl)])
            p.dma("sync", wf_sl[sl].rearrange("p (k i) -> p k i", k=16),
                  wipT[:, oc * 128:(oc + 1) * 128].rearrange("(k p) i -> p k i", p=128),
                  f"wd{sl}")
            p.wait("gpsimd", f"wd{sl}")
            if ("mwb", sl) in guard:
                p.wait("gpsimd", "pg", guard[("mwb", sl)])
            p.op("gpsimd", lambda h, sl=sl: h.tensor_copy(out=wslab_b[sl],
                                                          in_=wf_sl[sl]), inc="cv")
            guard[("mws", sl)] = p.semv["cv"]
            p.wait("tensor", "cv")
            p.wait("tensor", "dv")
            for th in range(2):
                bk = 2 + 2 * sl + th
                bankw("tensor", bk)
                for k in range(NCH):
                    p.op("tensor", lambda h, k=k, sl=sl, bk=bk, th=th,
                         st=(k == 0), sp=(k == NCH - 1): h.matmul(
                             P[bk][:, 0:512], wslab_b[sl][:, k * 128:(k + 1) * 128],
                             s.xn[:, k * TT + th * 512: k * TT + th * 512 + 512],
                             start=st, stop=sp), inc="pg" if k == NCH - 1 else None)
            guard[("mwb", sl)] = p.semv["pg"]
            if oc < 32:
                p.wait("vector", "pg")
                if ("hpad",) in guard:
                    p.wait("vector", "sc", guard[("hpad",)])
                for th in range(2):
                    bk = 2 + 2 * sl + th
                    p.op("vector", lambda h, bk=bk, th=th: h.tensor_copy(
                        out=hpad[:, 3 + th * 512: 3 + th * 512 + 512],
                        in_=P[bk][:, 0:512]), inc="ev")
                    guard[("evac", bk)] = p.semv["ev"]
                if oc == 0:
                    p.op("vector", lambda h: h.memset(hpad[:, 0:3], 0.0))
                p.op("vector", lambda h, oc=oc: h.tensor_copy(
                    out=tailb[:, oc * 4:(oc + 1) * 4], in_=hpad[:, 1023:1027]))
                p.op("vector", lambda h, oc=oc: h.tensor_scalar(
                    out=acc[:], in0=hpad[:, 0:1024], scalar1=convwc[:, 4 * oc:4 * oc + 1],
                    scalar2=None, op0=ALU.mult))
                for kk in range(1, 4):
                    p.op("vector", lambda h, oc=oc, kk=kk: h.scalar_tensor_tensor(
                        out=acc[:], in0=hpad[:, kk:kk + 1024],
                        scalar=convwc[:, 4 * oc + kk:4 * oc + kk + 1], in1=acc[:],
                        op0=ALU.mult, op1=ALU.add))
                p.op("vector", lambda h, oc=oc: h.tensor_copy(
                    out=accsave[:, oc * 3:(oc + 1) * 3], in_=acc[:, 0:3]), inc="dv")
                p.wait("scalar", "dv")
                p.wait("scalar", f"yo{sl}")
                p.op("scalar", lambda h, oc=oc, sl=sl: h.activation(
                    out=obf[sl], in_=acc[:], func=AF.Silu,
                    bias=convbc[:, oc:oc + 1]), inc="sc")
                guard[("hpad",)] = p.semv["sc"]
                p.wait("gpsimd", "sc")
                p.dma("gpsimd", u_sp[oc * 128:(oc + 1) * 128, :], obf[sl], f"yo{sl}")
            else:
                p.wait("scalar", "pg")
                p.wait("scalar", f"yo{sl}")
                for th in range(2):
                    bk = 2 + 2 * sl + th
                    p.op("scalar", lambda h, bk=bk, th=th, sl=sl: h.activation(
                        out=obf[sl][:, th * 512:(th + 1) * 512], in_=P[bk][:, 0:512],
                        func=AF.Silu), inc="sc")
                    guard[("evac", bk)] = p.semv["sc"]
                p.wait("gpsimd", "sc")
                p.dma("gpsimd", sg_sp[(oc - 32) * 128:(oc - 31) * 128, :], obf[sl],
                      f"yo{sl}")
        for bkx in range(2, 6):
            guard.pop(("evac", bkx), None)
        p.wait("tensor", "sc")
        p.wait("gpsimd", "dv")
        p.dma("gpsimd", t_cc_in.rearrange("(k p) f -> p k f", p=128),
              tailb.rearrange("p (k f) -> p k f", k=32), "st0")
        p.wait("gpsimd", "st0")
        p.op("gpsimd", lambda h: h.collective_compute(
            "AllGather", ALU.bypass, replica_groups=[list(range(8))],
            ins=[t_cc_in[:]], outs=[t_cc_out[:]]), inc="cc")
        p.wait("gpsimd", "cc", p.semv["cc"])
        p.op("vector", lambda h: h.memset(ntf[:], 0.0), inc="dv")
        p.wait("sync", "cc", p.semv["cc"])
        for r in range(8):
            p.wait("sync", "dv", p.semv["dv"])
            p.dma("sync", ntb.rearrange("p (k f) -> p k f", k=32),
                  t_cc_out[r * I4:(r + 1) * I4, :].rearrange("(k p) f -> p k f", p=128),
                  "ldc")
            p.wait("vector", "ldc")
            p.op("vector", lambda h, r=r: h.scalar_tensor_tensor(
                out=ntf[:], in0=ntb, scalar=nbrmc[:, r:r + 1], in1=ntf[:],
                op0=ALU.mult, op1=ALU.add), inc="dv")
        for oc in range(32):
            sl = oc % 2
            for (t, kk) in ((0, 0), (0, 1), (0, 2), (1, 0), (1, 1), (2, 0)):
                src = 4 * oc + 1 + t + kk
                p.op("vector", lambda h, oc=oc, t=t, kk=kk, src=src:
                     h.scalar_tensor_tensor(
                         out=accsave[:, oc * 3 + t: oc * 3 + t + 1],
                         in0=ntf[:, src:src + 1],
                         scalar=convwc[:, 4 * oc + kk:4 * oc + kk + 1],
                         in1=accsave[:, oc * 3 + t: oc * 3 + t + 1],
                         op0=ALU.mult, op1=ALU.add))
            p.op("vector", lambda h: h.nop(), inc="dv")
            p.wait("scalar", "dv")
            p.wait("scalar", f"yo{sl}")
            p.op("scalar", lambda h, oc=oc, sl=sl: h.activation(
                out=obf[sl][:, 0:3], in_=accsave[:, oc * 3:(oc + 1) * 3],
                func=AF.Silu, bias=convbc[:, oc:oc + 1]), inc="sc")
            p.wait("gpsimd", "sc")
            p.dma("gpsimd", u_sp[oc * 128:(oc + 1) * 128, 0:3], obf[sl][:, 0:3],
                  f"yo{sl}")
        p.wait("sync", "yo0")
        p.wait("sync", "yo1")
        for th in range(2):
            bts, bB, bC = 2 + th, 4 + th, (6 if th == 0 else 0)
            bankw("tensor", bts, bB, bC)
            for k in range(32):
                sl = k % 2
                if ("uxs", sl) in guard:
                    p.wait("sync", "pg", guard[("uxs", sl)])
                p.dma("sync", u_sl[sl][:, 0:512],
                      u_sp[k * 128:(k + 1) * 128, th * 512:(th + 1) * 512], f"ld{sl}")
                p.wait("tensor", f"ld{sl}")
                xv = xpb.rearrange("p (k r) -> p k r", k=32)
                p.op("tensor", lambda h, k=k, sl=sl, bts=bts, xv=xv, st=(k == 0),
                     sp=(k == 31): h.matmul(
                         P[bts][:, 0:512], xv[:, k, 0:128],
                         u_sl[sl][:, 0:512], start=st, stop=sp))
                p.op("tensor", lambda h, k=k, sl=sl, bB=bB, xv=xv, st=(k == 0),
                     sp=(k == 31): h.matmul(
                         P[bB][0:16, 0:512], xv[:, k, 128:144],
                         u_sl[sl][:, 0:512], start=st, stop=sp))
                p.op("tensor", lambda h, k=k, sl=sl, bC=bC, xv=xv, st=(k == 0),
                     sp=(k == 31): h.matmul(
                         P[bC][0:16, 0:512], xv[:, k, 144:160],
                         u_sl[sl][:, 0:512], start=st, stop=sp), inc="pg")
                guard[("uxs", sl)] = p.semv["pg"]
            p.wait("vector", "pg")
            p.op("vector", lambda h, th=th, bts=bts: h.tensor_copy(
                out=tsb[:, th * 512:(th + 1) * 512], in_=P[bts][:, 0:512]))
            p.op("vector", lambda h, th=th, bB=bB: h.tensor_copy(
                out=bcrow_b[0:16, th * 512:(th + 1) * 512], in_=P[bB][0:16, 0:512]))
            p.op("vector", lambda h, th=th, bC=bC: h.tensor_copy(
                out=bcrow_b[32:48, th * 512:(th + 1) * 512], in_=P[bC][0:16, 0:512]),
                inc="dv")
            guard[("evacdv", bts)] = p.semv["dv"]
            guard[("evacdv", bB)] = p.semv["dv"]
            guard[("evacdv", bC)] = p.semv["dv"]
        guard[("us1pg",)] = p.semv["pg"]
        p.wait("tensor", "dv")
        bankw("tensor", 0, 1)
        for n in range(16):
            for th in range(2):
                p.op("tensor", lambda h, n=n, th=th: h.matmul(
                    P[0][:, 0:512], selt[0:16, n * 128:(n + 1) * 128],
                    bcrow_b[0:16, th * 512:(th + 1) * 512], start=True, stop=True))
                p.op("tensor", lambda h, n=n, th=th: h.matmul(
                    P[1][:, 0:512], selt[32:48, n * 128:(n + 1) * 128],
                    bcrow_b[32:48, th * 512:(th + 1) * 512], start=True, stop=True),
                    inc="pe")
                p.wait("vector", "pe")
                p.op("vector", lambda h, n=n, th=th: h.tensor_copy(
                    out=Bb_all[:, n * 1024 + th * 512: n * 1024 + th * 512 + 512],
                    in_=P[0][:, 0:512]))
                p.op("vector", lambda h, n=n, th=th: h.tensor_copy(
                    out=Cb_all[:, n * 1024 + th * 512: n * 1024 + th * 512 + 512],
                    in_=P[1][:, 0:512]), inc="dv")
                p.wait("tensor", "dv")
        def emit_dt(ic):
            sl = ic % 2
            if ("dts", sl) in guard:
                p.wait("sync", "cv", guard[("dts", sl)])
            p.dma("sync", dtpf[sl], dtpT[:, ic * 128:(ic + 1) * 128], f"wd{sl}")
            p.wait("gpsimd", f"wd{sl}")
            if ("dtb", sl) in guard:
                p.wait("gpsimd", "pg", guard[("dtb", sl)])
            p.op("gpsimd", lambda h, sl=sl: h.tensor_copy(out=dtsl_b[sl],
                                                          in_=dtpf[sl]), inc="cv")
            guard[("dts", sl)] = p.semv["cv"]
            p.wait("tensor", "cv")
            for th in range(2):
                bk = 5 + th
                bankw("tensor", bk)
                p.op("tensor", lambda h, sl=sl, th=th, bk=bk: h.matmul(
                    P[bk][:, 0:512], dtsl_b[sl],
                    tsb[:, th * 512:(th + 1) * 512], start=True, stop=True), inc="pg")
            guard[("dtb", sl)] = p.semv["pg"]
            v = p.semv["pg"]
            p.wait("vector", "pg", v)
            p.wait("scalar", "pg", v)
            for th in range(2):
                bk = 5 + th
                ts_ = slice(th * 512, (th + 1) * 512)
                p.op("vector", lambda h, bk=bk, ic=ic, ts_=ts_: h.tensor_scalar(
                    out=dtu[:, ts_], in0=P[bk][:, 0:512], scalar1=dtbc[:, ic:ic + 1],
                    scalar2=0.0, op0=ALU.add, op1=ALU.max), inc="dv")
                p.wait("scalar", "dv")
                p.op("scalar", lambda h, bk=bk, ic=ic, ts_=ts_: h.activation(
                    out=dtq[:, ts_], in_=P[bk][:, 0:512], func=AF.Abs,
                    bias=dtbc[:, ic:ic + 1]), inc="ev")
                guard[("evacdv", bk)] = p.semv["dv"]
                guard[("evac", bk)] = p.semv["ev"]
                p.op("scalar", lambda h, ts_=ts_: h.activation(
                    out=dtq[:, ts_], in_=dtq[:, ts_], func=AF.Exp, scale=-1.0))
                p.op("scalar", lambda h, ts_=ts_: h.activation(
                    out=dtq[:, ts_], in_=dtq[:, ts_], func=AF.Ln, bias=1.0), inc="sc")
                p.wait("vector", "sc")
                p.op("vector", lambda h, ts_=ts_: h.tensor_tensor(
                    out=dtq[:, ts_], in0=dtq[:, ts_], in1=dtu[:, ts_], op=ALU.add),
                    inc="dv")
        p.wait("vector", "dv")
        p.op("vector", lambda h: h.memset(zrt[:], 0.0), inc="dv")
        for ic in range(32):
            sl = ic % 2
            emit_dt(ic)
            p.wait("sync", "pg", guard[("us1pg",)])
            if ("us1", sl) in guard:
                p.wait("sync", "dv", guard[("us1", sl)])
            p.dma("sync", u_sl[sl], u_sp[ic * 128:(ic + 1) * 128, :], f"ld{sl}")
            p.wait("vector", f"ld{sl}")
            p.op("vector", lambda h, sl=sl: h.tensor_tensor(
                out=dtu[:], in0=dtq[:], in1=u_sl[sl], op=ALU.mult))
            p.op("vector", lambda h, sl=sl, ic=ic: h.tensor_scalar(
                out=yvt[:], in0=u_sl[sl], scalar1=Dcol[:, ic:ic + 1], scalar2=None,
                op0=ALU.mult), inc="dv")
            guard[("us1", sl)] = p.semv["dv"]
            p.op("vector", lambda h: h.tensor_reduce(
                out=sdtc[:, 0:1], in_=dtq[:], axis=AX.X, op=ALU.add))
            p.op("vector", lambda h, ic=ic: h.tensor_scalar(
                out=summ[:, ic * 32: ic * 32 + 16], in0=Acol[:, ic * 16:(ic + 1) * 16],
                scalar1=sdtc[:, 0:1], scalar2=None, op0=ALU.mult), inc="dv")
            p.wait("scalar", "dv")
            p.op("scalar", lambda h, ic=ic: h.activation(
                out=summ[:, ic * 32: ic * 32 + 16], in_=summ[:, ic * 32: ic * 32 + 16],
                func=AF.Exp), inc="sc")
            for n in range(16):
                p.wait("scalar", "dv", p.semv["dv"])
                p.op("scalar", lambda h, ic=ic, n=n: h.activation(
                    out=dAt[:], in_=dtq[:], func=AF.Exp,
                    scale=Acol[:, ic * 16 + n: ic * 16 + n + 1]), inc="sc")
                p.wait("vector", "sc")
                p.op("vector", lambda h, n=n: h.tensor_tensor(
                    out=dBxt[:], in0=dtu[:], in1=Bb_all[:, n * 1024:(n + 1) * 1024],
                    op=ALU.mult))
                p.op("vector", lambda h: h.tensor_tensor_scan(
                    out=hsct[:], data0=dAt[:], data1=dBxt[:], initial=0.0,
                    op0=ALU.mult, op1=ALU.add))
                p.op("vector", lambda h, ic=ic, n=n: h.tensor_copy(
                    out=summ[:, ic * 32 + 16 + n: ic * 32 + 17 + n],
                    in_=hsct[:, 1023:1024]))
                p.op("vector", lambda h, n=n: h.tensor_tensor(
                    out=hsct[:], in0=hsct[:], in1=Cb_all[:, n * 1024:(n + 1) * 1024],
                    op=ALU.mult))
                p.op("vector", lambda h: h.tensor_tensor(
                    out=yvt[:], in0=yvt[:], in1=hsct[:], op=ALU.add), inc="dv")
            sl2 = ic % 2
            p.wait("gpsimd", "dv")
            if ic >= 2:
                pass
            p.dma("gpsimd", y1_sp[ic * 128:(ic + 1) * 128, :], yvt[:], f"so{sl2}")
            if ic < 31:
                p.wait("vector", f"so{sl2}")
        p.wait("gpsimd", "dv")
        p.dma("gpsimd", s_cc_in.rearrange("(k p) f -> p k f", p=128),
              summ.rearrange("p (k f) -> p k f", k=32), "st1")
        p.wait("gpsimd", "st1")
        p.op("gpsimd", lambda h: h.collective_compute(
            "AllGather", ALU.bypass, replica_groups=[list(range(8))],
            ins=[s_cc_in[:]], outs=[s_cc_out[:]]), inc="cc")
        p.wait("gpsimd", "cc", p.semv["cc"])
        p.op("vector", lambda h: h.memset(hinb[:], 0.0), inc="dv")
        p.wait("sync", "cc", p.semv["cc"])
        for r in range(8):
            p.wait("sync", "dv", p.semv["dv"])
            p.dma("sync", dAt.rearrange("p (k f) -> p k f", k=32),
                  s_cc_out[r * I4:(r + 1) * I4, :].rearrange("(k p) f -> p k f", p=128),
                  "ldc")
            p.wait("vector", "ldc")
            Pv = dAt.rearrange("p (k f) -> p k f", k=32)[:, :, 0:16]
            Hv = dAt.rearrange("p (k f) -> p k f", k=32)[:, :, 16:32]
            p.op("vector", lambda h, Pv=Pv: h.tensor_tensor(
                out=dBxt[:, 0:512], in0=hinb[:].rearrange("p (k f) -> p k f", k=32)
                if False else hinb[:], in1=Pv, op=ALU.mult))
            p.op("vector", lambda h, Hv=Hv: h.tensor_tensor(
                out=dBxt[:, 0:512], in0=dBxt[:, 0:512], in1=Hv, op=ALU.add))
            p.op("vector", lambda h: h.tensor_tensor(
                out=dBxt[:, 0:512], in0=dBxt[:, 0:512], in1=hinb[:], op=ALU.subtract))
            p.op("vector", lambda h, r=r: h.tensor_scalar(
                out=dBxt[:, 0:512], in0=dBxt[:, 0:512], scalar1=rmaskc[:, r:r + 1],
                scalar2=None, op0=ALU.mult))
            p.op("vector", lambda h: h.tensor_tensor(
                out=hinb[:], in0=hinb[:], in1=dBxt[:, 0:512], op=ALU.add), inc="dv")
        p.wait("sync", "so0")
        p.wait("sync", "so1")
        guard[("ygdone",)] = p.semv["dv"]
        p.wait("vector", "dv")
        p.op("vector", lambda h: h.memset(zrt[:], 0.0), inc="dv")
        for ic in range(32):
            sl = ic % 2
            emit_dt(ic)
            if ("ygdone",) in guard:
                p.wait("sync", "dv", guard[("ygdone",)])
            if ("sg2", sl) in guard:
                p.wait("sync", "dv", guard[("sg2", sl)])
            p.dma("sync", yvt[:], y1_sp[ic * 128:(ic + 1) * 128, :], f"ld{sl}")
            p.dma("sync", u_sl[sl], sg_sp[ic * 128:(ic + 1) * 128, :], f"ld{sl}")
            p.wait("vector", f"ld{sl}")
            for n in range(16):
                p.wait("scalar", "dv", p.semv["dv"])
                p.op("scalar", lambda h, ic=ic, n=n: h.activation(
                    out=dAt[:], in_=dtq[:], func=AF.Exp,
                    scale=Acol[:, ic * 16 + n: ic * 16 + n + 1]), inc="sc")
                p.wait("vector", "sc")
                p.op("vector", lambda h, ic=ic, n=n: h.tensor_tensor_scan(
                    out=hsct[:], data0=dAt[:], data1=zrt[:],
                    initial=hinb[:, ic * 16 + n: ic * 16 + n + 1],
                    op0=ALU.mult, op1=ALU.add))
                p.op("vector", lambda h, n=n: h.tensor_tensor(
                    out=hsct[:], in0=hsct[:], in1=Cb_all[:, n * 1024:(n + 1) * 1024],
                    op=ALU.mult))
                p.op("vector", lambda h: h.tensor_tensor(
                    out=yvt[:], in0=yvt[:], in1=hsct[:], op=ALU.add), inc="dv")
            p.wait("vector", f"so{sl}")
            p.op("vector", lambda h, sl=sl: h.tensor_tensor(
                out=obf[sl], in0=yvt[:], in1=u_sl[sl], op=ALU.mult), inc="dv")
            guard[("ygdone",)] = p.semv["dv"]
            guard[("sg2", sl)] = p.semv["dv"]
            p.wait("gpsimd", "dv")
            p.dma("gpsimd", yg_sp[ic * 128:(ic + 1) * 128, :], obf[sl], f"so{sl}")
        if dbg == "mamba":
            p.wait("gpsimd", "so0")
            p.wait("gpsimd", "so1")
            p.wait("gpsimd", "yo0")
            p.wait("gpsimd", "yo1")
            p.dma("gpsimd", dbg_u, u_sp, "st0")
            p.dma("gpsimd", dbg_sg, sg_sp, "st0")
            p.dma("gpsimd", dbg_yg, yg_sp, "st0")
            p.wait("gpsimd", "dv")
            p.op("gpsimd", lambda h: h.tensor_copy(out=A[0:64, 45312:46336],
                                                   in_=bcrow_b[:, :]), inc="cv")
            p.op("gpsimd", lambda h: h.tensor_copy(out=A[:, 46336:47360],
                                                   in_=tsb[:]), inc="cv")
            p.wait("gpsimd", "cv")
            p.dma("gpsimd", dbg_xp, xpb, "st0")
            p.dma("gpsimd", dbg_bc, A[0:64, 45312:46336], "st0")
            p.dma("gpsimd", dbg_ts, A[:, 46336:47360], "st0")
            p.wait("gpsimd", "st0")
        p.wait("sync", "so0")
        p.wait("sync", "so1")
        for oc in range(16):
            osl = oc % 2
            bks = (2 + 2 * osl, 3 + 2 * osl)
            for half in range(2):
                hsl = half
                if ("ow", osl, hsl) in guard:
                    p.wait("sync", "cv", guard[("ow", osl, hsl)])
                p.dma("sync", wf_sl[hsl].rearrange("p (k i) -> p k i", k=16),
                      wopT[half * 2048:(half + 1) * 2048, oc * 128:(oc + 1) * 128]
                      .rearrange("(k p) i -> p k i", p=128), f"wd{hsl}")
                p.wait("gpsimd", f"wd{hsl}")
                if ("owb", osl, hsl) in guard:
                    p.wait("gpsimd", "pg", guard[("owb", osl, hsl)])
                p.op("gpsimd", lambda h, hsl=hsl: h.tensor_copy(
                    out=wslab_b[hsl], in_=wf_sl[hsl]), inc="cv")
                guard[("ow", osl, hsl)] = p.semv["cv"]
                p.wait("tensor", "cv")
                if half == 0:
                    bankw("tensor", *bks)
                for k in range(16):
                    kk = half * 16 + k
                    sl = kk % 2
                    if ("ygs", sl) in guard:
                        p.wait("sync", "pg", guard[("ygs", sl)])
                    p.dma("sync", u_sl[sl], yg_sp[kk * 128:(kk + 1) * 128, :],
                          f"ld{sl}")
                    p.wait("tensor", f"ld{sl}")
                    for th in range(2):
                        p.op("tensor", lambda h, k=k, kk=kk, sl=sl, bk=bks[th],
                             th=th, hsl=hsl: h.matmul(
                                 P[bk][:, 0:512], wslab_b[hsl][:, k * 128:(k + 1) * 128],
                                 u_sl[sl][:, th * 512:(th + 1) * 512],
                                 start=(kk == 0), stop=(kk == 31)),
                             inc="pg" if th == 1 else None)
                    guard[("ygs", sl)] = p.semv["pg"]
                guard[("owb", osl, hsl)] = p.semv["pg"]
            chains_done = p.semv["pg"]
            p.wait("vector", "pg", chains_done)
            for th in range(2):
                residual_from_psum(bks[th], oc, th * 512, imm=1.0)

    if "g2" in phases:
        ghost_sublayer()

    l1st = {}

    def l1ffn_sublayer():
        dma_barrier()
        if not l1st:
            l1st["gcol"] = p.sbuf("gcol", [128, 64], F32)
            l1st["ucol"] = p.sbuf("ucol", [128, 64], F32)
            l1st["dcolh"] = p.sbuf("dcolh", [128, 16], F32)
            p.dma("sync", l1st["gcol"][:], gsc.rearrange("(k p) f -> p (k f)", p=128),
                  "ldc", slow=True)
            p.dma("sync", l1st["ucol"][:], usc.rearrange("(k p) f -> p (k f)", p=128),
                  "ldc", slow=True)
            p.dma("sync", l1st["dcolh"][:], dsc.rearrange("(k p) f -> p (k f)", p=128),
                  "ldc", slow=True)
            p.wait("vector", "ldc")
            p.op("vector", lambda h: h.tensor_scalar(
                out=l1st["dcolh"][:], in0=l1st["dcolh"][:], scalar1=0.5, scalar2=None,
                op0=ALU.mult), inc="dv")
        gcol, ucol, dcolh = l1st["gcol"], l1st["ucol"], l1st["dcolh"]
        g8 = idx_u8[:, 0:2048]
        u8_ = idx_u8[:, 2048:4096]
        d8 = [idx_u8[:, 4096 + k * 512: 4096 + (k + 1) * 512] for k in range(2)]
        wgb = A[:, 32768:34816]
        wub = A[:, 34816:36864]
        wdb = [A[:, 40960 + k * 512: 40960 + (k + 1) * 512] for k in range(2)]
        xsb = A[0:1, 38912:39936]
        hsb = A[0:1, 39936:40960]
        ln_into_xn()
        p.wait("sync", "dv")
        for th in range(2):
            t0 = th * 512
            bankw("tensor", 0, 1)
            p.wait("tensor", "dv")
            for c in range(NCH):
                p.op("tensor", lambda h, c=c, t0=t0, st=(c == 0),
                     sp=(c == NCH - 1): h.matmul(
                         pa[0:1, 0:512], s.ones_l[:, 0:1],
                         s.xn[:, c * TT + t0: c * TT + t0 + 512], start=st, stop=sp),
                     inc="pe" if c == NCH - 1 else None)
            p.wait("vector", "pe")
            p.op("vector", lambda h, th=th: h.tensor_copy(out=xsb[0:1, 0:512],
                 in_=pa[0:1, 0:512]), inc="dv")
            guard[("evacdv", 0)] = p.semv["dv"]
            for ic in range(64):
                sl = ic % 2
                if ("l1w",) in guard:
                    p.wait("sync", "cv", guard[("l1w",)])
                p.dma("sync", g8.rearrange("p (k i) -> p k i", k=16),
                      gwT[:, ic * 128:(ic + 1) * 128].rearrange("(k p) i -> p k i",
                                                                p=128), "wd0")
                p.dma("sync", u8_.rearrange("p (k i) -> p k i", k=16),
                      uwT[:, ic * 128:(ic + 1) * 128].rearrange("(k p) i -> p k i",
                                                                p=128), "wd0")
                p.wait("gpsimd", "wd0")
                if ("l1wb",) in guard:
                    p.wait("gpsimd", "pg", guard[("l1wb",)])
                p.op("gpsimd", lambda h: h.tensor_copy(out=wgb, in_=g8))
                p.op("gpsimd", lambda h: h.tensor_copy(out=wub, in_=u8_), inc="cv")
                guard[("l1w",)] = p.semv["cv"]
                bg, bu = 2 + sl, 4 + sl
                p.wait("tensor", "cv")
                p.wait("tensor", "dv")
                bankw("tensor", bg, bu)
                for (bk, wb) in ((bg, wgb), (bu, wub)):
                    for k in range(NCH):
                        p.op("tensor", lambda h, k=k, bk=bk, wb=wb, t0=t0,
                             st=(k == 0): h.matmul(
                                 P[bk][:, 0:512], wb[:, k * 128:(k + 1) * 128],
                                 s.xn[:, k * TT + t0: k * TT + t0 + 512],
                                 start=st, stop=False))
                    p.op("tensor", lambda h, bk=bk, t0=t0: h.matmul(
                        P[bk][:, 0:512], m128[0:1, :], xsb[0:1, 0:512],
                        start=False, stop=True), inc="pg")
                guard[("l1wb",)] = p.semv["pg"]
                p.wait("scalar", "pg")
                if "gsil" in guard:
                    p.wait("scalar", "ev", guard["gsil"])
                p.op("scalar", lambda h, bg=bg, ic=ic: h.activation(
                    out=gsil[:, 0:512], in_=P[bg][:, 0:512], func=AF.Silu,
                    scale=gcol[:, ic:ic + 1]), inc="sc")
                p.wait("vector", "sc")
                p.wait("vector", f"yo{sl}")
                if ("hsum", sl) in guard:
                    p.wait("vector", "pg", guard[("hsum", sl)])
                p.op("vector", lambda h, bu=bu, ic=ic, sl=sl: h.scalar_tensor_tensor(
                    out=obf[sl][:, 0:512], in0=P[bu][:, 0:512],
                    scalar=ucol[:, ic:ic + 1], in1=gsil[:, 0:512],
                    op0=ALU.mult, op1=ALU.mult), inc="ev")
                guard[("evac", bg)] = p.semv["ev"]
                guard[("evac", bu)] = p.semv["ev"]
                guard["gsil"] = p.semv["ev"]
                p.wait("tensor", "ev", p.semv["ev"])
                p.op("tensor", lambda h, ic=ic, st=(ic == 0), sp=(ic == 63): h.matmul(
                    pb[0:1, 0:512], s.ones_l[:, 0:1], obf[ic % 2][:, 0:512],
                    start=st, stop=sp), inc="pg")
                guard[("hsum", sl)] = p.semv["pg"]
                p.wait("gpsimd", "ev", p.semv["ev"])
                p.dma("gpsimd", h_sp[ic * 128:(ic + 1) * 128, t0:t0 + 512],
                      obf[sl][:, 0:512], f"yo{sl}")
            p.wait("vector", "pg")
            p.op("vector", lambda h: h.tensor_copy(out=hsb[0:1, 0:512],
                 in_=pb[0:1, 0:512]), inc="dv")
            guard[("evacdv", 1)] = p.semv["dv"]
            if dbg == "l1" and th == 1:
                p.wait("gpsimd", "yo0")
                p.wait("gpsimd", "yo1")
                p.dma("gpsimd", dbg_h1, h_sp, "st0")
                p.wait("gpsimd", "dv")
                p.dma("gpsimd", dbg_xs[0:1, 0:512], hsb[0:1, 0:512], "st0")
                p.wait("gpsimd", "st0")
            p.wait("sync", "yo0")
            p.wait("sync", "yo1")
            banks = [6, 0, 1, 2]
            for og in range(4):
                bankw("tensor", *banks)
                p.wait("tensor", "dv", p.semv["dv"])
                for ic in range(64):
                    sl = ic % 2
                    if ("l1d", sl) in guard:
                        p.wait("sync", "cv", guard[("l1d", sl)])
                    p.wait("sync", f"wde{sl}")
                    p.dma("sync", d8[sl],
                          dwT[ic * 128:(ic + 1) * 128, og * 512:(og + 1) * 512],
                          f"wde{sl}")
                    if ("l1h", sl) in guard:
                        p.wait("sync", "pg", guard[("l1h", sl)])
                    p.dma("sync", u_sl[sl][:, 0:512],
                          h_sp[ic * 128:(ic + 1) * 128, t0:t0 + 512], f"ld{sl}")
                    p.wait("gpsimd", f"wde{sl}")
                    if ("l1db", sl) in guard:
                        p.wait("gpsimd", "pg", guard[("l1db", sl)])
                    p.op("gpsimd", lambda h, sl=sl: h.tensor_copy(out=wdb[sl],
                                                                  in_=d8[sl]),
                         inc="cv")
                    guard[("l1d", sl)] = p.semv["cv"]
                    p.wait("tensor", "cv")
                    p.wait("tensor", f"ld{sl}")
                    for j in range(4):
                        p.op("tensor", lambda h, b=banks[j], j=j, sl=sl, ic=ic:
                             h.matmul(P[b][:, 0:512],
                                      wdb[sl][:, j * 128:(j + 1) * 128],
                                      u_sl[sl][:, 0:512],
                                      start=(ic == 0), stop=False),
                             inc="pg" if j == 3 else None)
                    guard[("l1db", sl)] = p.semv["pg"]
                    guard[("l1h", sl)] = p.semv["pg"]
                for j in range(4):
                    p.op("tensor", lambda h, b=banks[j]: h.matmul(
                        P[b][:, 0:512], m128[0:1, :], hsb[0:1, 0:512],
                        start=False, stop=True), inc="pg" if j == 3 else None)
                chains_done = p.semv["pg"]
                p.wait("vector", "pg", chains_done)
                for j in range(4):
                    oc = og * 4 + j
                    residual_from_psum(banks[j], oc, t0, scale_col=dcolh[:, oc:oc + 1])

    if "f1" in phases:
        l1ffn_sublayer()

    if "attn" in phases:
        dma_barrier()
        nwcol = p.sbuf("nwcol", [128, 32], F32)
        hflagc = p.sbuf("hflagc", [128, 1], F32)
        identt = p.sbuf("identt", [128, 128], BF16)
        permb = p.sbuf("permb", [128, 128], BF16)
        scol = p.sbuf("scol", [128, 1], F32)
        nbs = p.sbuf("nbs", [128, 32], mybir.dt.uint32)

        att_o = A[:, 0:16384]
        maskt = A[:, 16384:25600]
        kbuf = A[:, 25600:27648]
        vbuf = A[:, 27648:29696]
        VTt = A[:, 29696:31744]
        qh_t = A[:, 31744:32768]
        qnb = A[:, 32768:33792]
        Pb = A[:, 33792:35072]
        PT = A[:, 35072:36224]
        wob = [A[:, 36864 + k * 2048: 36864 + (k + 1) * 2048] for k in range(2)]
        cosA = Fa[:, 6144:7168]
        sinA = Fa[:, 7168:8192]
        t1A = Fa[:, 0:1024]
        t2A = Fa[:, 2048:3072]
        qnf = Fa[:, 3072:4096]
        qswf = Fa[:, 4096:5120]
        SA = Fa[:, 0:1280]
        mxA = Fa[:, 1280:1288]
        wof = [Fa[:, 0 + k * 2048: (k + 1) * 2048] for k in range(2)]
        p.dma("sync", nwcol[:, 0:16], qnw.rearrange("(k p) f -> p (k f)", p=128),
              "ldc", slow=True)
        p.dma("sync", nwcol[:, 16:32], knw.rearrange("(k p) f -> p (k f)", p=128),
              "ldc", slow=True)
        p.dma("sync", hflagc[:], hflag, "ldc")
        p.dma("sync", identt[:], identb_i, "ldc")
        p.dma("sync", permb[:], permT, "ldc")
        p.dma("sync", nbs[:], nbro, "ldc")
        p.op("vector", lambda h: h.memset(scol[:], 2 ** -3.5), inc="dv")
        ln_into_xn()
        p.wait("sync", "dv")
        p.dma("sync", maskt.rearrange("p (q r) -> p q r", q=8),
              maskb.rearrange("p (q r) -> p q r", q=8), "ldc")
        for oc in range(48):
            sl = oc % 2
            if ("aw", sl) in guard:
                p.wait("sync", "cv", guard[("aw", sl)])
            p.dma("sync", wof[sl].rearrange("p (k i) -> p k i", k=16),
                  wqkvT[:, oc * 128:(oc + 1) * 128].rearrange("(k p) i -> p k i",
                                                             p=128), f"wd{sl}")
            p.wait("gpsimd", f"wd{sl}")
            if ("awb", sl) in guard:
                p.wait("gpsimd", "pg", guard[("awb", sl)])
            p.op("gpsimd", lambda h, sl=sl: h.tensor_copy(out=wob[sl], in_=wof[sl]),
                 inc="cv")
            guard[("aw", sl)] = p.semv["cv"]
            p.wait("tensor", "cv")
            p.wait("tensor", "dv")
            for th in range(2):
                bk = 2 + 2 * sl + th
                bankw("tensor", bk)
                for k in range(NCH):
                    p.op("tensor", lambda h, k=k, sl=sl, bk=bk, th=th,
                         st=(k == 0), sp=(k == NCH - 1): h.matmul(
                             P[bk][:, 0:512], wob[sl][:, k * 128:(k + 1) * 128],
                             s.xn[:, k * TT + th * 512: k * TT + th * 512 + 512],
                             start=st, stop=sp), inc="pg" if k == NCH - 1 else None)
            guard[("awb", sl)] = p.semv["pg"]
            p.wait("vector", "pg")
            p.wait("vector", f"yo{sl}")
            for th in range(2):
                bk = 2 + 2 * sl + th
                p.op("vector", lambda h, bk=bk, th=th, sl=sl: h.tensor_copy(
                    out=obf[sl][:, th * 512:(th + 1) * 512], in_=P[bk][:, 0:512]),
                    inc="ev")
                guard[("evac", bk)] = p.semv["ev"]
            p.wait("gpsimd", "ev")
            if oc < 16:
                p.dma("gpsimd", qraw_sp[oc * 128:(oc + 1) * 128, :], obf[sl], f"yo{sl}")
            elif oc < 32:
                p.dma("gpsimd", kraw_sp[(oc - 16) * 128:(oc - 15) * 128, :], obf[sl],
                      f"yo{sl}")
            else:
                p.dma("gpsimd", kv_cc_in[D + (oc - 32) * 128: D + (oc - 31) * 128, :],
                      obf[sl], f"yo{sl}")
        if dbg == "xr":
            p.wait("gpsimd", "ev")
            p.wait("gpsimd", "dv")
            p.dma("gpsimd", dbg_x1, XR[:], "st1")
        p.wait("sync", "yo0")
        p.wait("sync", "yo1")
        for which, raw_sp, wofs in (("q", qraw_sp, 0), ("k", kraw_sp, 16)):
            p.wait("sync", "dv")
            for th in range(2):
                bankw("tensor", 0)
                p.wait("tensor", "dv")
                for c in range(NCH):
                    sl = c % 2
                    if ("aq", sl) in guard:
                        p.wait("sync", "dv", guard[("aq", sl)])
                    p.dma("sync", u_sl[sl][:, 0:512],
                          raw_sp[c * 128:(c + 1) * 128, th * 512:(th + 1) * 512],
                          f"ld{sl}")
                    p.wait("vector", f"ld{sl}")
                    if ("aq2", sl) in guard:
                        p.wait("vector", "pe", guard[("aq2", sl)])
                    p.op("vector", lambda h, sl=sl: h.tensor_tensor(
                        out=qnb[:, sl * 512:(sl + 1) * 512], in0=u_sl[sl][:, 0:512],
                        in1=u_sl[sl][:, 0:512], op=ALU.mult), inc="dv")
                    guard[("aq", sl)] = p.semv["dv"]
                    p.wait("tensor", "dv")
                    p.op("tensor", lambda h, sl=sl, st=(c == 0),
                         sp=(c == NCH - 1): h.matmul(
                             pa[0:1, 0:512], s.ones_l[:, 0:1],
                             qnb[:, sl * 512:(sl + 1) * 512], start=st, stop=sp),
                         inc="pe")
                    guard[("aq2", sl)] = p.semv["pe"]
                p.wait("vector", "pe")
                p.op("vector", lambda h, th=th: h.tensor_scalar(
                    out=s.r1[0:1, th * 512:(th + 1) * 512], in0=pa[0:1, 0:512],
                    scalar1=1.0 / D, scalar2=None, op0=ALU.mult), inc="dv")
                guard[("evacdv", 0)] = p.semv["dv"]
            p.wait("scalar", "dv")
            p.op("scalar", lambda h: h.activation(out=s.r2[0:1, 0:TT],
                                                  in_=s.r1[0:1, 0:TT], func=AF.Ln,
                                                  bias=s.eps6[0:1, 0:1]))
            p.op("scalar", lambda h: h.activation(out=s.r3[0:1, 0:TT],
                                                  in_=s.r2[0:1, 0:TT], func=AF.Exp,
                                                  scale=-0.5), inc="sc")
            for th in range(2):
                p.wait("tensor", "sc")
                p.wait("tensor", "dv")
                bankw("tensor", 1)
                p.op("tensor", lambda h, th=th: h.matmul(
                    pb[:, 0:512], s.ones_c[0:1, :], s.r3[0:1, th * 512:(th + 1) * 512],
                    start=True, stop=True), inc="pe")
                p.wait("vector", "pe")
                p.op("vector", lambda h, th=th: h.tensor_copy(
                    out=s.rb[:, th * 512:(th + 1) * 512], in_=pb[:, 0:512]), inc="dv")
                guard[("evacdv", 1)] = p.semv["dv"]
            if which == "q":
                p.dma("sync", cosA[:], cosb, "ldc")
                p.dma("sync", sinA[:], sinb, "ldc")
            p.wait("sync", "dv")
            for c in range(NCH):
                sl = c % 2
                if ("ar", sl) in guard:
                    p.wait("sync", "dv", guard[("ar", sl)])
                p.dma("sync", u_sl[sl], raw_sp[c * 128:(c + 1) * 128, :], f"ld{sl}")
                p.wait("vector", f"ld{sl}")
                p.wait("vector", "ldc")
                p.op("vector", lambda h, sl=sl, c=c, wofs=wofs: h.scalar_tensor_tensor(
                    out=qnf[:], in0=u_sl[sl], scalar=nwcol[:, wofs + c: wofs + c + 1],
                    in1=s.rb[:, 0:TT], op0=ALU.mult, op1=ALU.mult))
                guard[("ar", sl)] = p.semv["dv"] + 1
                p.op("vector", lambda h: h.tensor_copy(out=qnb, in_=qnf[:]), inc="dv")
                for th in range(2):
                    p.wait("tensor", "dv")
                    bankw("tensor", 6)
                    p.op("tensor", lambda h, th=th: h.matmul(
                        P[6][:, 0:512], permb[:], qnb[:, th * 512:(th + 1) * 512],
                        start=True, stop=True), inc="pe")
                    p.wait("vector", "pe")
                    p.op("vector", lambda h, th=th: h.tensor_copy(
                        out=qswf[:, th * 512:(th + 1) * 512], in_=P[6][:, 0:512]),
                        inc="ev")
                    guard[("evac", 6)] = p.semv["ev"]
                p.op("vector", lambda h: h.tensor_tensor(
                    out=t1A[:], in0=qnf[:], in1=cosA[:], op=ALU.mult))
                p.op("vector", lambda h: h.tensor_tensor(
                    out=t2A[:], in0=qswf[:], in1=sinA[:], op=ALU.mult))
                p.wait("vector", f"yo{sl}")
                p.op("vector", lambda h, sl=sl: h.tensor_tensor(
                    out=obf[sl], in0=t1A[:], in1=t2A[:], op=ALU.add), inc="dv")
                p.wait("gpsimd", "dv")
                dst = (q_sp if which == "q" else kv_cc_in)
                p.dma("gpsimd", dst[c * 128:(c + 1) * 128, :], obf[sl], f"yo{sl}")
        if dbg == "xr":
            p.wait("gpsimd", "dv")
            p.dma("gpsimd", dbg_x2, XR[:], "st1")
        p.wait("gpsimd", "yo0")
        p.wait("gpsimd", "yo1")
        p.op("gpsimd", lambda h: h.collective_compute(
            "AllGather", ALU.bypass, replica_groups=[list(range(8))],
            ins=[kv_cc_in[:]], outs=[kv_cc_out[:]]), inc="cc")
        p.wait("gpsimd", "cc", p.semv["cc"])

        p.wait("sync", "ldc")
        for hh in range(16):
            if hh > 0:
                p.wait("sync", "pg", guard[("heads",)])
                p.wait("gpsimd", "pg", guard[("heads",)])
            p.dma("sync", qh_t, q_sp[hh * 128:(hh + 1) * 128, :], "ld0")
            p.dma("sync", kbuf[:, 1024:2048], kv_cc_in[hh * 128:(hh + 1) * 128, :],
                  "ld0")
            p.dma("sync", vbuf[:, 1024:2048],
                  kv_cc_in[D + hh * 128: D + (hh + 1) * 128, :], "ld0")
            p.op("gpsimd", lambda h, hh=hh: h.indirect_dma_start(
                out=kbuf[:, 0:1024], out_offset=None, in_=kv_cc_out,
                in_offset=bass.IndirectOffsetOnAxis(ap=nbs[:, hh:hh + 1], axis=0)),
                inc="st0", k=16)
            p.op("gpsimd", lambda h, hh=hh: h.indirect_dma_start(
                out=vbuf[:, 0:1024], out_offset=None, in_=kv_cc_out,
                in_offset=bass.IndirectOffsetOnAxis(ap=nbs[:, 16 + hh:17 + hh],
                                                    axis=0)), inc="st0", k=16)
            p.wait("vector", "st0")
            p.wait("vector", "ld0")
            p.op("vector", lambda h: h.tensor_scalar(
                out=kbuf[:, 0:1024], in0=kbuf[:, 0:1024], scalar1=hflagc[:, 0:1],
                scalar2=None, op0=ALU.mult))
            p.op("vector", lambda h: h.tensor_scalar(
                out=vbuf[:, 0:1024], in0=vbuf[:, 0:1024], scalar1=hflagc[:, 0:1],
                scalar2=None, op0=ALU.mult), inc="dv")
            for j in range(16):
                p.wait("tensor", "dv")
                if ("vt",) in guard:
                    p.wait("tensor", "ev", guard[("vt",)])
                p.op("tensor", lambda h, j=j: h.transpose(
                    pvt[:, 0:128], vbuf[:, j * 128:(j + 1) * 128], identt[:]),
                    inc="pe")
                p.wait("vector", "pe")
                p.op("vector", lambda h, j=j: h.tensor_copy(
                    out=VTt[:, j * 128:(j + 1) * 128], in_=pvt[:, 0:128]), inc="ev")
                guard[("vt",)] = p.semv["ev"]
            for qb in range(8):
                segs = ((0, 512, 2), (512, 512, 3), (1024, 128, 4))
                for (o, n, bk) in segs:
                    bankw("tensor", bk)
                    p.op("tensor", lambda h, o=o, n=n, bk=bk, qb=qb: h.matmul(
                        P[bk][:, 0:n], qh_t[:, qb * 128:(qb + 1) * 128],
                        kbuf[:, qb * 128 + o: qb * 128 + o + n],
                        start=True, stop=True), inc="pg" if bk == 4 else None)
                p.wait("vector", "pg")
                for (o, n, bk) in segs:
                    p.op("vector", lambda h, o=o, n=n, bk=bk, qb=qb:
                         h.scalar_tensor_tensor(
                             out=SA[:, o:o + n], in0=P[bk][:, 0:n],
                             scalar=scol[:, 0:1],
                             in1=maskt[:, qb * 1152 + o: qb * 1152 + o + n],
                             op0=ALU.mult, op1=ALU.add),
                         inc="ev" if bk == 4 else None)
                guard[("evac", 2)] = guard[("evac", 3)] = guard[("evac", 4)] = \
                    p.semv["ev"]
                p.op("vector", lambda h: h.reduce_max(out=mxA[:, 0:1],
                                                      in_=SA[:, 0:1152], axis=AX.X))
                p.op("vector", lambda h: h.tensor_scalar(
                    out=mxA[:, 1:2], in0=mxA[:, 0:1], scalar1=-1.0, scalar2=None,
                    op0=ALU.mult), inc="dv")
                p.wait("scalar", "dv")
                p.op("scalar", lambda h: h.activation(
                    out=Pb[:, 0:1152], in_=SA[:, 0:1152], func=AF.Exp,
                    bias=mxA[:, 1:2]), inc="sc")
                p.wait("vector", "sc")
                p.op("vector", lambda h: h.reduce_sum(out=mxA[:, 2:3],
                                                      in_=Pb[:, 0:1152], axis=AX.X))
                p.op("vector", lambda h: h.reciprocal(out=mxA[:, 3:4],
                                                      in_=mxA[:, 2:3]))
                p.op("vector", lambda h: h.tensor_scalar(
                    out=Pb[:, 0:1152], in0=Pb[:, 0:1152], scalar1=mxA[:, 3:4],
                    scalar2=None, op0=ALU.mult), inc="dv")
                for kc in range(9):
                    p.wait("tensor", "dv")
                    if ("pt",) in guard:
                        p.wait("tensor", "ev", guard[("pt",)])
                    p.op("tensor", lambda h, kc=kc: h.transpose(
                        ptr[:, 0:128], Pb[:, kc * 128:(kc + 1) * 128], identt[:]),
                        inc="pe")
                    p.wait("vector", "pe")
                    p.op("vector", lambda h, kc=kc: h.tensor_copy(
                        out=PT[:, kc * 128:(kc + 1) * 128], in_=ptr[:, 0:128]),
                        inc="ev")
                    guard[("pt",)] = p.semv["ev"]
                bankw("tensor", 5)
                p.wait("tensor", "ev", guard[("pt",)])
                for kc in range(9):
                    p.op("tensor", lambda h, kc=kc, qb=qb: h.matmul(
                        P[5][0:128, 0:128], VTt[:, (qb + kc) * 128:(qb + kc + 1) * 128],
                        PT[:, kc * 128:(kc + 1) * 128],
                        start=(kc == 0), stop=(kc == 8)), inc="pg" if kc == 8 else None)
                p.wait("vector", "pg")
                p.op("vector", lambda h, hh=hh, qb=qb: h.tensor_copy(
                    out=att_o[:, hh * 1024 + qb * 128: hh * 1024 + (qb + 1) * 128],
                    in_=P[5][0:128, 0:128]), inc="ev")
                guard[("evac", 5)] = p.semv["ev"]
            guard[("heads",)] = p.semv["pg"]
        if dbg == "at":
            p.wait("gpsimd", "ev")
            p.dma("gpsimd", dbg_q, q_sp, "st1")
            p.dma("gpsimd", dbg_k, kv_cc_in[0:D, :], "st1")
            p.dma("gpsimd", dbg_v, kv_cc_in[D:2 * D, :], "st1")
            p.dma("gpsimd", dbg_ao, att_o, "st1")
            p.wait("gpsimd", "st1")
        if dbg == "xr":
            p.wait("gpsimd", "ev")
            p.wait("gpsimd", "dv")
            p.dma("gpsimd", dbg_x3, XR[:], "st1")
            p.wait("gpsimd", "st1")
        p.wait("sync", "ev")
        for oc in range(16):
            sl = oc % 2
            if ("wo", sl) in guard:
                p.wait("sync", "cv", guard[("wo", sl)])
            p.dma("sync", wof[sl].rearrange("p (k i) -> p k i", k=16),
                  woT[:, oc * 128:(oc + 1) * 128].rearrange("(k p) i -> p k i", p=128),
                  f"wd{sl}")
            p.wait("gpsimd", f"wd{sl}")
            if ("wob", sl) in guard:
                p.wait("gpsimd", "pg", guard[("wob", sl)])
            p.op("gpsimd", lambda h, sl=sl: h.tensor_copy(out=wob[sl], in_=wof[sl]),
                 inc="cv")
            guard[("wo", sl)] = p.semv["cv"]
            p.wait("tensor", "cv")
            p.wait("tensor", "ev")
            for th in range(2):
                bk = 2 + 2 * sl + th
                bankw("tensor", bk)
                for k in range(NCH):
                    p.op("tensor", lambda h, k=k, sl=sl, bk=bk, th=th,
                         st=(k == 0), sp=(k == NCH - 1): h.matmul(
                             P[bk][:, 0:512], wob[sl][:, k * 128:(k + 1) * 128],
                             att_o[:, k * 1024 + th * 512: k * 1024 + th * 512 + 512],
                             start=st, stop=sp), inc="pg" if k == NCH - 1 else None)
            guard[("wob", sl)] = p.semv["pg"]
            p.wait("vector", "pg")
            for th in range(2):
                bk = 2 + 2 * sl + th
                residual_from_psum(bk, oc, th * 512, imm=1.0)

    if "f2" in phases:
        l1ffn_sublayer()

    def finalize():
        p.wait("gpsimd", "ev")
        p.wait("gpsimd", "dv")
        for c in range(NCH):
            p.dma("gpsimd", yT[c * 128:(c + 1) * 128, :],
                  XR[:, c * TT:(c + 1) * TT], f"st{c % 2}")
        p.wait("gpsimd", "st0")
        p.wait("gpsimd", "st1")

    return p, locals(), finalize



def make_attn_masks8(core):
    m = np.full((8, 128, 1152), -1e30, dtype=np.float32)
    for qb in range(8):
        v = min(core * 8 + qb, 8)
        for ql in range(128):
            lo = ql + 1
            if v < 8:
                lo = max(lo, 1024 - v * 128)
            m[qb, ql, lo:ql + 1025] = 0.0
    return np_bf16(m.transpose(1, 0, 2).reshape(128, 8 * 1152))


def make_rope_tables(t0, T):
    pos = np.arange(t0, t0 + T, dtype=np.float64)
    inv = 1.0 / (10000.0 ** (np.arange(0, 128, 2, dtype=np.float64) / 128.0))
    fr = pos[None, :] * inv[:, None]
    cos = np.cos(fr).astype(np.float32)
    sin = np.sin(fr).astype(np.float32)
    return (np.ascontiguousarray(np.concatenate([cos, cos], axis=0)),
            np.ascontiguousarray(np.concatenate([sin, sin], axis=0)))


def make_perm():
    Pm = np.zeros((128, 128), dtype=np.float32)
    for d in range(64):
        Pm[d, 64 + d] = -1.0
        Pm[64 + d, d] = 1.0
    return np_bf16(Pm.T)


def make_sel16():
    selv = np.zeros((64, 16 * 128), dtype=np.float32)
    for n in range(16):
        selv[n, n * 128:(n + 1) * 128] = 1.0
        selv[32 + n, n * 128:(n + 1) * 128] = 1.0
    return np_bf16(selv)


def stage_mega(core, ins):
    L = ins
    c = core
    xT = np.ascontiguousarray(np.asarray(L["L0_attn_x"])[0].T[:, c * TT:(c + 1) * TT]
                              .astype(np.float32))
    gT = np.asarray(L["L0_ffn_gate_idx"]).T
    uT = np.asarray(L["L0_ffn_up_idx"]).T
    dT = np.asarray(L["L0_ffn_down_idx"]).T
    rm = np.zeros((128, 8), np.float32)
    rm[:, 0:c] = 1.0
    nb = np.zeros((128, 8), np.float32)
    if c > 0:
        nb[:, c - 1] = 1.0
    cosq, sinq = make_rope_tables(c * TT, TT)
    return {
        "xT": xT,
        "gidxT_s": np.ascontiguousarray(gT[c * 256:(c + 1) * 256]).astype(np.uint8),
        "uidxT_s": np.ascontiguousarray(uT[c * 256:(c + 1) * 256]).astype(np.uint8),
        "didxT_s": np.ascontiguousarray(dT[:, c * 256:(c + 1) * 256]).astype(np.uint8),
        "lut": np.ascontiguousarray(np.asarray(L["L0_ffn_lut"]),
                                    dtype=np.float32).reshape(1, 16),
        "wipT": np.ascontiguousarray(np.asarray(L["L0_attn_in_proj_w"]).T
                                     .astype(np.float32)),
        "convw": np.ascontiguousarray(np.asarray(L["L0_attn_conv_w"])[:, 0, :]
                                      .astype(np.float32)),
        "convb": np.ascontiguousarray(np.asarray(L["L0_attn_conv_b"])
                                      .astype(np.float32).reshape(-1, 1)),
        "xpT": np.ascontiguousarray(np.asarray(L["L0_attn_x_proj_w"]).T
                                    .astype(np.float32)),
        "dtpT": np.ascontiguousarray(np.asarray(L["L0_attn_dt_proj_w"]).T
                                     .astype(np.float32)),
        "dtb": np.ascontiguousarray(np.asarray(L["L0_attn_dt_proj_b"])
                                    .astype(np.float32).reshape(-1, 1)),
        "Alog": np.ascontiguousarray(np.asarray(L["L0_attn_A_log"]).astype(np.float32)),
        "Dvec": np.ascontiguousarray(np.asarray(L["L0_attn_D"])
                                     .astype(np.float32).reshape(-1, 1)),
        "wopT": np.ascontiguousarray(np.asarray(L["L0_attn_out_proj_w"]).T
                                     .astype(np.float32)),
        "sel": make_sel16(),
        "rmask": rm,
        "nbrm": nb,
        "nbro": (np.array([[((c + 7) % 8) * 2 * D + hh * 128 + pp
                            for hh in range(16)] +
                           [((c + 7) % 8) * 2 * D + D + hh * 128 + pp
                            for hh in range(16)]
                           for pp in range(128)], dtype=np.uint32)),
        "gwT": np.ascontiguousarray(np.asarray(L["L1_ffn_gate_w"]).T).astype(np.uint8),
        "uwT": np.ascontiguousarray(np.asarray(L["L1_ffn_up_w"]).T).astype(np.uint8),
        "dwT": np.ascontiguousarray(np.asarray(L["L1_ffn_down_w"]).T).astype(np.uint8),
        "gsc": np.ascontiguousarray(np.asarray(L["L1_ffn_gate_scale"]),
                                    dtype=np.float32).reshape(-1, 1),
        "usc": np.ascontiguousarray(np.asarray(L["L1_ffn_up_scale"]),
                                    dtype=np.float32).reshape(-1, 1),
        "dsc": np.ascontiguousarray(np.asarray(L["L1_ffn_down_scale"]),
                                    dtype=np.float32).reshape(-1, 1),
        "wqkvT": np.ascontiguousarray(np.asarray(L["L1_attn_w_qkv"]).T
                                      .astype(np.float32)),
        "qnw": np.ascontiguousarray(np.asarray(L["L1_attn_q_norm_w"])
                                    .astype(np.float32).reshape(-1, 1)),
        "knw": np.ascontiguousarray(np.asarray(L["L1_attn_k_norm_w"])
                                    .astype(np.float32).reshape(-1, 1)),
        "woT": np.ascontiguousarray(np.asarray(L["L1_attn_w_o"]).T.astype(np.float32)),
        "cosb": cosq,
        "sinb": sinq,
        "permT": make_perm(),
        "maskb": make_attn_masks8(c),
        "identb_i": np_bf16(np.eye(128)),
        "hflag": np.full((128, 1), 1.0 if c > 0 else 0.0, np.float32),
    }



from concourse.bass_utils import run_bass_kernel_spmd

_MEGA = {}


def kernel(**inputs):
    if "nc" not in _MEGA:
        p, env, fin = build_mega()
        fin()
        _MEGA["nc"] = p.build()
    nc = _MEGA["nc"]
    L = {k: np.asarray(v) for k, v in inputs.items()}
    in_maps = [stage_mega(c, L) for c in range(8)]
    rs = run_bass_kernel_spmd(nc, in_maps, core_ids=list(range(8))).results
    yT = np.concatenate([r["yT"] for r in rs], axis=1)
    return np.ascontiguousarray(yT.T).reshape(1, 8192, 2048).astype(np.float32)


# version 06 (final)
# speedup vs baseline: 1.0012x; 1.0012x over previous
import numpy as np
import ml_dtypes
from contextlib import ExitStack
import concourse.bass as bass
import concourse.mybir as mybir

F32 = mybir.dt.float32
BF16 = mybir.dt.bfloat16
U8 = mybir.dt.uint8
AF = mybir.ActivationFunctionType
ALU = mybir.AluOpType
AX = mybir.AxisListType

ENGS = ("sync", "scalar", "vector", "tensor", "gpsimd")


class Prog:

    def __init__(self, name="prog"):
        self.nc = bass.Bass("TRN2", target_bir_lowering=False)
        self.st = ExitStack()
        self.ops = {e: [] for e in ENGS}
        self.semv = {}
        self.sems = {}
        self.name = name

    def sem(self, name):
        self.sems[name] = self.st.enter_context(self.nc.semaphore(name))
        self.semv[name] = 0
        return name

    def sbuf(self, name, shape, dt):
        return self.st.enter_context(self.nc.sbuf_tensor(name, shape, dt))

    def psum(self, name, shape, dt=F32):
        return self.st.enter_context(self.nc.psum_tensor(name, shape, dt))

    def wait(self, eng, sem, val=None):
        v = self.semv[sem] if val is None else val
        if v > 0:
            self.ops[eng].append(("w", sem, v))

    def op(self, eng, fn, inc=None, k=1):
        if inc is not None:
            self.semv[inc] += k
        self.ops[eng].append(("o", fn, inc, k,
                              self.semv[inc] if inc is not None else 0))

    def dma(self, eng, out, in_, sem, slow=False):
        if self.semv[sem] > 0:
            self.ops[eng].append(("w", sem, self.semv[sem]))
        self.semv[sem] += 16
        o, i = out, in_
        self.ops[eng].append(("o", lambda e: e.dma_start(
            out=o, in_=i, allow_slow_non_contiguous=slow), sem, 16, None))

    def build(self):
        with self.nc.Block() as block:
            for eng in ENGS:
                if not self.ops[eng]:
                    continue
                items = self.ops[eng]

                auto_self = eng in ("vector", "scalar", "gpsimd")
                own = None
                own_count = [0]
                if auto_self:
                    own = self.st.enter_context(self.nc.semaphore(f"own_{eng}"))

                def body(h, items=items, auto_self=auto_self, own=own,
                         own_count=own_count):
                    hwm = {}
                    pending = None
                    for it in items:
                        if it[0] == "w":
                            _, s, v = it
                            if hwm.get(s, -1) < v:
                                h.wait_ge(self.sems[s], v)
                                hwm[s] = v
                        else:
                            _, fn, inc, k, postv = it
                            if auto_self and pending is not None:
                                ps, pv = pending
                                key = ps if isinstance(ps, str) else "__own__"
                                if hwm.get(key, -1) < pv:
                                    hh = self.sems[ps] if isinstance(ps, str) else ps
                                    h.wait_ge(hh, pv)
                                    hwm[key] = pv
                                pending = None
                            instr = fn(h)
                            if inc is not None:
                                instr.then_inc(self.sems[inc], k)
                                if auto_self:
                                    pending = (inc, postv) if postv is not None else None
                            elif auto_self:
                                own_count[0] += 1
                                instr.then_inc(own, 1)
                                pending = (own, own_count[0])

                getattr(block, eng)(body)
        self.st.close()
        return self.nc


def np_bf16(x):
    return np.ascontiguousarray(x).astype(ml_dtypes.bfloat16)


class NS:
    pass



def emit_ln(p, s, pa, pb, T, nch, eps=1e-5, rms=False):
    D = nch * 128
    nth = T // 512
    for c in range(nch):
        sl = slice(c * T, (c + 1) * T)
        p.op("vector", lambda h, sl=sl: h.tensor_tensor(
            out=s.xn[:, sl], in0=s.xb[:, sl], in1=s.xb[:, sl], op=ALU.mult),
            inc="dv" if c == nch - 1 else None)
    for th in range(nth):
        p.wait("tensor", "dv")
        for c in range(nch):
            k0 = c * T + th * 512
            if not rms:
                p.op("tensor", lambda h, k0=k0, st=(c == 0), sp=(c == nch - 1): h.matmul(
                    pa[0:1, :], s.ones_l[:, 0:1], s.xb[:, k0:k0 + 512], start=st, stop=sp))
            p.op("tensor", lambda h, k0=k0, st=(c == 0), sp=(c == nch - 1): h.matmul(
                pb[0:1, :], s.ones_l[:, 0:1], s.xn[:, k0:k0 + 512], start=st, stop=sp),
                inc="pe" if c == nch - 1 else None)
        tsl = slice(th * 512, th * 512 + 512)
        p.wait("vector", "pe")
        if not rms:
            p.op("vector", lambda h, tsl=tsl: h.tensor_scalar(
                out=s.r0[0:1, tsl], in0=pa[0:1, :], scalar1=1.0 / D, scalar2=None,
                op0=ALU.mult))
        p.op("vector", lambda h, tsl=tsl: h.tensor_scalar(
            out=s.r1[0:1, tsl], in0=pb[0:1, :], scalar1=1.0 / D, scalar2=None,
            op0=ALU.mult), inc="dv")
    if not rms:
        p.op("vector", lambda h: h.tensor_tensor(
            out=s.r2[0:1, 0:T], in0=s.r0[0:1, 0:T], in1=s.r0[0:1, 0:T], op=ALU.mult))
        p.op("vector", lambda h: h.tensor_tensor(
            out=s.r1[0:1, 0:T], in0=s.r1[0:1, 0:T], in1=s.r2[0:1, 0:T],
            op=ALU.subtract), inc="dv")
    p.wait("scalar", "dv")
    p.op("scalar", lambda h: h.activation(out=s.r2[0:1, 0:T], in_=s.r1[0:1, 0:T],
                                          func=AF.Ln, bias=s.epsc[0:1, 0:1]))
    p.op("scalar", lambda h: h.activation(out=s.r3[0:1, 0:T], in_=s.r2[0:1, 0:T],
                                          func=AF.Exp, scale=-0.5), inc="sc")
    for th in range(nth):
        p.wait("tensor", "sc")
        p.wait("tensor", "dv")
        sl = slice(th * 512, th * 512 + 512)
        if not rms:
            p.op("tensor", lambda h, sl=sl: h.matmul(
                pa[:, 0:512], s.ones_c[0:1, :], s.r0[0:1, sl], start=True, stop=True))
        p.op("tensor", lambda h, sl=sl: h.matmul(
            pb[:, 0:512], s.ones_c[0:1, :], s.r3[0:1, sl], start=True, stop=True),
            inc="pe")
        p.wait("vector", "pe")
        if not rms:
            p.op("vector", lambda h, sl=sl: h.tensor_copy(out=s.mb[:, sl], in_=pa[:, 0:512]))
        p.op("vector", lambda h, sl=sl: h.tensor_copy(out=s.rb[:, sl], in_=pb[:, 0:512]),
             inc="dv")
    for c in range(nch):
        sl = slice(c * T, (c + 1) * T)
        if rms:
            p.op("vector", lambda h, sl=sl: h.tensor_tensor(
                out=s.xn[:, sl], in0=s.xb[:, sl], in1=s.rb[:, 0:T], op=ALU.mult),
                inc="dv" if c == nch - 1 else None)
        else:
            p.op("vector", lambda h, sl=sl: h.tensor_tensor(
                out=s.xn[:, sl], in0=s.xb[:, sl], in1=s.mb[:, 0:T], op=ALU.subtract))
            p.op("vector", lambda h, sl=sl: h.tensor_tensor(
                out=s.xn[:, sl], in0=s.xn[:, sl], in1=s.rb[:, 0:T], op=ALU.mult),
                inc="dv" if c == nch - 1 else None)


def emit_load_x_as_bf16(p, s, xT_ap, T, nch):
    conv_dv = {}
    for c in range(nch):
        slot = c % 2
        if c >= 2:
            p.wait("sync", "dv", conv_dv[c - 2])
        p.dma("sync", s.xtmp[:, slot * T:(slot + 1) * T],
              xT_ap[c * 128:(c + 1) * 128, :], f"ld{slot}")
        p.wait("vector", f"ld{slot}")
        p.op("vector", lambda h, c=c, slot=slot: h.tensor_copy(
            out=s.xb[:, c * T:(c + 1) * T], in_=s.xtmp[:, slot * T:(slot + 1) * T]),
            inc="dv")
        conv_dv[c] = p.semv["dv"]



def build_ghost(TT=1024, debug=False):
    D, I5, NCH, IC = 2048, 5504, 16, 43
    p = Prog("ghost")
    nc = p.nc

    xT = nc.dram_tensor("xT", [D, TT], F32, kind="ExternalInput").ap()
    gidxT = nc.dram_tensor("gidxT", [D, I5], U8, kind="ExternalInput").ap()
    uidxT = nc.dram_tensor("uidxT", [D, I5], U8, kind="ExternalInput").ap()
    didxT = nc.dram_tensor("didxT", [I5, D], U8, kind="ExternalInput").ap()
    lut = nc.dram_tensor("lut", [1, 16], F32, kind="ExternalInput").ap()
    yT = nc.dram_tensor("yT", [D, TT], F32, kind="ExternalOutput").ap()

    if debug:
        dbg_xn = nc.dram_tensor("dbg_xn", [128, 16 * TT], BF16, kind="ExternalOutput").ap()
        dbg_wg = nc.dram_tensor("dbg_wg", [256, I5], BF16, kind="ExternalOutput").ap()
        dbg_wd = nc.dram_tensor("dbg_wd", [256, D], BF16, kind="ExternalOutput").ap()
        dbg_h = nc.dram_tensor("dbg_h", [128, 43 * 512], BF16, kind="ExternalOutput").ap()
    WgS = nc.dram_tensor("WgS", [D, I5], BF16).ap()
    WuS = nc.dram_tensor("WuS", [D, I5], BF16).ap()
    WdS = nc.dram_tensor("WdS", [I5, D], BF16).ap()

    for sn in ("ld0", "ld1", "ldc", "wd0", "wd1", "wdc0", "wdc1", "wde0", "wde1",
               "cv", "dq", "so0", "so1", "pg", "ev", "dv", "pe", "sc", "yo0", "yo1"):
        p.sem(sn)

    s = NS()
    AB = 70 * 1024
    s.A = p.sbuf("A", [128, AB], BF16)
    s.xb = s.A[:, 0:NCH * TT]
    s.xn = s.A[:, NCH * TT:2 * NCH * TT]
    o_deq = 2 * NCH * TT
    idxb = [s.A[:, o_deq + k * I5: o_deq + (k + 1) * I5] for k in range(2)]
    accb = [s.A[:, o_deq + (2 + k) * I5: o_deq + (3 + k) * I5] for k in range(2)]
    tmpb = s.A[:, o_deq + 4 * I5: o_deq + 5 * I5]
    hb = s.A[:, o_deq: o_deq + IC * 512]
    o_w = o_deq + 5 * I5
    wg_l = [s.A[:, o_w + k * 2048: o_w + (k + 1) * 2048] for k in range(2)]
    wu_l = [s.A[:, o_w + (2 + k) * 2048: o_w + (3 + k) * 2048] for k in range(2)]
    o_wd = o_w + 4 * 2048
    wd_l = [s.A[:, o_wd + k * 512: o_wd + (k + 1) * 512] for k in range(2)]
    assert o_wd + 2 * 512 <= AB

    idx_u8 = p.sbuf("idx_u8", [128, 2 * I5], U8)
    s.xtmp = p.sbuf("xtmp", [128, 2 * TT], F32)
    s.ones_l = p.sbuf("ones_l", [128, 1], BF16)
    s.ones_c = p.sbuf("ones_c", [1, 128], F32)
    s.r0 = p.sbuf("r0", [1, TT], F32)
    s.r1 = p.sbuf("r1", [1, TT], F32)
    s.r2 = p.sbuf("r2", [1, TT], F32)
    s.r3 = p.sbuf("r3", [1, TT], F32)
    s.mb = p.sbuf("mb", [128, TT], F32)
    s.rb = p.sbuf("rb", [128, TT], F32)
    s.epsc = p.sbuf("epsc", [128, 1], F32)
    lutr = p.sbuf("lutr", [1, 16], F32)
    lut128 = p.sbuf("lut128", [128, 16], F32)
    gsil = p.sbuf("gsil", [128, 512], F32)
    xq = p.sbuf("xq", [128, 2 * 512], F32)
    oute = p.sbuf("oute", [128, 2 * 512], F32)

    P = [p.psum(f"P{i}", [128, 512]) for i in range(8)]
    pa, pb = P[0], P[1]

    p.dma("sync", lutr[:], lut, "ldc")
    p.op("vector", lambda h: h.memset(s.ones_l[:], 1.0))
    p.op("vector", lambda h: h.memset(s.epsc[:], 1e-5))
    p.op("vector", lambda h: h.memset(s.ones_c[:], 1.0), inc="dv")
    p.wait("tensor", "ldc")
    p.wait("tensor", "dv")
    p.op("tensor", lambda h: h.matmul(pa[:, 0:16], s.ones_c[0:1, :], lutr[0:1, :],
                                      start=True, stop=True), inc="pe")
    p.wait("vector", "pe")
    p.op("vector", lambda h: h.tensor_copy(out=lut128[:], in_=pa[:, 0:16]), inc="dv")
    emit_load_x_as_bf16(p, s, xT, TT, NCH)

    emit_ln(p, s, pa, pb, TT, NCH)

    jobs = []
    for i in range(16):
        jobs.append((gidxT[i * 128:(i + 1) * 128, :], I5, WgS[i * 128:(i + 1) * 128, :]))
    for i in range(16):
        jobs.append((uidxT[i * 128:(i + 1) * 128, :], I5, WuS[i * 128:(i + 1) * 128, :]))
    for i in range(IC):
        jobs.append((didxT[i * 128:(i + 1) * 128, :], D, WdS[i * 128:(i + 1) * 128, :]))
    n_jobs = len(jobs)
    for sidx, (src, cols, dst) in enumerate(jobs):
        sl = sidx % 2
        if sidx >= 2:
            p.wait("sync", "dq", sidx - 1)
        p.dma("sync", idx_u8[:, sl * I5: sl * I5 + cols], src, f"wdc{sl}")
        p.wait("gpsimd", f"wdc{sl}")
        p.op("gpsimd", lambda h, sl=sl, cols=cols: h.tensor_copy(
            out=idxb[sl][:, 0:cols], in_=idx_u8[:, sl * I5: sl * I5 + cols]), inc="cv")
        p.wait("vector", "cv")
        if sidx >= 2:
            p.wait("vector", f"so{sl}")
        p.op("vector", lambda h, sl=sl, cols=cols: h.tensor_scalar(
            out=accb[sl][:, 0:cols], in0=idxb[sl][:, 0:cols], scalar1=0.0,
            scalar2=lut128[:, 0:1], op0=ALU.is_equal, op1=ALU.mult))
        for v in range(1, 16):
            p.op("vector", lambda h, sl=sl, cols=cols, v=v: h.tensor_scalar(
                out=tmpb[:, 0:cols], in0=idxb[sl][:, 0:cols], scalar1=float(v),
                scalar2=lut128[:, v:v + 1], op0=ALU.is_equal, op1=ALU.mult))
            p.op("vector", lambda h, sl=sl, cols=cols: h.tensor_tensor(
                out=accb[sl][:, 0:cols], in0=accb[sl][:, 0:cols], in1=tmpb[:, 0:cols],
                op=ALU.add), inc="dq" if v == 15 else None)
        p.wait("gpsimd", "dq")
        p.dma("gpsimd", dst, accb[sl][:, 0:cols], f"so{sl}")

    if debug:
        p.wait("gpsimd", "dv")
        p.dma("gpsimd", dbg_xn, s.xn, "yo0")
        p.wait("gpsimd", "so", 75 * 16)
        p.dma("gpsimd", dbg_wg, WgS[0:256, :], "yo0")
        p.dma("gpsimd", dbg_wd, WdS[0:256, :], "yo0")

    guard = {}

    for th in range(2):
        t0 = th * 512
        if th == 1:
            p.wait("vector", "pg")
        p.wait("sync", "so0")
        p.wait("sync", "so1")
        for ic in range(IC):
            sl = ic % 2
            if ("wslot", sl) in guard:
                p.wait("sync", "pg", guard[("wslot", sl)])
            p.dma("sync", wg_l[sl].rearrange("p (k i) -> p k i", k=16),
                  WgS[:, ic * 128:(ic + 1) * 128].rearrange("(k p) i -> p k i", p=128),
                  f"wd{sl}")
            p.dma("sync", wu_l[sl].rearrange("p (k i) -> p k i", k=16),
                  WuS[:, ic * 128:(ic + 1) * 128].rearrange("(k p) i -> p k i", p=128),
                  f"wd{sl}")
            bg, bu = 2 + sl, 4 + sl
            p.wait("tensor", f"wd{sl}")
            p.wait("tensor", "dv")
            if ("evac", bg) in guard:
                p.wait("tensor", "ev", guard[("evac", bg)])
            for k in range(NCH):
                p.op("tensor", lambda h, k=k, sl=sl, bg=bg, t0=t0,
                     st=(k == 0), sp=(k == NCH - 1): h.matmul(
                         P[bg][:, 0:512], wg_l[sl][:, k * 128:(k + 1) * 128],
                         s.xn[:, k * TT + t0: k * TT + t0 + 512], start=st, stop=sp))
            for k in range(NCH):
                p.op("tensor", lambda h, k=k, sl=sl, bu=bu, t0=t0,
                     st=(k == 0), sp=(k == NCH - 1): h.matmul(
                         P[bu][:, 0:512], wu_l[sl][:, k * 128:(k + 1) * 128],
                         s.xn[:, k * TT + t0: k * TT + t0 + 512], start=st, stop=sp),
                     inc="pg" if k == NCH - 1 else None)
            guard[("wslot", sl)] = p.semv["pg"]
            p.wait("scalar", "pg")
            if "gsil" in guard:
                p.wait("scalar", "ev", guard["gsil"])
            p.op("scalar", lambda h, bg=bg: h.activation(
                out=gsil[:, 0:512], in_=P[bg][:, 0:512], func=AF.Silu), inc="sc")
            p.wait("vector", "sc")
            p.op("vector", lambda h, bu=bu, ic=ic: h.tensor_tensor(
                out=hb[:, ic * 512:(ic + 1) * 512], in0=gsil[:, 0:512],
                in1=P[bu][:, 0:512], op=ALU.mult), inc="ev")
            guard[("evac", bg)] = p.semv["ev"]
            guard[("evac", bu)] = p.semv["ev"]
            guard["gsil"] = p.semv["ev"]
        if debug and th == 0:
            p.wait("gpsimd", "ev")
            p.dma("gpsimd", dbg_h, hb, "yo0")
        p.wait("tensor", "ev")
        banks = [6, 7, 0, 1]
        for og in range(4):
            for b in banks:
                if ("evac", b) in guard:
                    p.wait("tensor", "ev", guard[("evac", b)])
            for ic in range(IC):
                sl = ic % 2
                if ("wdslot", sl) in guard:
                    p.wait("sync", "pg", guard[("wdslot", sl)])
                p.dma("sync", wd_l[sl],
                      WdS[ic * 128:(ic + 1) * 128, og * 512:(og + 1) * 512], f"wde{sl}")
                p.wait("tensor", f"wde{sl}")
                for j in range(4):
                    p.op("tensor", lambda h, b=banks[j], j=j, sl=sl, ic=ic: h.matmul(
                        P[b][:, 0:512], wd_l[sl][:, j * 128:(j + 1) * 128],
                        hb[:, ic * 512:(ic + 1) * 512],
                        start=(ic == 0), stop=(ic == IC - 1)),
                        inc="pg" if j == 3 else None)
                guard[("wdslot", sl)] = p.semv["pg"]
            chains_done = p.semv["pg"]
            for j in range(4):
                oc = og * 4 + j
                xs = slice((j % 2) * 512, (j % 2) * 512 + 512)
                if ("xq", j % 2) in guard:
                    p.wait("sync", "ev", guard[("xq", j % 2)])
                p.dma("sync", xq[:, xs], xT[oc * 128:(oc + 1) * 128, t0:t0 + 512],
                      f"ld{j % 2}")
                p.wait("vector", "pg", chains_done)
                p.wait("vector", f"ld{j % 2}")
                p.wait("vector", f"yo{j % 2}")
                p.op("vector", lambda h, b=banks[j], xs=xs: h.scalar_tensor_tensor(
                    out=oute[:, xs], in0=P[b][:, 0:512], scalar=0.5, in1=xq[:, xs],
                    op0=ALU.mult, op1=ALU.add), inc="ev")
                guard[("evac", banks[j])] = p.semv["ev"]
                guard[("xq", j % 2)] = p.semv["ev"]
                p.wait("gpsimd", "ev")
                p.dma("gpsimd", yT[oc * 128:(oc + 1) * 128, t0:t0 + 512], oute[:, xs],
                      f"yo{j % 2}")
    p.wait("gpsimd", "yo0")
    p.wait("gpsimd", "yo1")
    return p.build()


def stage_ghost_inputs(x_slice_T, gate_idx, up_idx, down_idx, lut):
    return {
        "xT": np.ascontiguousarray(x_slice_T, dtype=np.float32),
        "gidxT": np.ascontiguousarray(np.asarray(gate_idx).T).astype(np.uint8),
        "uidxT": np.ascontiguousarray(np.asarray(up_idx).T).astype(np.uint8),
        "didxT": np.ascontiguousarray(np.asarray(down_idx).T).astype(np.uint8),
        "lut": np.ascontiguousarray(np.asarray(lut), dtype=np.float32).reshape(1, 16),
    }



def build_mamba_front(TT=1024):
    D, I4, NCH = 2048, 4096, 16
    p = Prog("mfront")
    nc = p.nc
    xT = nc.dram_tensor("xT", [D, TT], F32, kind="ExternalInput").ap()
    wipT = nc.dram_tensor("wipT", [D, 2 * I4], F32, kind="ExternalInput").ap()
    hinT = nc.dram_tensor("hinT", [I4, TT], BF16, kind="ExternalOutput").ap()
    sgT = nc.dram_tensor("sgT", [I4, TT], BF16, kind="ExternalOutput").ap()

    for sn in ("ld0", "ld1", "wd0", "wd1", "cv", "dv", "pe", "sc", "ev", "pg",
               "yo0", "yo1", "yo2", "yo3"):
        p.sem(sn)

    s = NS()
    s.A = p.sbuf("A", [128, 40 * 1024], BF16)
    s.xb = s.A[:, 0:NCH * TT]
    s.xn = s.A[:, NCH * TT:2 * NCH * TT]
    wslab_b = [s.A[:, 32 * TT + k * 2048: 32 * TT + (k + 1) * 2048] for k in range(2)]
    outb = [s.A[:, 32 * TT + 2 * 2048 + k * 512: 32 * TT + 2 * 2048 + (k + 1) * 512]
            for k in range(4)]
    s.xtmp = p.sbuf("xtmp", [128, 2 * TT], F32)
    wslab_f = [p.sbuf(f"wsf{k}", [128, 2048], F32) for k in range(2)]
    s.ones_l = p.sbuf("ones_l", [128, 1], BF16)
    s.ones_c = p.sbuf("ones_c", [1, 128], F32)
    s.epsc = p.sbuf("epsc", [128, 1], F32)
    s.r0 = p.sbuf("r0", [1, TT], F32)
    s.r1 = p.sbuf("r1", [1, TT], F32)
    s.r2 = p.sbuf("r2", [1, TT], F32)
    s.r3 = p.sbuf("r3", [1, TT], F32)
    s.mb = p.sbuf("mb", [128, TT], F32)
    s.rb = p.sbuf("rb", [128, TT], F32)
    P = [p.psum(f"P{i}", [128, 512]) for i in range(6)]
    pa, pb = P[0], P[1]

    p.op("vector", lambda h: h.memset(s.ones_l[:], 1.0))
    p.op("vector", lambda h: h.memset(s.epsc[:], 1e-5))
    p.op("vector", lambda h: h.memset(s.ones_c[:], 1.0), inc="dv")
    emit_load_x_as_bf16(p, s, xT, TT, NCH)
    emit_ln(p, s, pa, pb, TT, NCH)

    guard = {}
    for oc in range(2 * I4 // 128):
        sl = oc % 2
        if ("wslot", sl) in guard:
            p.wait("sync", "cv", guard[("wslot", sl)])
        p.dma("sync", wslab_f[sl].rearrange("p (k i) -> p k i", k=16),
              wipT[:, oc * 128:(oc + 1) * 128].rearrange("(k p) i -> p k i", p=128),
              f"wd{sl}")
        p.wait("vector", f"wd{sl}")
        if ("wbslot", sl) in guard:
            p.wait("vector", "pg", guard[("wbslot", sl)])
        p.op("vector", lambda h, sl=sl: h.tensor_copy(out=wslab_b[sl], in_=wslab_f[sl][:]),
             inc="cv")
        guard[("wslot", sl)] = p.semv["cv"]
        p.wait("tensor", "cv")
        p.wait("tensor", "dv")
        for th in range(2):
            bk = 2 + 2 * (oc % 2) + th
            if ("evac", bk) in guard:
                p.wait("tensor", "ev", guard[("evac", bk)])
            for k in range(NCH):
                p.op("tensor", lambda h, k=k, sl=sl, bk=bk, th=th,
                     st=(k == 0), sp=(k == NCH - 1): h.matmul(
                         P[bk][:, 0:512], wslab_b[sl][:, k * 128:(k + 1) * 128],
                         s.xn[:, k * TT + th * 512: k * TT + th * 512 + 512],
                         start=st, stop=sp), inc="pg" if k == NCH - 1 else None)
        guard[("wbslot", sl)] = p.semv["pg"]
        p.wait("scalar", "pg")
        for th in range(2):
            bk = 2 + 2 * (oc % 2) + th
            slot = 2 * (oc % 2) + th
            ob = outb[slot]
            p.wait("scalar", f"yo{slot}")
            if oc < 32:
                p.op("scalar", lambda h, bk=bk, ob=ob: h.copy(out=ob, in_=P[bk][:, 0:512]),
                     inc="ev")
            else:
                p.op("scalar", lambda h, bk=bk, ob=ob: h.activation(
                    out=ob, in_=P[bk][:, 0:512], func=AF.Silu), inc="ev")
            guard[("evac", bk)] = p.semv["ev"]
            p.wait("gpsimd", "ev")
            dst = (hinT if oc < 32 else sgT)
            r0 = (oc if oc < 32 else oc - 32) * 128
            p.dma("gpsimd", dst[r0:r0 + 128, th * 512:th * 512 + 512], ob, f"yo{slot}")
    for sn in ("yo0", "yo1", "yo2", "yo3"):
        p.wait("gpsimd", sn)
    return p.build()



def build_conv(CH=512, T=8192):
    p = Prog("conv")
    nc = p.nc
    hinT = nc.dram_tensor("hinT", [CH, T], BF16, kind="ExternalInput").ap()
    convw = nc.dram_tensor("convw", [CH, 4], F32, kind="ExternalInput").ap()
    convb = nc.dram_tensor("convb", [CH, 1], F32, kind="ExternalInput").ap()
    uT = nc.dram_tensor("uT", [CH, T], BF16, kind="ExternalOutput").ap()
    for sn in ("ld0", "ld1", "ldc", "dv", "sc", "yo0", "yo1"):
        p.sem(sn)
    nic = CH // 128
    hsb = p.sbuf("hsb", [128, 2 * (T + 3)], BF16)
    acc = p.sbuf("acc", [128, T], F32)
    ub = p.sbuf("ub", [128, 2 * T], BF16)
    wc = p.sbuf("wc", [128, 4 * nic], F32)
    bc = p.sbuf("bc", [128, nic], F32)
    p.dma("sync", wc.rearrange("p (k f) -> p k f", k=nic),
          convw.rearrange("(k p) f -> p k f", p=128), "ldc")
    p.dma("sync", bc[:], convb.rearrange("(k p) f -> p (k f)", p=128), "ldc", slow=True)
    guard = {}
    for ic in range(nic):
        sl = ic % 2
        o = sl * (T + 3)
        if ("h", sl) in guard:
            p.wait("sync", "sc", guard[("h", sl)])
        p.dma("sync", hsb[:, o + 3: o + 3 + T], hinT[ic * 128:(ic + 1) * 128, :],
              f"ld{sl}")
        p.wait("vector", f"ld{sl}")
        p.wait("vector", "ldc")
        if ic == 0:
            p.op("vector", lambda h: h.memset(hsb[:, 0:3], 0.0))
            p.op("vector", lambda h: h.memset(hsb[:, T + 3: T + 6], 0.0))
        if ic >= 1:
            p.wait("vector", "sc", guard[("acc", 0)])
        p.op("vector", lambda h, o=o, ic=ic: h.tensor_scalar(
            out=acc[:], in0=hsb[:, o: o + T], scalar1=wc[:, 4 * ic: 4 * ic + 1],
            scalar2=None, op0=ALU.mult))
        for k in range(1, 4):
            p.op("vector", lambda h, o=o, ic=ic, k=k: h.scalar_tensor_tensor(
                out=acc[:], in0=hsb[:, o + k: o + k + T],
                scalar=wc[:, 4 * ic + k: 4 * ic + k + 1], in1=acc[:],
                op0=ALU.mult, op1=ALU.add), inc="dv" if k == 3 else None)
        guard[("h", sl)] = p.semv["dv"]
        p.wait("scalar", "dv")
        p.wait("scalar", f"yo{sl}")
        p.op("scalar", lambda h, sl=sl, ic=ic: h.activation(
            out=ub[:, sl * T:(sl + 1) * T], in_=acc[:], func=AF.Silu,
            bias=bc[:, ic:ic + 1]), inc="sc")
        guard[("h", sl)] = p.semv["sc"]
        guard[("acc", 0)] = p.semv["sc"]
        p.wait("gpsimd", "sc")
        p.dma("gpsimd", uT[ic * 128:(ic + 1) * 128, :], ub[:, sl * T:(sl + 1) * T],
              f"yo{sl}")
    p.wait("gpsimd", "yo0")
    p.wait("gpsimd", "yo1")
    return p.build()



def build_xproj_dt(TT=1024, do_dt=True, do_bc=True, dt_mode='full'):
    I4, R = 4096, 128
    p = Prog("xproj")
    nc = p.nc
    uT = nc.dram_tensor("uT", [I4, TT], BF16, kind="ExternalInput").ap()
    xpT = nc.dram_tensor("xpT", [I4, 160], F32, kind="ExternalInput").ap()
    dtpT = nc.dram_tensor("dtpT", [R, I4], F32, kind="ExternalInput").ap()
    dtb = nc.dram_tensor("dtb", [I4, 1], F32, kind="ExternalInput").ap()
    dtT = nc.dram_tensor("dtT", [I4, TT], F32, kind="ExternalOutput").ap()
    Bo = nc.dram_tensor("Bo", [16, TT], F32, kind="ExternalOutput").ap()
    Co = nc.dram_tensor("Co", [16, TT], F32, kind="ExternalOutput").ap()
    for sn in ("ld", "cv", "dv", "pe", "sc", "ev", "pg", "yo0", "yo1", "yob"):
        p.sem(sn)
    nk = I4 // 128
    ub = p.sbuf("ub", [128, nk * TT], BF16)
    xpf = p.sbuf("xpf", [128, nk * 160], F32)
    xpb = p.sbuf("xpb", [128, nk * 160], BF16)
    dtpf = p.sbuf("dtpf", [128, I4], F32)
    dtpb = p.sbuf("dtpb", [128, I4], BF16)
    dtbc = p.sbuf("dtbc", [128, nk], F32)
    tsb = p.sbuf("tsb", [128, TT], BF16)
    t1 = p.sbuf("t1", [128, 512], F32)
    a1 = p.sbuf("a1", [128, 512], F32)
    dto = p.sbuf("dto", [128, 2 * 512], F32)
    bco = p.sbuf("bco", [32, TT], F32)
    P = [p.psum(f"P{i}", [128, 512]) for i in range(6)]

    p.dma("sync", ub.rearrange("p (k t) -> p k t", k=nk),
          uT.rearrange("(k p) t -> p k t", p=128), "ld")
    p.dma("sync", xpf.rearrange("p (k r) -> p k r", k=nk),
          xpT.rearrange("(k p) r -> p k r", p=128), "ld")
    p.dma("sync", dtpf[:], dtpT, "ld")
    p.dma("sync", dtbc[:], dtb.rearrange("(k p) f -> p (k f)", p=128), "ld", slow=True)
    p.wait("vector", "ld")
    p.op("vector", lambda h: h.tensor_copy(out=xpb[:], in_=xpf[:]))
    p.op("vector", lambda h: h.tensor_copy(out=dtpb[:], in_=dtpf[:]), inc="cv")
    p.wait("tensor", "cv")
    for th in range(2):
        for k in range(nk):
            p.op("tensor", lambda h, k=k, th=th, st=(k == 0), sp=(k == nk - 1): h.matmul(
                P[0 + th][:, 0:512], xpb[:, k * 160: k * 160 + 128],
                ub[:, k * TT + th * 512: k * TT + th * 512 + 512], start=st, stop=sp))
            p.op("tensor", lambda h, k=k, th=th, st=(k == 0), sp=(k == nk - 1): h.matmul(
                P[2 + th][0:32, 0:512], xpb[:, k * 160 + 128: (k + 1) * 160],
                ub[:, k * TT + th * 512: k * TT + th * 512 + 512], start=st, stop=sp),
                inc="pg" if k == nk - 1 else None)
        p.wait("vector", "pg")
        p.op("vector", lambda h, th=th: h.tensor_copy(
            out=tsb[:, th * 512:(th + 1) * 512], in_=P[0 + th][:, 0:512]))
        p.op("vector", lambda h, th=th: h.tensor_copy(
            out=bco[0:32, th * 512:(th + 1) * 512], in_=P[2 + th][0:32, 0:512]),
            inc="dv")
    p.wait("gpsimd", "dv")
    if do_bc:
        p.dma("gpsimd", Bo, bco[0:16, :], "yob")
        p.dma("gpsimd", Co, bco[16:32, :], "yob")
    else:
        p.dma("gpsimd", Bo, bco[0:16, :], "yob")
        p.dma("gpsimd", Co, bco[0:16, :], "yob")
    guard = {}
    for oc in range(nk if do_dt else 0):
        for th in range(2):
            bk = 4 + (oc * 2 + th) % 2
            if ("e", bk) in guard:
                p.wait("tensor", "ev", guard[("e", bk)])
            if ("ed", bk) in guard:
                p.wait("tensor", "dv", guard[("ed", bk)])
            else:
                p.wait("tensor", "dv")
            p.op("tensor", lambda h, oc=oc, th=th, bk=bk: h.matmul(
                P[bk][:, 0:512], dtpb[:, oc * 128:(oc + 1) * 128],
                tsb[:, th * 512:(th + 1) * 512], start=True, stop=True), inc="pg")
            v = p.semv["pg"]
            p.wait("vector", "pg", v)
            p.wait("scalar", "pg", v)
            p.wait("vector", f"yo{th}")
            p.wait("scalar", f"yo{th}")
            if dt_mode in ('abs', 'absexp', 'absexpln'):
                p.wait("scalar", "dv")
                p.op("scalar", lambda h, bk=bk, oc=oc: h.activation(
                    out=a1[:], in_=P[bk][:, 0:512], func=AF.Abs, bias=dtbc[:, oc:oc + 1]),
                    inc="ev")
                guard[("e", bk)] = p.semv["ev"]
                if dt_mode in ('absexp', 'absexpln'):
                    p.op("scalar", lambda h: h.activation(out=a1[:], in_=a1[:],
                                                          func=AF.Exp, scale=-1.0),
                         inc="sc")
                if dt_mode == 'absexpln':
                    p.op("scalar", lambda h: h.activation(out=a1[:], in_=a1[:],
                                                          func=AF.Ln, bias=1.0),
                         inc="sc")
                p.op("vector", lambda h, th=th: h.tensor_copy(
                    out=dto[:, th * 512:(th + 1) * 512], in_=a1[:]), inc="dv")
            elif dt_mode == 'copy':
                p.wait("scalar", "dv")
                p.op("scalar", lambda h, bk=bk: h.copy(out=a1[:], in_=P[bk][:, 0:512]),
                     inc="ev")
                guard[("e", bk)] = p.semv["ev"]
                p.op("vector", lambda h, th=th: h.tensor_copy(
                    out=dto[:, th * 512:(th + 1) * 512], in_=a1[:]), inc="dv")
            elif dt_mode == 'ts':
                p.op("vector", lambda h, bk=bk, oc=oc: h.tensor_scalar(
                    out=t1[:], in0=P[bk][:, 0:512], scalar1=dtbc[:, oc:oc + 1],
                    scalar2=0.0, op0=ALU.add, op1=ALU.max), inc="ev")
                guard[("e", bk)] = p.semv["ev"]
                p.op("vector", lambda h, th=th: h.tensor_copy(
                    out=dto[:, th * 512:(th + 1) * 512], in_=t1[:]), inc="dv")
            else:
                p.op("vector", lambda h, bk=bk, oc=oc: h.tensor_scalar(
                    out=t1[:], in0=P[bk][:, 0:512], scalar1=dtbc[:, oc:oc + 1],
                    scalar2=0.0, op0=ALU.add, op1=ALU.max), inc="dv")
                p.wait("scalar", "dv")
                p.op("scalar", lambda h, bk=bk, oc=oc: h.activation(
                    out=a1[:], in_=P[bk][:, 0:512], func=AF.Abs, bias=dtbc[:, oc:oc + 1]),
                    inc="ev")
                guard[("e", bk)] = p.semv["ev"]
                p.op("scalar", lambda h: h.activation(out=a1[:], in_=a1[:], func=AF.Exp,
                                                      scale=-1.0))
                p.op("scalar", lambda h: h.activation(out=a1[:], in_=a1[:], func=AF.Ln,
                                                      bias=1.0), inc="sc")
                p.wait("vector", "sc")
                p.op("vector", lambda h, th=th: h.tensor_tensor(
                    out=dto[:, th * 512:(th + 1) * 512], in0=t1[:], in1=a1[:], op=ALU.add),
                    inc="dv")
            guard[("ed", bk)] = p.semv["dv"]
            p.wait("gpsimd", "dv")
            p.dma("gpsimd", dtT[oc * 128:(oc + 1) * 128, th * 512:(th + 1) * 512],
                  dto[:, th * 512:(th + 1) * 512], f"yo{th}")
    if not do_dt:
        p.wait("gpsimd", "ld")
        p.op("gpsimd", lambda h: h.memset(dto[:, 0:4], 0.0))
        p.dma("gpsimd", dtT[0:128, 0:4], dto[:, 0:4], "yo0")
    p.wait("gpsimd", "yo0")
    p.wait("gpsimd", "yo1")
    p.wait("gpsimd", "yob")
    return p.build()



def build_scan(CH=512, T=8192, TC=2048):
    N = 16
    p = Prog("scan")
    nc = p.nc
    uT = nc.dram_tensor("uT", [CH, T], BF16, kind="ExternalInput").ap()
    sgT = nc.dram_tensor("sgT", [CH, T], BF16, kind="ExternalInput").ap()
    dtT = nc.dram_tensor("dtT", [CH, T], F32, kind="ExternalInput").ap()
    Bf = nc.dram_tensor("Bf", [16, T], F32, kind="ExternalInput").ap()
    Cf = nc.dram_tensor("Cf", [16, T], F32, kind="ExternalInput").ap()
    Alog = nc.dram_tensor("Alog", [CH, N], F32, kind="ExternalInput").ap()
    sel = nc.dram_tensor("sel", [16, 16 * 128], BF16, kind="ExternalInput").ap()
    Dv = nc.dram_tensor("Dv", [CH, 1], F32, kind="ExternalInput").ap()
    ygT = nc.dram_tensor("ygT", [CH, T], BF16, kind="ExternalOutput").ap()

    for sn in ("ld", "ldc", "wd", "dv", "pe", "sc", "ev", "pg", "yo0", "yo1"):
        p.sem(sn)
    nic = CH // 128
    ntc = T // TC
    dt_t = [p.sbuf(f"dt{ic}", [128, TC], F32) for ic in range(nic)]
    u_t = [p.sbuf(f"u{ic}", [128, TC], BF16) for ic in range(nic)]
    sg_t = [p.sbuf(f"sg{ic}", [128, TC], BF16) for ic in range(nic)]
    dtu_t = [p.sbuf(f"dtu{ic}", [128, TC], F32) for ic in range(nic)]
    y_t = [p.sbuf(f"y{ic}", [128, TC], F32) for ic in range(nic)]
    dA = p.sbuf("dA", [128, TC], F32)
    dBx = p.sbuf("dBx", [128, TC], F32)
    hh = p.sbuf("hh", [128, TC], F32)
    Bb = p.sbuf("Bb", [128, TC], F32)
    Cb = p.sbuf("Cb", [128, TC], F32)
    Brow = p.sbuf("Brow", [16, TC], BF16)
    Crow = p.sbuf("Crow", [16, TC], BF16)
    Brf = p.sbuf("Brf", [16, TC], F32)
    Crf = p.sbuf("Crf", [16, TC], F32)
    Acol = p.sbuf("Acol", [128, nic * N], F32)
    Dcol = p.sbuf("Dcol", [128, nic], F32)
    state = p.sbuf("state", [128, nic * N], F32)
    ygb = p.sbuf("ygb", [128, 2 * TC], BF16)
    ones_c = p.sbuf("ones_c", [1, 128], BF16)
    selt = p.sbuf("selt", [16, 16 * 128], BF16)
    pbc = p.psum("pbc", [128, 512])
    pbc2 = p.psum("pbc2", [128, 512])

    p.dma("sync", selt[:], sel, "ldc")
    p.dma("sync", Acol.rearrange("p (k n) -> p k n", k=nic),
          Alog.rearrange("(k p) n -> p k n", p=128), "ldc")
    p.dma("sync", Dcol[:], Dv.rearrange("(k p) f -> p (k f)", p=128), "ldc", slow=True)
    p.wait("vector", "ldc")
    p.op("vector", lambda h: h.memset(ones_c[:], 1.0))
    p.op("vector", lambda h: h.memset(state[:], 0.0), inc="dv")
    p.wait("scalar", "ldc")
    p.op("scalar", lambda h: h.activation(out=Acol[:], in_=Acol[:], func=AF.Exp),
         inc="sc")
    p.wait("vector", "sc")
    p.op("vector", lambda h: h.tensor_scalar(
        out=Acol[:], in0=Acol[:], scalar1=-1.0, scalar2=None, op0=ALU.mult), inc="dv")

    guard = {("dAfree",): p.semv["dv"]}
    for tc in range(ntc):
        t0 = tc * TC
        if tc > 0:
            p.wait("sync", "dv", guard[("bcrows",)])
        p.dma("sync", Brf[:], Bf[:, t0:t0 + TC], "wd")
        p.dma("sync", Crf[:], Cf[:, t0:t0 + TC], "wd")
        p.wait("vector", "wd")
        if tc > 0:
            p.wait("vector", "dv", guard[("bcrows",)])
        p.op("vector", lambda h: h.tensor_copy(out=Brow[:], in_=Brf[:]))
        p.op("vector", lambda h: h.tensor_copy(out=Crow[:], in_=Crf[:]), inc="dv")
        guard[("bcrows",)] = p.semv["dv"]
        for ic in range(nic):
            if tc > 0:
                p.wait("sync", "dv", guard[("done", ic)])
            p.dma("sync", dt_t[ic][:], dtT[ic * 128:(ic + 1) * 128, t0:t0 + TC], "ld")
            p.dma("sync", u_t[ic][:], uT[ic * 128:(ic + 1) * 128, t0:t0 + TC], "ld")
            p.dma("sync", sg_t[ic][:], sgT[ic * 128:(ic + 1) * 128, t0:t0 + TC], "ld")
        p.wait("vector", "ld")
        p.wait("scalar", "ld")
        for ic in range(nic):
            p.op("vector", lambda h, ic=ic: h.tensor_scalar(
                out=y_t[ic][:], in0=u_t[ic][:], scalar1=Dcol[:, ic:ic + 1],
                scalar2=None, op0=ALU.mult))
            p.op("vector", lambda h, ic=ic: h.tensor_tensor(
                out=dtu_t[ic][:], in0=dt_t[ic][:], in1=u_t[ic][:], op=ALU.mult),
                inc="dv")
        for n in range(N):
            nbt = TC // 512
            for j in range(nbt):
                p.wait("tensor", "dv")
                if ("bb", 0) in guard:
                    p.wait("tensor", "dv", guard[("bb", 0)])
                p.op("tensor", lambda h, n=n, j=j: h.matmul(
                    pbc[:, 0:512], selt[:, n * 128:(n + 1) * 128],
                    Brow[:, j * 512: (j + 1) * 512],
                    start=True, stop=True))
                p.op("tensor", lambda h, n=n, j=j: h.matmul(
                    pbc2[:, 0:512], selt[:, n * 128:(n + 1) * 128],
                    Crow[:, j * 512: (j + 1) * 512],
                    start=True, stop=True), inc="pe")
                p.wait("vector", "pe")
                p.op("vector", lambda h, j=j: h.tensor_copy(
                    out=Bb[:, j * 512:(j + 1) * 512], in_=pbc[:, 0:512]))
                p.op("vector", lambda h, j=j: h.tensor_copy(
                    out=Cb[:, j * 512:(j + 1) * 512], in_=pbc2[:, 0:512]), inc="dv")
            guard[("bb", 0)] = p.semv["dv"]
            for ic in range(nic):
                p.wait("scalar", "dv", guard.get(("dAfree",), 0))
                p.op("scalar", lambda h, ic=ic, n=n: h.activation(
                    out=dA[:], in_=dt_t[ic][:], func=AF.Exp,
                    scale=Acol[:, ic * N + n: ic * N + n + 1]), inc="sc")
                p.wait("vector", "sc")
                p.op("vector", lambda h, ic=ic: h.tensor_tensor(
                    out=dBx[:], in0=dtu_t[ic][:], in1=Bb[:], op=ALU.mult))
                p.op("vector", lambda h, ic=ic, n=n: h.tensor_tensor_scan(
                    out=hh[:], data0=dA[:], data1=dBx[:],
                    initial=state[:, ic * N + n: ic * N + n + 1],
                    op0=ALU.mult, op1=ALU.add))
                guard[("dAfree",)] = p.semv["dv"] + 2
                p.op("vector", lambda h, ic=ic, n=n: h.tensor_copy(
                    out=state[:, ic * N + n: ic * N + n + 1], in_=hh[:, TC - 1: TC]))
                p.op("vector", lambda h, ic=ic: h.tensor_tensor(
                    out=hh[:], in0=hh[:], in1=Cb[:], op=ALU.mult))
                p.op("vector", lambda h, ic=ic: h.tensor_tensor(
                    out=y_t[ic][:], in0=y_t[ic][:], in1=hh[:], op=ALU.add), inc="dv")
                guard[("dAfree",)] = p.semv["dv"]
        for ic in range(nic):
            sl = ic % 2
            p.wait("vector", f"yo{sl}")
            p.op("vector", lambda h, ic=ic, sl=sl: h.tensor_tensor(
                out=ygb[:, sl * TC:(sl + 1) * TC], in0=y_t[ic][:], in1=sg_t[ic][:],
                op=ALU.mult), inc="dv")
            guard[("done", ic)] = p.semv["dv"]
            p.wait("gpsimd", "dv")
            p.dma("gpsimd", ygT[ic * 128:(ic + 1) * 128, t0:t0 + TC],
                  ygb[:, sl * TC:(sl + 1) * TC], f"yo{sl}")
    p.wait("gpsimd", "yo0")
    p.wait("gpsimd", "yo1")
    return p.build()



def build_outproj(TT=1024, k_dim=4096):
    I4, D = k_dim, 2048
    p = Prog("outp")
    nc = p.nc
    ygT = nc.dram_tensor("ygT", [I4, TT], BF16, kind="ExternalInput").ap()
    wopT = nc.dram_tensor("wopT", [I4, D], F32, kind="ExternalInput").ap()
    xT = nc.dram_tensor("xT", [D, TT], F32, kind="ExternalInput").ap()
    yT = nc.dram_tensor("yT", [D, TT], F32, kind="ExternalOutput").ap()
    for sn in ("ldy", "ld0", "ld1", "wd0", "wd1", "cv", "dv", "pe", "ev", "pg",
               "yo0", "yo1"):
        p.sem(sn)
    nk = I4 // 128
    ygb = p.sbuf("ygb", [128, nk * TT], BF16)
    wf = [p.sbuf(f"wf{k}", [128, nk * 128], F32) for k in range(2)]
    wb = [p.sbuf(f"wb{k}", [128, nk * 128], BF16) for k in range(2)]
    xq = p.sbuf("xq", [128, 2 * 512], F32)
    oute = p.sbuf("oute", [128, 2 * 512], F32)
    P = [p.psum(f"P{i}", [128, 512]) for i in range(4)]

    p.dma("sync", ygb.rearrange("p (k t) -> p k t", k=nk),
          ygT.rearrange("(k p) t -> p k t", p=128), "ldy")
    guard = {}
    for oc in range(D // 128):
        sl = oc % 2
        if ("w", sl) in guard:
            p.wait("sync", "cv", guard[("w", sl)])
        p.dma("sync", wf[sl].rearrange("p (k i) -> p k i", k=nk),
              wopT[:, oc * 128:(oc + 1) * 128].rearrange("(k p) i -> p k i", p=128),
              f"wd{sl}")
        p.wait("vector", f"wd{sl}")
        if ("wb", sl) in guard:
            p.wait("vector", "pg", guard[("wb", sl)])
        p.op("vector", lambda h, sl=sl: h.tensor_copy(out=wb[sl][:], in_=wf[sl][:]),
             inc="cv")
        guard[("w", sl)] = p.semv["cv"]
        p.wait("tensor", "cv")
        p.wait("tensor", "ldy")
        for th in range(2):
            bk = 2 * sl + th
            if ("e", bk) in guard:
                p.wait("tensor", "ev", guard[("e", bk)])
            for k in range(nk):
                p.op("tensor", lambda h, k=k, th=th, sl=sl, bk=bk,
                     st=(k == 0), sp=(k == nk - 1): h.matmul(
                         P[bk][:, 0:512], wb[sl][:, k * 128:(k + 1) * 128],
                         ygb[:, k * TT + th * 512: k * TT + th * 512 + 512],
                         start=st, stop=sp), inc="pg" if k == nk - 1 else None)
        guard[("wb", sl)] = p.semv["pg"]
        for th in range(2):
            bk = 2 * sl + th
            xs = slice(th * 512, (th + 1) * 512)
            if ("xq", th) in guard:
                p.wait("sync", "ev", guard[("xq", th)])
            p.dma("sync", xq[:, xs], xT[oc * 128:(oc + 1) * 128, th * 512:(th + 1) * 512],
                  f"ld{th}")
            p.wait("vector", "pg")
            p.wait("vector", f"ld{th}")
            p.wait("vector", f"yo{th}")
            p.op("vector", lambda h, bk=bk, xs=xs: h.tensor_tensor(
                out=oute[:, xs], in0=P[bk][:, 0:512], in1=xq[:, xs], op=ALU.add),
                inc="ev")
            guard[("e", bk)] = p.semv["ev"]
            guard[("xq", th)] = p.semv["ev"]
            p.wait("gpsimd", "ev")
            p.dma("gpsimd", yT[oc * 128:(oc + 1) * 128, th * 512:(th + 1) * 512],
                  oute[:, xs], f"yo{th}")
    p.wait("gpsimd", "yo0")
    p.wait("gpsimd", "yo1")
    return p.build()


def make_sel():
    sel = np.zeros((16, 16 * 128), dtype=np.float32)
    for n in range(16):
        sel[n, n * 128:(n + 1) * 128] = 1.0
    return np_bf16(sel)



def build_l1ffn(TT=1024):
    D, I8, NCH = 2048, 8192, 16
    NIC = I8 // 128
    p = Prog("l1ffn")
    nc = p.nc
    xT = nc.dram_tensor("xT", [D, TT], F32, kind="ExternalInput").ap()
    gwT = nc.dram_tensor("gwT", [D, I8], U8, kind="ExternalInput").ap()
    uwT = nc.dram_tensor("uwT", [D, I8], U8, kind="ExternalInput").ap()
    dwT = nc.dram_tensor("dwT", [I8, D], U8, kind="ExternalInput").ap()
    gsc = nc.dram_tensor("gsc", [I8, 1], F32, kind="ExternalInput").ap()
    usc = nc.dram_tensor("usc", [I8, 1], F32, kind="ExternalInput").ap()
    dsc = nc.dram_tensor("dsc", [D, 1], F32, kind="ExternalInput").ap()
    yT = nc.dram_tensor("yT", [D, TT], F32, kind="ExternalOutput").ap()

    for sn in ("ld0", "ld1", "ldc", "wd0", "wd1", "wde0", "wde1", "cv", "dv", "pe",
               "sc", "ev", "pg", "yo0", "yo1", "hs"):
        p.sem(sn)

    s = NS()
    AB = 74 * 1024
    s.A = p.sbuf("A", [128, AB], BF16)
    s.xb = s.A[:, 0:NCH * TT]
    s.xn = s.A[:, NCH * TT:2 * NCH * TT]
    hb = s.A[:, 2 * NCH * TT: 2 * NCH * TT + NIC * 512]
    o_w = 2 * NCH * TT + NIC * 512
    wgb = [s.A[:, o_w + k * 2048: o_w + (k + 1) * 2048] for k in range(2)]
    wub = [s.A[:, o_w + (2 + k) * 2048: o_w + (3 + k) * 2048] for k in range(2)]
    o_wd = o_w + 4 * 2048
    wdb = [s.A[:, o_wd + k * 512: o_wd + (k + 1) * 512] for k in range(2)]
    assert o_wd + 1024 <= AB

    wg8 = [p.sbuf(f"wg8{k}", [128, 2048], U8) for k in range(2)]
    wu8 = [p.sbuf(f"wu8{k}", [128, 2048], U8) for k in range(2)]
    wd8 = [p.sbuf(f"wd8{k}", [128, 512], U8) for k in range(2)]
    s.xtmp = p.sbuf("xtmp", [128, 2 * TT], F32)
    s.ones_l = p.sbuf("ones_l", [128, 1], BF16)
    s.ones_c = p.sbuf("ones_c", [1, 128], F32)
    s.epsc = p.sbuf("epsc", [128, 1], F32)
    s.r0 = p.sbuf("r0", [1, TT], F32)
    s.r1 = p.sbuf("r1", [1, TT], F32)
    s.r2 = p.sbuf("r2", [1, TT], F32)
    s.r3 = p.sbuf("r3", [1, TT], F32)
    s.mb = p.sbuf("mb", [128, TT], F32)
    s.rb = p.sbuf("rb", [128, TT], F32)
    m128 = p.sbuf("m128", [1, 128], BF16)
    xsb = p.sbuf("xsb", [1, TT], BF16)
    hsb = p.sbuf("hsb", [1, TT], BF16)
    gcol = p.sbuf("gcol", [128, NIC], F32)
    ucol = p.sbuf("ucol", [128, NIC], F32)
    dcol = p.sbuf("dcol", [128, NCH], F32)
    dcolh = p.sbuf("dcolh", [128, NCH], F32)
    gs = p.sbuf("gs", [128, 512], F32)
    xq = p.sbuf("xq", [128, 2 * 512], F32)
    oute = p.sbuf("oute", [128, 2 * 512], F32)
    P = [p.psum(f"P{i}", [128, 512]) for i in range(8)]
    pa, pb = P[0], P[1]

    p.op("vector", lambda h: h.memset(s.ones_l[:], 1.0))
    p.op("vector", lambda h: h.memset(s.epsc[:], 1e-5))
    p.op("vector", lambda h: h.memset(m128[:], -128.0))
    p.op("vector", lambda h: h.memset(s.ones_c[:], 1.0), inc="dv")
    p.dma("sync", gcol[:], gsc.rearrange("(k p) f -> p (k f)", p=128), "ldc", slow=True)
    p.dma("sync", ucol[:], usc.rearrange("(k p) f -> p (k f)", p=128), "ldc", slow=True)
    p.dma("sync", dcol[:], dsc.rearrange("(k p) f -> p (k f)", p=128), "ldc", slow=True)
    p.wait("vector", "ldc")
    p.wait("scalar", "ldc")
    p.op("vector", lambda h: h.tensor_scalar(
        out=dcolh[:], in0=dcol[:], scalar1=0.5, scalar2=None, op0=ALU.mult), inc="dv")
    emit_load_x_as_bf16(p, s, xT, TT, NCH)
    emit_ln(p, s, pa, pb, TT, NCH)

    for th in range(2):
        p.wait("tensor", "dv")
        for c in range(NCH):
            k0 = c * TT + th * 512
            p.op("tensor", lambda h, k0=k0, st=(c == 0), sp=(c == NCH - 1): h.matmul(
                pa[0:1, :], s.ones_l[:, 0:1], s.xn[:, k0:k0 + 512], start=st, stop=sp),
                inc="pe" if c == NCH - 1 else None)
        p.wait("vector", "pe")
        p.op("vector", lambda h, th=th: h.tensor_copy(
            out=xsb[0:1, th * 512:(th + 1) * 512], in_=pa[0:1, :]), inc="dv")

    guard = {}
    for th in range(2):
        t0 = th * 512
        if th == 1:
            p.wait("vector", "pg")
        for ic in range(NIC):
            sl = ic % 2
            if ("w8", sl) in guard:
                p.wait("sync", "cv", guard[("w8", sl)])
            p.dma("sync", wg8[sl][:].rearrange("p (k i) -> p k i", k=16),
                  gwT[:, ic * 128:(ic + 1) * 128].rearrange("(k p) i -> p k i", p=128),
                  f"wd{sl}")
            p.dma("sync", wu8[sl][:].rearrange("p (k i) -> p k i", k=16),
                  uwT[:, ic * 128:(ic + 1) * 128].rearrange("(k p) i -> p k i", p=128),
                  f"wd{sl}")
            p.wait("gpsimd", f"wd{sl}")
            if ("wb", sl) in guard:
                p.wait("gpsimd", "pg", guard[("wb", sl)])
            p.op("gpsimd", lambda h, sl=sl: h.tensor_copy(out=wgb[sl], in_=wg8[sl][:]))
            p.op("gpsimd", lambda h, sl=sl: h.tensor_copy(out=wub[sl], in_=wu8[sl][:]),
                 inc="cv")
            guard[("w8", sl)] = p.semv["cv"]
            bg, bu = 2 + sl, 4 + sl
            p.wait("tensor", "cv")
            p.wait("tensor", "dv")
            for (bk, wb) in ((bg, wgb[sl]), (bu, wub[sl])):
                if ("evac", bk) in guard:
                    p.wait("tensor", "ev", guard[("evac", bk)])
                for k in range(NCH):
                    p.op("tensor", lambda h, k=k, bk=bk, wb=wb, t0=t0,
                         st=(k == 0): h.matmul(
                             P[bk][:, 0:512], wb[:, k * 128:(k + 1) * 128],
                             s.xn[:, k * TT + t0: k * TT + t0 + 512],
                             start=st, stop=False))
                p.op("tensor", lambda h, bk=bk, t0=t0: h.matmul(
                    P[bk][:, 0:512], m128[0:1, :], xsb[0:1, t0:t0 + 512],
                    start=False, stop=True), inc="pg")
            guard[("wb", sl)] = p.semv["pg"]
            p.wait("scalar", "pg")
            if "gs" in guard:
                p.wait("scalar", "ev", guard["gs"])
            p.op("scalar", lambda h, bg=bg, ic=ic: h.activation(
                out=gs[:, 0:512], in_=P[bg][:, 0:512], func=AF.Silu,
                scale=gcol[:, ic:ic + 1]), inc="sc")
            p.wait("vector", "sc")
            p.op("vector", lambda h, bu=bu, ic=ic: h.scalar_tensor_tensor(
                out=hb[:, ic * 512:(ic + 1) * 512], in0=P[bu][:, 0:512],
                scalar=ucol[:, ic:ic + 1], in1=gs[:, 0:512],
                op0=ALU.mult, op1=ALU.mult), inc="ev")
            guard[("evac", bg)] = p.semv["ev"]
            guard[("evac", bu)] = p.semv["ev"]
            guard["gs"] = p.semv["ev"]
        p.wait("tensor", "ev")
        if ("hs",) in guard:
            p.wait("tensor", "dv", guard[("hs",)])
        for c in range(NIC):
            p.op("tensor", lambda h, c=c, st=(c == 0), sp=(c == NIC - 1): h.matmul(
                pb[0:1, :], s.ones_l[:, 0:1], hb[:, c * 512:(c + 1) * 512],
                start=st, stop=sp), inc="pe" if c == NIC - 1 else None)
        p.wait("vector", "pe")
        p.op("vector", lambda h, t0=t0: h.tensor_copy(out=hsb[0:1, 0:512],
                                                      in_=pb[0:1, :]), inc="dv")
        guard[("hs",)] = p.semv["dv"]
        banks = [6, 7, 0, 1]
        for og in range(4):
            for b in banks:
                if ("evac", b) in guard:
                    p.wait("tensor", "ev", guard[("evac", b)])
            p.wait("tensor", "dv", guard[("hs",)])
            for ic in range(NIC):
                sl = ic % 2
                if ("wd8", sl) in guard:
                    p.wait("sync", "cv", guard[("wd8", sl)])
                p.dma("sync", wd8[sl][:],
                      dwT[ic * 128:(ic + 1) * 128, og * 512:(og + 1) * 512], f"wde{sl}")
                p.wait("gpsimd", f"wde{sl}")
                if ("wdb", sl) in guard:
                    p.wait("gpsimd", "pg", guard[("wdb", sl)])
                p.op("gpsimd", lambda h, sl=sl: h.tensor_copy(out=wdb[sl], in_=wd8[sl][:]),
                     inc="cv")
                guard[("wd8", sl)] = p.semv["cv"]
                p.wait("tensor", "cv")
                for j in range(4):
                    p.op("tensor", lambda h, b=banks[j], j=j, sl=sl, ic=ic: h.matmul(
                        P[b][:, 0:512], wdb[sl][:, j * 128:(j + 1) * 128],
                        hb[:, ic * 512:(ic + 1) * 512],
                        start=(ic == 0), stop=False),
                        inc="pg" if j == 3 else None)
                guard[("wdb", sl)] = p.semv["pg"]
            for j in range(4):
                p.op("tensor", lambda h, b=banks[j]: h.matmul(
                    P[b][:, 0:512], m128[0:1, :], hsb[0:1, 0:512],
                    start=False, stop=True), inc="pg" if j == 3 else None)
            chains_done = p.semv["pg"]
            for j in range(4):
                oc = og * 4 + j
                xs = slice((j % 2) * 512, (j % 2) * 512 + 512)
                if ("xq", j % 2) in guard:
                    p.wait("sync", "ev", guard[("xq", j % 2)])
                p.dma("sync", xq[:, xs], xT[oc * 128:(oc + 1) * 128, t0:t0 + 512],
                      f"ld{j % 2}")
                p.wait("vector", "pg", chains_done)
                p.wait("vector", f"ld{j % 2}")
                p.wait("vector", f"yo{j % 2}")
                p.op("vector", lambda h, b=banks[j], xs=xs, oc=oc: h.scalar_tensor_tensor(
                    out=oute[:, xs], in0=P[b][:, 0:512], scalar=dcolh[:, oc:oc + 1],
                    in1=xq[:, xs], op0=ALU.mult, op1=ALU.add), inc="ev")
                guard[("evac", banks[j])] = p.semv["ev"]
                guard[("xq", j % 2)] = p.semv["ev"]
                p.wait("gpsimd", "ev")
                p.dma("gpsimd", yT[oc * 128:(oc + 1) * 128, t0:t0 + 512],
                      oute[:, xs], f"yo{j % 2}")
    p.wait("gpsimd", "yo0")
    p.wait("gpsimd", "yo1")
    return p.build()


def stage_l1ffn_inputs(x_slice_T, gate_w, gate_scale, up_w, up_scale, down_w, down_scale):
    return {
        "xT": np.ascontiguousarray(x_slice_T, dtype=np.float32),
        "gwT": np.ascontiguousarray(np.asarray(gate_w).T).astype(np.uint8),
        "uwT": np.ascontiguousarray(np.asarray(up_w).T).astype(np.uint8),
        "dwT": np.ascontiguousarray(np.asarray(down_w).T).astype(np.uint8),
        "gsc": np.ascontiguousarray(np.asarray(gate_scale), dtype=np.float32).reshape(-1, 1),
        "usc": np.ascontiguousarray(np.asarray(up_scale), dtype=np.float32).reshape(-1, 1),
        "dsc": np.ascontiguousarray(np.asarray(down_scale), dtype=np.float32).reshape(-1, 1),
    }



def build_attn_front(TT=1024):
    D, NCH = 2048, 16
    p = Prog("afront")
    nc = p.nc
    xT = nc.dram_tensor("xT", [D, TT], F32, kind="ExternalInput").ap()
    wqkvT = nc.dram_tensor("wqkvT", [D, 3 * D], F32, kind="ExternalInput").ap()
    qnw = nc.dram_tensor("qnw", [D, 1], F32, kind="ExternalInput").ap()
    knw = nc.dram_tensor("knw", [D, 1], F32, kind="ExternalInput").ap()
    cosb = nc.dram_tensor("cosb", [128, TT], F32, kind="ExternalInput").ap()
    sinb = nc.dram_tensor("sinb", [128, TT], F32, kind="ExternalInput").ap()
    permT = nc.dram_tensor("permT", [128, 128], BF16, kind="ExternalInput").ap()
    qT = nc.dram_tensor("qT", [D, TT], BF16, kind="ExternalOutput").ap()
    kT = nc.dram_tensor("kT", [D, TT], BF16, kind="ExternalOutput").ap()
    vT = nc.dram_tensor("vT", [D, TT], BF16, kind="ExternalOutput").ap()

    for sn in ("ld0", "ld1", "ldc", "wd0", "wd1", "cv", "dv", "pe", "sc", "ev",
               "pg", "yo0", "yo1", "yo2", "yo3"):
        p.sem(sn)

    s = NS()
    AB = 58 * 1024
    s.A = p.sbuf("A", [128, AB], BF16)
    s.xb = s.A[:, 0:NCH * TT]
    s.xn = s.A[:, NCH * TT:2 * NCH * TT]
    q2 = s.A[:, 2 * NCH * TT: 2 * NCH * TT + TT]
    qnb = s.A[:, 2 * NCH * TT + TT: 2 * NCH * TT + 2 * TT]
    o_q = 2 * NCH * TT + 2 * TT
    qf = s.A[:, o_q: o_q + NCH * TT]
    o_w = o_q + NCH * TT
    wslab_b = [s.A[:, o_w + k * 2048: o_w + (k + 1) * 2048] for k in range(2)]
    outb = [s.A[:, o_w + 2 * 2048 + k * 512: o_w + 2 * 2048 + (k + 1) * 512]
            for k in range(4)]
    permb = s.A[:, o_w + 2 * 2048 + 4 * 512: o_w + 2 * 2048 + 4 * 512 + 128]
    assert o_w + 2 * 2048 + 4 * 512 + 128 <= AB
    wslab_f = [p.sbuf(f"wsf{k}", [128, 2048], F32) for k in range(2)]
    s.xtmp = p.sbuf("xtmp", [128, 2 * TT], F32)
    s.ones_l = p.sbuf("ones_l", [128, 1], BF16)
    s.ones_c = p.sbuf("ones_c", [1, 128], F32)
    s.epsc = p.sbuf("epsc", [128, 1], F32)
    s.eps6 = p.sbuf("eps6", [128, 1], F32)
    s.r0 = p.sbuf("r0", [1, TT], F32)
    s.r1 = p.sbuf("r1", [1, TT], F32)
    s.r2 = p.sbuf("r2", [1, TT], F32)
    s.r3 = p.sbuf("r3", [1, TT], F32)
    s.mb = p.sbuf("mb", [128, TT], F32)
    s.rb = p.sbuf("rb", [128, TT], F32)
    cosf = p.sbuf("cosf", [128, TT], F32)
    sinf = p.sbuf("sinf", [128, TT], F32)
    nwcol = p.sbuf("nwcol", [128, 2 * NCH], F32)
    t1 = p.sbuf("t1", [128, TT], F32)
    t2 = p.sbuf("t2", [128, TT], F32)
    qsw = p.sbuf("qsw", [128, TT], F32)
    P = [p.psum(f"P{i}", [128, 512]) for i in range(8)]
    pa, pb = P[0], P[1]

    p.op("vector", lambda h: h.memset(s.ones_l[:], 1.0))
    p.op("vector", lambda h: h.memset(s.epsc[:], 1e-5))
    p.op("vector", lambda h: h.memset(s.eps6[:], 1e-6))
    p.op("vector", lambda h: h.memset(s.ones_c[:], 1.0), inc="dv")
    p.dma("sync", cosf[:], cosb, "ldc")
    p.dma("sync", sinf[:], sinb, "ldc")
    p.dma("sync", permb, permT, "ldc")
    p.dma("sync", nwcol[:, 0:NCH], qnw.rearrange("(k p) f -> p (k f)", p=128),
          "ldc", slow=True)
    p.dma("sync", nwcol[:, NCH:2 * NCH], knw.rearrange("(k p) f -> p (k f)", p=128),
          "ldc", slow=True)
    p.wait("vector", "ldc")
    p.wait("tensor", "ldc")
    emit_load_x_as_bf16(p, s, xT, TT, NCH)
    emit_ln(p, s, pa, pb, TT, NCH)

    guard = {}
    def qkv_chunk(oc, bk_pair, out_cb):
        sl = oc % 2
        if ("w", sl) in guard:
            p.wait("sync", "cv", guard[("w", sl)])
        p.dma("sync", wslab_f[sl].rearrange("p (k i) -> p k i", k=16),
              wqkvT[:, oc * 128:(oc + 1) * 128].rearrange("(k p) i -> p k i", p=128),
              f"wd{sl}")
        p.wait("vector", f"wd{sl}")
        if ("wb", sl) in guard:
            p.wait("vector", "pg", guard[("wb", sl)])
        p.op("vector", lambda h, sl=sl: h.tensor_copy(out=wslab_b[sl], in_=wslab_f[sl][:]),
             inc="cv")
        guard[("w", sl)] = p.semv["cv"]
        p.wait("tensor", "cv")
        p.wait("tensor", "dv")
        for th in range(2):
            bk = bk_pair[th]
            if ("evac", bk) in guard:
                p.wait("tensor", "ev", guard[("evac", bk)])
            for k in range(NCH):
                p.op("tensor", lambda h, k=k, sl=sl, bk=bk, th=th,
                     st=(k == 0), sp=(k == NCH - 1): h.matmul(
                         P[bk][:, 0:512], wslab_b[sl][:, k * 128:(k + 1) * 128],
                         s.xn[:, k * TT + th * 512: k * TT + th * 512 + 512],
                         start=st, stop=sp), inc="pg" if k == NCH - 1 else None)
        guard[("wb", sl)] = p.semv["pg"]
        for th in range(2):
            out_cb(bk_pair[th], th)

    for oc in range(32, 48):
        def v_out(bk, th, oc=oc):
            slot = 2 * (oc % 2) + th
            p.wait("scalar", "pg")
            p.wait("scalar", f"yo{slot}")
            p.op("scalar", lambda h, bk=bk, slot=slot: h.copy(
                out=outb[slot], in_=P[bk][:, 0:512]), inc="ev")
            guard[("evac", bk)] = p.semv["ev"]
            p.wait("gpsimd", "ev")
            p.dma("gpsimd", vT[(oc - 32) * 128:(oc - 31) * 128, th * 512:th * 512 + 512],
                  outb[slot], f"yo{slot}")
        qkv_chunk(oc, (2 + 2 * (oc % 2), 3 + 2 * (oc % 2) - 1 + 1), v_out)

    p.wait("vector", "yo2")
    p.wait("vector", "yo3")
    for which, base_oc, wofs, outT in (("q", 0, 0, qT), ("k", 16, NCH, kT)):
        for oc in range(base_oc, base_oc + 16):
            def f_out(bk, th, oc=oc):
                c = oc - base_oc
                p.wait("vector", "pg")
                p.op("vector", lambda h, bk=bk, th=th, c=c: h.tensor_copy(
                    out=qf[:, c * TT + th * 512: c * TT + th * 512 + 512],
                    in_=P[bk][:, 0:512]), inc="ev")
                guard[("evac", bk)] = p.semv["ev"]
            qkv_chunk(oc, (2 + 2 * (oc % 2), 3 + 2 * (oc % 2) - 1 + 1), f_out)
        for th in range(2):
            p.wait("vector", "ev")
            for c in range(NCH):
                sl = slice(c * TT + th * 512, c * TT + th * 512 + 512)
                q2s = slice((c % 2) * 512, (c % 2) * 512 + 512)
                if c >= 2:
                    p.wait("vector", "pe", guard[("q2", c % 2)])
                p.op("vector", lambda h, sl=sl, q2s=q2s: h.tensor_tensor(
                    out=q2[:, q2s], in0=qf[:, sl], in1=qf[:, sl], op=ALU.mult),
                    inc="dv")
                p.wait("tensor", "dv")
                p.op("tensor", lambda h, q2s=q2s, st=(c == 0), sp=(c == NCH - 1): h.matmul(
                    pa[0:1, :], s.ones_l[:, 0:1], q2[:, q2s],
                    start=st, stop=sp), inc="pe")
                guard[("q2", c % 2)] = p.semv["pe"]
            p.wait("vector", "pe")
            p.op("vector", lambda h, th=th: h.tensor_scalar(
                out=s.r1[0:1, th * 512:(th + 1) * 512], in0=pa[0:1, :],
                scalar1=1.0 / D, scalar2=None, op0=ALU.mult), inc="dv")
        p.wait("scalar", "dv")
        p.op("scalar", lambda h: h.activation(out=s.r2[0:1, 0:TT], in_=s.r1[0:1, 0:TT],
                                              func=AF.Ln, bias=s.eps6[0:1, 0:1]))
        p.op("scalar", lambda h: h.activation(out=s.r3[0:1, 0:TT], in_=s.r2[0:1, 0:TT],
                                              func=AF.Exp, scale=-0.5), inc="sc")
        for th in range(2):
            p.wait("tensor", "sc")
            p.wait("tensor", "dv")
            p.op("tensor", lambda h, th=th: h.matmul(
                pb[:, 0:512], s.ones_c[0:1, :], s.r3[0:1, th * 512:(th + 1) * 512],
                start=True, stop=True), inc="pe")
            p.wait("vector", "pe")
            p.op("vector", lambda h, th=th: h.tensor_copy(
                out=s.rb[:, th * 512:(th + 1) * 512], in_=pb[:, 0:512]), inc="dv")
        for c in range(NCH):
            csl = slice(c * TT, (c + 1) * TT)
            p.op("vector", lambda h, csl=csl, c=c, wofs=wofs: h.scalar_tensor_tensor(
                out=qf[:, csl], in0=qf[:, csl], scalar=nwcol[:, wofs + c: wofs + c + 1],
                in1=s.rb[:, 0:TT], op0=ALU.mult, op1=ALU.mult))
            p.op("vector", lambda h, csl=csl: h.tensor_copy(out=qnb, in_=qf[:, csl]),
                 inc="dv")
            for th in range(2):
                p.wait("tensor", "dv")
                if ("psw", th) in guard:
                    p.wait("tensor", "ev", guard[("psw", th)])
                p.op("tensor", lambda h, th=th: h.matmul(
                    P[6 + th][:, 0:512], permb, qnb[th * 512:(th + 1) * 512]
                    if False else qnb[:, th * 512:(th + 1) * 512],
                    start=True, stop=True), inc="pe")
                p.wait("vector", "pe")
                p.op("vector", lambda h, th=th: h.tensor_copy(
                    out=qsw[:, th * 512:(th + 1) * 512], in_=P[6 + th][:, 0:512]),
                    inc="ev")
                guard[("psw", th)] = p.semv["ev"]
            p.op("vector", lambda h, csl=csl: h.tensor_tensor(
                out=t1[:], in0=qf[:, csl], in1=cosf[:], op=ALU.mult))
            p.op("vector", lambda h: h.tensor_tensor(
                out=t2[:], in0=qsw[:], in1=sinf[:], op=ALU.mult))
            slot = c % 2
            p.wait("vector", f"yo{slot}")
            p.op("vector", lambda h, slot=slot: h.tensor_tensor(
                out=outb[slot], in_=None, in0=t1[:, 0:512], in1=t2[:, 0:512],
                op=ALU.add) if False else h.tensor_tensor(
                out=outb[slot], in0=t1[:, 0:512], in1=t2[:, 0:512], op=ALU.add))
            p.op("vector", lambda h, slot=slot: h.tensor_tensor(
                out=outb[2 + slot], in0=t1[:, 512:1024], in1=t2[:, 512:1024],
                op=ALU.add), inc="dv")
            p.wait("gpsimd", "dv")
            p.dma("gpsimd", outT[c * 128:(c + 1) * 128, 0:512], outb[slot], f"yo{slot}")
            p.dma("gpsimd", outT[c * 128:(c + 1) * 128, 512:1024], outb[2 + slot],
                  f"yo{slot}")
    for sn in ("yo0", "yo1", "yo2", "yo3"):
        p.wait("gpsimd", sn)
    return p.build()



def build_attn_core(NH=2, T=8192, W=1024):
    KB = T + W
    NQ = T // 128
    p = Prog("acore")
    nc = p.nc
    qh = nc.dram_tensor("qh", [NH * 128, T], BF16, kind="ExternalInput").ap()
    kh = nc.dram_tensor("kh", [NH * 128, T], BF16, kind="ExternalInput").ap()
    vh = nc.dram_tensor("vh", [NH * 128, T], BF16, kind="ExternalInput").ap()
    maskb = nc.dram_tensor("maskb", [128, 9 * 1152], F32, kind="ExternalInput").ap()
    ident = nc.dram_tensor("ident", [128, 128], BF16, kind="ExternalInput").ap()
    oh = nc.dram_tensor("oh", [NH * 128, T], BF16, kind="ExternalOutput").ap()

    for sn in ("ld", "dv", "pe", "sc", "ev", "pg", "yo0", "yo1"):
        p.sem(sn)

    qb_t = p.sbuf("qb_t", [128, T], BF16)
    kbuf = p.sbuf("kbuf", [128, KB], BF16)
    vbuf = p.sbuf("vbuf", [128, KB], BF16)
    VT = p.sbuf("VT", [128, KB], BF16)
    maskt = p.sbuf("maskt", [128, 9 * 1152], F32)
    identt = p.sbuf("identt", [128, 128], BF16)
    S = p.sbuf("S", [128, 1152], F32)
    Pb = p.sbuf("Pb", [128, 1152], BF16)
    PT = p.sbuf("PT", [128, 9 * 128], BF16)
    mx = p.sbuf("mx", [128, 4], F32)
    ob = p.sbuf("ob", [128, 2 * 128], BF16)
    scol = p.sbuf("scol", [128, 1], F32)
    P = [p.psum(f"P{i}", [128, 512]) for i in range(5)]
    ptr = p.psum("ptr", [128, 128], BF16)
    pvt = p.psum("pvt", [128, 128], BF16)

    p.dma("sync", maskt[:], maskb, "ld")
    p.dma("sync", identt[:], ident, "ld")
    p.op("vector", lambda h: h.memset(scol[:], 2 ** -3.5), inc="dv")

    guard = {}
    for head in range(NH):
        if head > 0:
            p.wait("sync", "pg")
            p.wait("sync", "ev")
        r0 = head * 128
        p.dma("sync", qb_t[:], qh[r0:r0 + 128, :], "ld")
        p.dma("sync", kbuf[:, W:KB], kh[r0:r0 + 128, :], "ld")
        p.dma("sync", vbuf[:, W:KB], vh[r0:r0 + 128, :], "ld")
        p.wait("vector", "ld")
        if head == 0:
            p.op("vector", lambda h: h.memset(kbuf[:, 0:W], 0.0))
            p.op("vector", lambda h: h.memset(vbuf[:, 0:W], 0.0), inc="dv")
        for kc in range(KB // 128):
            if ("vt",) in guard:
                p.wait("tensor", "ev", guard[("vt",)])
            p.wait("tensor", "ld")
            p.wait("tensor", "dv")
            p.op("tensor", lambda h, kc=kc: h.transpose(
                pvt[:], vbuf[:, kc * 128:(kc + 1) * 128], identt[:]), inc="pe")
            p.wait("vector", "pe")
            p.op("vector", lambda h, kc=kc: h.tensor_copy(
                out=VT[:, kc * 128:(kc + 1) * 128], in_=pvt[:]), inc="ev")
            guard[("vt",)] = p.semv["ev"]
        for qb in range(NQ):
            var = min(qb, 8)
            k0 = qb * 128
            segs = [(0, 512, 0), (512, 512, 1), (1024, 128, 2)]
            for (o, n, bk) in segs:
                if ("sb", bk) in guard:
                    p.wait("tensor", "ev", guard[("sb", bk)])
                p.op("tensor", lambda h, o=o, n=n, bk=bk, k0=k0, qb=qb: h.matmul(
                    P[bk][:, 0:n], qb_t[:, qb * 128:(qb + 1) * 128],
                    kbuf[:, k0 + o: k0 + o + n], start=True, stop=True),
                    inc="pg" if bk == 2 else None)
            p.wait("vector", "pg")
            for (o, n, bk) in segs:
                p.op("vector", lambda h, o=o, n=n, bk=bk, var=var: h.scalar_tensor_tensor(
                    out=S[:, o:o + n], in0=P[bk][:, 0:n], scalar=scol[:, 0:1],
                    in1=maskt[:, var * 1152 + o: var * 1152 + o + n],
                    op0=ALU.mult, op1=ALU.add), inc="ev" if bk == 2 else None)
            guard[("sb", 0)] = guard[("sb", 1)] = guard[("sb", 2)] = p.semv["ev"]
            p.op("vector", lambda h: h.reduce_max(out=mx[:, 0:1], in_=S[:], axis=AX.X))
            p.op("vector", lambda h: h.tensor_scalar(
                out=mx[:, 1:2], in0=mx[:, 0:1], scalar1=-1.0, scalar2=None,
                op0=ALU.mult), inc="dv")
            p.wait("scalar", "dv")
            p.op("scalar", lambda h: h.activation(
                out=Pb[:], in_=S[:], func=AF.Exp, bias=mx[:, 1:2]), inc="sc")
            p.wait("vector", "sc")
            p.op("vector", lambda h: h.reduce_sum(out=mx[:, 2:3], in_=Pb[:], axis=AX.X))
            p.op("vector", lambda h: h.reciprocal(out=mx[:, 3:4], in_=mx[:, 2:3]))
            p.op("vector", lambda h: h.tensor_scalar(
                out=Pb[:], in0=Pb[:], scalar1=mx[:, 3:4], scalar2=None,
                op0=ALU.mult), inc="dv")
            for kc in range(9):
                p.wait("tensor", "dv")
                if ("pt",) in guard:
                    p.wait("tensor", "ev", guard[("pt",)])
                p.op("tensor", lambda h, kc=kc: h.transpose(
                    ptr[:], Pb[:, kc * 128:(kc + 1) * 128], identt[:]), inc="pe")
                p.wait("vector", "pe")
                p.op("vector", lambda h, kc=kc: h.tensor_copy(
                    out=PT[:, kc * 128:(kc + 1) * 128], in_=ptr[:]), inc="ev")
                guard[("pt",)] = p.semv["ev"]
            if ("ob",) in guard:
                p.wait("tensor", "ev", guard[("ob",)])
            p.wait("tensor", "ev", guard[("pt",)])
            for kc in range(9):
                p.op("tensor", lambda h, kc=kc, qb=qb: h.matmul(
                    P[4][:, 0:128], VT[:, (qb + kc) * 128:(qb + kc + 1) * 128],
                    PT[:, kc * 128:(kc + 1) * 128],
                    start=(kc == 0), stop=(kc == 8)), inc="pg" if kc == 8 else None)
            sl = qb % 2
            p.wait("vector", "pg")
            p.wait("vector", f"yo{sl}")
            p.op("vector", lambda h, sl=sl: h.tensor_copy(
                out=ob[:, sl * 128:(sl + 1) * 128], in_=P[4][:, 0:128]), inc="ev")
            guard[("ob",)] = p.semv["ev"]
            p.wait("gpsimd", "ev")
            p.dma("gpsimd", oh[r0:r0 + 128, qb * 128:(qb + 1) * 128],
                  ob[:, sl * 128:(sl + 1) * 128], f"yo{sl}")
    p.wait("gpsimd", "yo0")
    p.wait("gpsimd", "yo1")
    return p.build()


def make_attn_masks():
    m = np.full((9, 128, 1152), -1e30, dtype=np.float32)
    for v in range(9):
        for ql in range(128):
            lo = ql + 1
            if v < 8:
                lo = max(lo, 1024 - v * 128)
            m[v, ql, lo:ql + 1025] = 0.0
    return np.ascontiguousarray(m.transpose(1, 0, 2).reshape(128, 9 * 1152))


def make_rope_tables(t0, TT):
    pos = np.arange(t0, t0 + TT, dtype=np.float64)
    inv = 1.0 / (10000.0 ** (np.arange(0, 128, 2, dtype=np.float64) / 128.0))
    fr = pos[None, :] * inv[:, None]
    cos = np.cos(fr).astype(np.float32)
    sin = np.sin(fr).astype(np.float32)
    cosb = np.concatenate([cos, cos], axis=0)
    sinb = np.concatenate([sin, sin], axis=0)
    return np.ascontiguousarray(cosb), np.ascontiguousarray(sinb)


def make_perm():
    Pm = np.zeros((128, 128), dtype=np.float32)
    for d in range(64):
        Pm[d, 64 + d] = -1.0
        Pm[64 + d, d] = 1.0
    return np_bf16(Pm.T)

"""Single-launch megakernel for mk_macaron_d2 (token-sharded, x resident in SBUF).

Exchanges (gpsimd collective AllGather on internal DRAM):
  AG-W: ghost bf16 weight shares (each core dequants 1/8, straightened to WgS/WuS/WdS)
  AG-T: mamba conv h_in 4-token tails (neighbor selected by per-core mask)
  AG-S: mamba scan chunk summaries (P_chunk=exp(A*sum dt), h_end)
  AG-KV: roped K | V (attention halo; neighbor slice read via register-offset DMA)
Per-core data keeps the program uniform: rope tables (absolute positions), band
masks (variant pre-selected per core), rank prefix masks, neighbor masks/offsets.
"""
import numpy as np
import concourse.bass as bass
import concourse.bass as bass

D, TT, NCH = 2048, 1024, 16
I5, IC5 = 5504, 43
I4, I8 = 4096, 8192

DMA_SEMS = ("ld0", "ld1", "ldc", "wd0", "wd1", "wdc0", "wdc1", "wde0", "wde1",
            "so0", "so1", "yo0", "yo1", "st0", "st1")


def build_mega(phases=("g1", "mamba", "g2", "f1", "attn", "f2"), dbg=None):
    p = Prog("mega")
    nc = p.nc

    xT = nc.dram_tensor("xT", [D, TT], F32, kind="ExternalInput").ap()
    yT = nc.dram_tensor("yT", [D, TT], F32, kind="ExternalOutput").ap()
    gidxT_s = nc.dram_tensor("gidxT_s", [256, I5], U8, kind="ExternalInput").ap()
    uidxT_s = nc.dram_tensor("uidxT_s", [256, I5], U8, kind="ExternalInput").ap()
    didxT_s = nc.dram_tensor("didxT_s", [I5, 256], U8, kind="ExternalInput").ap()
    lut = nc.dram_tensor("lut", [1, 16], F32, kind="ExternalInput").ap()
    wipT = nc.dram_tensor("wipT", [D, 2 * I4], F32, kind="ExternalInput").ap()
    convw = nc.dram_tensor("convw", [I4, 4], F32, kind="ExternalInput").ap()
    convb = nc.dram_tensor("convb", [I4, 1], F32, kind="ExternalInput").ap()
    xpT = nc.dram_tensor("xpT", [I4, 160], F32, kind="ExternalInput").ap()
    dtpT = nc.dram_tensor("dtpT", [128, I4], F32, kind="ExternalInput").ap()
    dtb = nc.dram_tensor("dtb", [I4, 1], F32, kind="ExternalInput").ap()
    Alog = nc.dram_tensor("Alog", [I4, 16], F32, kind="ExternalInput").ap()
    Dvec = nc.dram_tensor("Dvec", [I4, 1], F32, kind="ExternalInput").ap()
    wopT = nc.dram_tensor("wopT", [I4, D], F32, kind="ExternalInput").ap()
    sel = nc.dram_tensor("sel", [64, 16 * 128], BF16, kind="ExternalInput").ap()
    rmask = nc.dram_tensor("rmask", [128, 8], F32, kind="ExternalInput").ap()
    nbrm = nc.dram_tensor("nbrm", [128, 8], F32, kind="ExternalInput").ap()
    import concourse.mybir as mybir
    nbro = nc.dram_tensor("nbro", [128, 32], mybir.dt.uint32, kind="ExternalInput").ap()
    gwT = nc.dram_tensor("gwT", [D, I8], U8, kind="ExternalInput").ap()
    uwT = nc.dram_tensor("uwT", [D, I8], U8, kind="ExternalInput").ap()
    dwT = nc.dram_tensor("dwT", [I8, D], U8, kind="ExternalInput").ap()
    gsc = nc.dram_tensor("gsc", [I8, 1], F32, kind="ExternalInput").ap()
    usc = nc.dram_tensor("usc", [I8, 1], F32, kind="ExternalInput").ap()
    dsc = nc.dram_tensor("dsc", [D, 1], F32, kind="ExternalInput").ap()
    wqkvT = nc.dram_tensor("wqkvT", [D, 3 * D], F32, kind="ExternalInput").ap()
    qnw = nc.dram_tensor("qnw", [D, 1], F32, kind="ExternalInput").ap()
    knw = nc.dram_tensor("knw", [D, 1], F32, kind="ExternalInput").ap()
    woT = nc.dram_tensor("woT", [D, D], F32, kind="ExternalInput").ap()
    cosb = nc.dram_tensor("cosb", [128, TT], F32, kind="ExternalInput").ap()
    sinb = nc.dram_tensor("sinb", [128, TT], F32, kind="ExternalInput").ap()
    permT = nc.dram_tensor("permT", [128, 128], BF16, kind="ExternalInput").ap()
    maskb = nc.dram_tensor("maskb", [128, 8 * 1152], BF16, kind="ExternalInput").ap()
    identb_i = nc.dram_tensor("identb_i", [128, 128], BF16, kind="ExternalInput").ap()
    hflag = nc.dram_tensor("hflag", [128, 1], F32, kind="ExternalInput").ap()

    if dbg == "gw":
        dbg_wg = nc.dram_tensor("dbg_wg", [D, 512], BF16, kind="ExternalOutput").ap()
        dbg_wd = nc.dram_tensor("dbg_wd", [512, D], BF16, kind="ExternalOutput").ap()
    if dbg == "xn":
        dbg_xn = nc.dram_tensor("dbg_xn", [128, 16 * TT], BF16, kind="ExternalOutput").ap()
    if dbg == "mamba":
        dbg_u = nc.dram_tensor("dbg_u", [I4, TT], BF16, kind="ExternalOutput").ap()
        dbg_sg = nc.dram_tensor("dbg_sg", [I4, TT], BF16, kind="ExternalOutput").ap()
        dbg_yg = nc.dram_tensor("dbg_yg", [I4, TT], BF16, kind="ExternalOutput").ap()
        dbg_bc = nc.dram_tensor("dbg_bc", [64, TT], BF16, kind="ExternalOutput").ap()
        dbg_ts = nc.dram_tensor("dbg_ts", [128, TT], BF16, kind="ExternalOutput").ap()
        dbg_xp = nc.dram_tensor("dbg_xp", [128, 5120], BF16, kind="ExternalOutput").ap()
    if dbg == "xr":
        dbg_x1 = nc.dram_tensor("dbg_x1", [128, 16 * TT], F32, kind="ExternalOutput").ap()
        dbg_x2 = nc.dram_tensor("dbg_x2", [128, 16 * TT], F32, kind="ExternalOutput").ap()
        dbg_x3 = nc.dram_tensor("dbg_x3", [128, 16 * TT], F32, kind="ExternalOutput").ap()
    if dbg == "at":
        dbg_q = nc.dram_tensor("dbg_q", [D, TT], BF16, kind="ExternalOutput").ap()
        dbg_k = nc.dram_tensor("dbg_k", [D, TT], BF16, kind="ExternalOutput").ap()
        dbg_v = nc.dram_tensor("dbg_v", [D, TT], BF16, kind="ExternalOutput").ap()
        dbg_ao = nc.dram_tensor("dbg_ao", [128, 16 * TT], BF16, kind="ExternalOutput").ap()
    if dbg == "l1":
        dbg_h1 = nc.dram_tensor("dbg_h1", [I8, TT], BF16, kind="ExternalOutput").ap()
        dbg_xs = nc.dram_tensor("dbg_xs", [1, TT], F32, kind="ExternalOutput").ap()
    if dbg == "hb":
        dbg_hb = nc.dram_tensor("dbg_hb", [128, IC5 * 512], BF16, kind="ExternalOutput").ap()
    NGW = 2 * 256 * I5 + I5 * 256
    g_cc_in = nc.dram_tensor("g_cc_in", [NGW], BF16).ap()
    g_cc_out = nc.dram_tensor("g_cc_out", [8 * NGW], BF16).ap()
    WgS = nc.dram_tensor("WgS", [D, I5], BF16).ap()
    WuS = nc.dram_tensor("WuS", [D, I5], BF16).ap()
    WdS = nc.dram_tensor("WdS", [I5, D], BF16).ap()
    t_cc_in = nc.dram_tensor("t_cc_in", [I4, 4], BF16).ap()
    t_cc_out = nc.dram_tensor("t_cc_out", [8 * I4, 4], BF16).ap()
    s_cc_in = nc.dram_tensor("s_cc_in", [I4, 32], F32).ap()
    s_cc_out = nc.dram_tensor("s_cc_out", [8 * I4, 32], F32).ap()
    kv_cc_in = nc.dram_tensor("kv_cc_in", [2 * D, TT], BF16).ap()
    kv_cc_out_t = nc.dram_tensor("kv_cc_out", [8 * 2 * D, TT], BF16)
    kv_cc_out = kv_cc_out_t.ap()
    u_sp = nc.dram_tensor("u_sp", [I4, TT], BF16).ap()
    sg_sp = nc.dram_tensor("sg_sp", [I4, TT], BF16).ap()
    y1_sp = nc.dram_tensor("y1_sp", [I4, TT], F32).ap()
    yg_sp = nc.dram_tensor("yg_sp", [I4, TT], BF16).ap()
    kraw_sp = nc.dram_tensor("kraw_sp", [D, TT], BF16).ap()
    qraw_sp = nc.dram_tensor("qraw_sp", [D, TT], BF16).ap()
    q_sp = nc.dram_tensor("q_sp", [D, TT], BF16).ap()
    h_sp = nc.dram_tensor("h_sp", [I8, TT], BF16).ap()

    for sn in DMA_SEMS:
        p.sem(sn)
    for sn in ("cv", "dq", "pg", "ev", "dv", "pe", "sc", "cc", "hs"):
        p.sem(sn)

    s = NS()
    XR = p.sbuf("XR", [128, NCH * TT], F32)
    AE = 47 * 1024
    A = p.sbuf("A", [128, AE], BF16)
    FE = 8 * 1024
    Fa = p.sbuf("Fa", [128, FE], F32)
    s.xn = A[:, 0:16 * 1024]
    s.xb = A[:, 16 * 1024:32 * 1024]
    ARE = 32 * 1024
    s.mb = Fa[:, 0:TT]
    s.rb = Fa[:, TT:2 * TT]
    s.r0 = Fa[0:1, 2 * TT:3 * TT]
    s.r1 = Fa[0:1, 3 * TT:4 * TT]
    s.r2 = Fa[0:1, 4 * TT:5 * TT]
    s.r3 = Fa[0:1, 5 * TT:6 * TT]
    s.xtmp = Fa[:, 6 * TT:8 * TT]
    s.ones_l = p.sbuf("ones_l", [128, 1], BF16)
    s.ones_c = p.sbuf("ones_c", [1, 128], F32)
    s.epsc = p.sbuf("epsc", [128, 1], F32)
    s.eps6 = p.sbuf("eps6", [128, 1], F32)
    lutr = p.sbuf("lutr", [1, 16], F32)
    lut128 = p.sbuf("lut128", [128, 16], F32)
    m128 = p.sbuf("m128", [1, 128], BF16)
    gsil = A[:, 47616:48128]
    P = [p.psum(f"P{i}", [128, 512]) for i in range(7)]
    ptx = p.psum("ptx", [128, 1024], BF16)
    ptr = ptx[:, 0:128]
    pvt = ptx[:, 128:256]
    pa, pb = P[0], P[1]
    guard = {}

    def bankw(eng, *banks):
        for b in banks:
            if ("evac", b) in guard:
                p.wait(eng, "ev", guard[("evac", b)])
            if ("evacdv", b) in guard:
                p.wait(eng, "dv", guard[("evacdv", b)])

    def dma_barrier():
        for sn in DMA_SEMS:
            if p.semv[sn] > 0:
                for e_ in ("sync", "gpsimd", "vector", "scalar", "tensor"):
                    p.wait(e_, sn)
        for sn in ("pg", "ev", "dv", "sc", "cv"):
            if p.semv[sn] > 0:
                p.wait("sync", sn)
                p.wait("gpsimd", sn)

    def ln_into_xn(eps=1e-5):
        bankw("tensor", 0, 1)
        p.wait("vector", "ev")
        p.wait("vector", "cv")
        p.wait("vector", "dv")
        for sn_ in ("so0", "so1", "yo0", "yo1", "st0", "st1"):
            if p.semv[sn_] > 0:
                p.wait("vector", sn_)
        for c in range(NCH):
            slq = slice(c * TT, (c + 1) * TT)
            p.op("vector", lambda h, slq=slq: h.tensor_copy(out=s.xb[:, slq],
                                                            in_=XR[:, slq]),
                 inc="dv" if c == NCH - 1 else None)
        emit_ln(p, s, pa, pb, TT, NCH, eps=eps)
        guard[("evacdv", 0)] = guard[("evacdv", 1)] = p.semv["dv"]

    def residual_from_psum(bank, oc, t0, scale_col=None, imm=1.0):
        xs = slice(oc * TT + t0, oc * TT + t0 + 512)
        if scale_col is not None:
            p.op("vector", lambda h, bank=bank, xs=xs, sc=scale_col:
                 h.scalar_tensor_tensor(out=XR[:, xs], in0=P[bank][:, 0:512],
                                        scalar=sc, in1=XR[:, xs],
                                        op0=ALU.mult, op1=ALU.add), inc="ev")
        else:
            p.op("vector", lambda h, bank=bank, xs=xs, imm=imm:
                 h.scalar_tensor_tensor(out=XR[:, xs], in0=P[bank][:, 0:512],
                                        scalar=imm, in1=XR[:, xs],
                                        op0=ALU.mult, op1=ALU.add), inc="ev")
        guard[("evac", bank)] = p.semv["ev"]

    p.dma("sync", lutr[:], lut, "ldc")
    p.op("vector", lambda h: h.memset(s.ones_l[:], 1.0))
    p.op("vector", lambda h: h.memset(s.epsc[:], 1e-5))
    p.op("vector", lambda h: h.memset(s.eps6[:], 1e-6))
    p.op("vector", lambda h: h.memset(m128[:], -128.0))
    p.op("vector", lambda h: h.memset(s.ones_c[:], 1.0), inc="dv")
    p.wait("tensor", "ldc")
    p.wait("tensor", "dv")
    p.op("tensor", lambda h: h.matmul(pa[:, 0:16], s.ones_c[0:1, :], lutr[0:1, :],
                                      start=True, stop=True), inc="pe")
    p.wait("vector", "pe")
    p.op("vector", lambda h: h.tensor_copy(out=lut128[:], in_=pa[:, 0:16]), inc="dv")
    for c in range(NCH):
        slot = c % 2
        if c >= 2:
            p.wait("sync", "dv", guard[("xload", c - 2)])
        p.dma("sync", s.xtmp[:, slot * TT:(slot + 1) * TT],
              xT[c * 128:(c + 1) * 128, :], f"ld{slot}")
        p.wait("vector", f"ld{slot}")
        p.op("vector", lambda h, c=c, slot=slot: h.tensor_copy(
            out=XR[:, c * TT:(c + 1) * TT], in_=s.xtmp[:, slot * TT:(slot + 1) * TT]),
            inc="dv")
        guard[("xload", c)] = p.semv["dv"]

    if "g1" in phases or "g2" in phases:
        HW = 2752
        idx_u8 = p.sbuf("idx_u8", [128, 2 * HW], U8)
        idxb = [A[:, k * HW: (k + 1) * HW] for k in range(2)]
        accb = [A[:, (2 + k) * HW: (3 + k) * HW] for k in range(2)]
        tmpb = A[:, 4 * HW: 5 * HW]
        jobs = []
        gv = g_cc_in[0:256 * I5].rearrange("(q t) -> q t", q=256)
        uv = g_cc_in[256 * I5:2 * 256 * I5].rearrange("(q t) -> q t", q=256)
        dv_ = g_cc_in[2 * 256 * I5:NGW].rearrange("(q t) -> q t", q=I5)
        for i in range(2):
            for ch in range(2):
                jobs.append((gidxT_s[i * 128:(i + 1) * 128, ch * HW:(ch + 1) * HW], HW,
                             gv[i * 128:(i + 1) * 128, ch * HW:(ch + 1) * HW]))
        for i in range(2):
            for ch in range(2):
                jobs.append((uidxT_s[i * 128:(i + 1) * 128, ch * HW:(ch + 1) * HW], HW,
                             uv[i * 128:(i + 1) * 128, ch * HW:(ch + 1) * HW]))
        for i in range(IC5):
            jobs.append((didxT_s[i * 128:(i + 1) * 128, :], 256,
                         dv_[i * 128:(i + 1) * 128, :]))
        for sidx, (src, cols, dst) in enumerate(jobs):
            sl = sidx % 2
            if sidx >= 2:
                p.wait("sync", "dq", sidx - 1)
            p.dma("sync", idx_u8[:, sl * HW: sl * HW + cols], src, f"wdc{sl}")
            p.wait("gpsimd", f"wdc{sl}")
            p.op("gpsimd", lambda h, sl=sl, cols=cols: h.tensor_copy(
                out=idxb[sl][:, 0:cols], in_=idx_u8[:, sl * HW: sl * HW + cols]),
                inc="cv")
            p.wait("vector", "cv")
            p.wait("vector", "dv")
            if sidx >= 2:
                p.wait("vector", f"so{sl}")
            p.op("vector", lambda h, sl=sl, cols=cols: h.tensor_scalar(
                out=accb[sl][:, 0:cols], in0=idxb[sl][:, 0:cols], scalar1=0.0,
                scalar2=lut128[:, 0:1], op0=ALU.is_equal, op1=ALU.mult))
            for v in range(1, 16):
                p.op("vector", lambda h, sl=sl, cols=cols, v=v: h.tensor_scalar(
                    out=tmpb[:, 0:cols], in0=idxb[sl][:, 0:cols], scalar1=float(v),
                    scalar2=lut128[:, v:v + 1], op0=ALU.is_equal, op1=ALU.mult))
                p.op("vector", lambda h, sl=sl, cols=cols: h.tensor_tensor(
                    out=accb[sl][:, 0:cols], in0=accb[sl][:, 0:cols],
                    in1=tmpb[:, 0:cols], op=ALU.add), inc="dq" if v == 15 else None)
            p.wait("gpsimd", "dq")
            p.dma("gpsimd", dst, accb[sl][:, 0:cols], f"so{sl}")
        p.wait("gpsimd", "so0")
        p.wait("gpsimd", "so1")
        p.op("gpsimd", lambda h: h.collective_compute(
            "AllGather", ALU.bypass, replica_groups=[list(range(8))],
            ins=[g_cc_in[:]], outs=[g_cc_out[:]]), inc="cc")
        p.wait("gpsimd", "cc", p.semv["cc"])
        for r in range(8):
            base = r * NGW
            p.dma("gpsimd", WgS[r * 256:(r + 1) * 256, :],
                  g_cc_out[base: base + 256 * I5].rearrange("(q t) -> q t", q=256),
                  "st0")
            p.dma("gpsimd", WuS[r * 256:(r + 1) * 256, :],
                  g_cc_out[base + 256 * I5: base + 2 * 256 * I5].rearrange(
                      "(q t) -> q t", q=256), "st0")
            p.dma("gpsimd", WdS[:, r * 256:(r + 1) * 256],
                  g_cc_out[base + 2 * 256 * I5: base + NGW].rearrange(
                      "(q t) -> q t", q=I5), "st0")
        p.wait("sync", "st0")
        if dbg == "gw":
            p.wait("gpsimd", "st0")
            for cc in range(4):
                p.dma("gpsimd", dbg_wg[:, cc * 128:(cc + 1) * 128],
                      WgS[:, cc * 128:(cc + 1) * 128], "st1")
            p.dma("gpsimd", dbg_wd[:, :],
                  WdS[1000:1512, :], "st1")
            p.wait("gpsimd", "st1")

    def ghost_sublayer():
        dma_barrier()
        hb = A[:, 16 * 1024: 16 * 1024 + IC5 * 512]
        o_w = 16 * 1024 + IC5 * 512
        wg_l = [A[:, o_w + k * 2048: o_w + (k + 1) * 2048] for k in range(2)]
        wu_l = [A[:, o_w + (2 + k) * 2048: o_w + (3 + k) * 2048] for k in range(2)]
        o_wd = o_w + 4 * 2048
        wd_l = [A[:, o_wd + k * 512: o_wd + (k + 1) * 512] for k in range(2)]
        assert o_wd + 1024 <= AE
        ln_into_xn()
        if dbg == "xn":
            p.wait("gpsimd", "dv")
            p.dma("gpsimd", dbg_xn, s.xn, "st1")
        for th in range(2):
            t0 = th * 512
            if th == 1:
                p.wait("vector", "pg")
            for ic in range(IC5):
                sl = ic % 2
                if ("gwslot", sl) in guard:
                    p.wait("sync", "pg", guard[("gwslot", sl)])
                p.dma("sync", wg_l[sl].rearrange("p (k i) -> p k i", k=16),
                      WgS[:, ic * 128:(ic + 1) * 128].rearrange(
                          "(k p) i -> p k i", p=128), f"wd{sl}")
                p.dma("sync", wu_l[sl].rearrange("p (k i) -> p k i", k=16),
                      WuS[:, ic * 128:(ic + 1) * 128].rearrange(
                          "(k p) i -> p k i", p=128), f"wd{sl}")
                bg, bu = 2 + sl, 4 + sl
                p.wait("tensor", f"wd{sl}")
                p.wait("tensor", "dv")
                bankw("tensor", bg, bu)
                wl = {bg: wg_l[sl], bu: wu_l[sl]}
                for bk in (bg, bu):
                    for k in range(NCH):
                        p.op("tensor", lambda h, k=k, bk=bk, w=wl[bk], t0=t0,
                             st=(k == 0), sp=(k == NCH - 1): h.matmul(
                                 P[bk][:, 0:512], w[:, k * 128:(k + 1) * 128],
                                 s.xn[:, k * TT + t0: k * TT + t0 + 512],
                                 start=st, stop=sp),
                             inc="pg" if k == NCH - 1 else None)
                guard[("gwslot", sl)] = p.semv["pg"]
                p.wait("scalar", "pg")
                if "gsil" in guard:
                    p.wait("scalar", "ev", guard["gsil"])
                p.op("scalar", lambda h, bg=bg: h.activation(
                    out=gsil[:, 0:512], in_=P[bg][:, 0:512], func=AF.Silu), inc="sc")
                p.wait("vector", "sc")
                p.op("vector", lambda h, bu=bu, ic=ic: h.tensor_tensor(
                    out=hb[:, ic * 512:(ic + 1) * 512], in0=gsil[:, 0:512],
                    in1=P[bu][:, 0:512], op=ALU.mult), inc="ev")
                guard[("evac", bg)] = p.semv["ev"]
                guard[("evac", bu)] = p.semv["ev"]
                guard["gsil"] = p.semv["ev"]
            p.wait("tensor", "ev")
            if dbg == "hb" and th == 0:
                p.wait("gpsimd", "ev")
                p.dma("gpsimd", dbg_hb, hb, "st1")
            banks = [6, 0, 1, 2]
            for og in range(4):
                bankw("tensor", *banks)
                for ic in range(IC5):
                    sl = ic % 2
                    if ("gwdslot", sl) in guard:
                        p.wait("sync", "pg", guard[("gwdslot", sl)])
                    p.dma("sync", wd_l[sl],
                          WdS[ic * 128:(ic + 1) * 128, og * 512:(og + 1) * 512],
                          f"wde{sl}")
                    p.wait("tensor", f"wde{sl}")
                    for j in range(4):
                        p.op("tensor", lambda h, b=banks[j], j=j, sl=sl, ic=ic:
                             h.matmul(P[b][:, 0:512],
                                      wd_l[sl][:, j * 128:(j + 1) * 128],
                                      hb[:, ic * 512:(ic + 1) * 512],
                                      start=(ic == 0), stop=(ic == IC5 - 1)),
                             inc="pg" if j == 3 else None)
                    guard[("gwdslot", sl)] = p.semv["pg"]
                chains_done = p.semv["pg"]
                p.wait("vector", "pg", chains_done)
                for j in range(4):
                    residual_from_psum(banks[j], og * 4 + j, t0, imm=0.5)

    if "g1" in phases:
        ghost_sublayer()

    if "mamba" in phases:
        dma_barrier()
        dtbc = p.sbuf("dtbc", [128, 32], F32)
        convwc = p.sbuf("convwc", [128, 128], F32)
        convbc = p.sbuf("convbc", [128, 32], F32)
        Acol = p.sbuf("Acol", [128, 512], F32)
        Dcol = p.sbuf("Dcol", [128, 32], F32)
        rmaskc = p.sbuf("rmaskc", [128, 8], F32)
        nbrmc = p.sbuf("nbrmc", [128, 8], F32)
        selt = p.sbuf("selt", [64, 2048], BF16)
        bcrow_b = p.sbuf("bcrow_b", [64, TT], BF16)
        Bb_all = A[:, 0:16384]
        Cb_all = A[:, 16384:32768]
        wslab_b = [A[:, 32768 + k * 2048: 32768 + (k + 1) * 2048] for k in range(2)]
        u_sl = [A[:, 36864 + k * 1024: 36864 + (k + 1) * 1024] for k in range(2)]
        tsb = A[:, 38912:39936]
        xpb = A[:, 39936:45056]
        ntb = A[:, 45056:45184]
        tailb = A[:, 45184:45312]
        obf = [A[:, 45312 + k * 1024: 45312 + (k + 1) * 1024] for k in range(2)]
        dtsl_b = [A[:, 47360 + k * 128: 47360 + (k + 1) * 128] for k in range(2)]
        assert 47360 + 256 <= AE
        xpf = Fa[:, 0:5120]
        wf_sl = [Fa[:, 0 + k * 2048: (k + 1) * 2048] for k in range(2)]
        hpad = Fa[:, 4096:5123]
        acc = Fa[:, 5123:6147]
        dtq = Fa[:, 0:1024]
        dtu = Fa[:, 1024:2048]
        dAt = Fa[:, 2048:3072]
        dBxt = Fa[:, 3072:4096]
        hsct = Fa[:, 4096:5120]
        yvt = Fa[:, 5120:6144]
        zrt = Fa[:, 3072:4096]
        summ = Fa[:, 6144:7168]
        hinb = Fa[:, 7168:7680]
        accsave = Fa[:, 7680:7776]
        sdtc = Fa[:, 7776:7777]
        dtpf = [Fa[0:128, 7792 + k * 128: 7792 + (k + 1) * 128] for k in range(2)]
        ntf = Fa[:, 8048:8176]
        p.dma("sync", xpf.rearrange("p (k r) -> p k r", k=32),
              xpT.rearrange("(k p) r -> p k r", p=128), "ldc")
        p.dma("sync", dtbc[:], dtb.rearrange("(k p) f -> p (k f)", p=128), "ldc",
              slow=True)
        p.dma("sync", convwc.rearrange("p (k f) -> p k f", k=32),
              convw.rearrange("(k p) f -> p k f", p=128), "ldc")
        p.dma("sync", convbc[:], convb.rearrange("(k p) f -> p (k f)", p=128), "ldc",
              slow=True)
        p.dma("sync", Acol.rearrange("p (k n) -> p k n", k=32),
              Alog.rearrange("(k p) n -> p k n", p=128), "ldc")
        p.dma("sync", Dcol[:], Dvec.rearrange("(k p) f -> p (k f)", p=128), "ldc",
              slow=True)
        p.dma("sync", rmaskc[:], rmask, "ldc")
        p.dma("sync", nbrmc[:], nbrm, "ldc")
        p.dma("sync", selt[:], sel, "ldc")
        p.wait("vector", "ldc")
        p.wait("vector", "pg")
        p.op("vector", lambda h: h.tensor_copy(out=xpb, in_=xpf), inc="dv")
        p.wait("scalar", "ldc")
        p.op("scalar", lambda h: h.activation(out=Acol[:], in_=Acol[:], func=AF.Exp),
             inc="sc")
        p.wait("vector", "sc")
        p.op("vector", lambda h: h.tensor_scalar(
            out=Acol[:], in0=Acol[:], scalar1=-1.0, scalar2=None, op0=ALU.mult),
            inc="dv")
        guard[("acoln",)] = p.semv["dv"]
        ln_into_xn()
        p.wait("sync", "dv")
        for oc in range(64):
            sl = oc % 2
            if ("mws", sl) in guard:
                p.wait("sync", "cv", guard[("mws", sl)])
            p.dma("sync", wf_sl[sl].rearrange("p (k i) -> p k i", k=16),
                  wipT[:, oc * 128:(oc + 1) * 128].rearrange("(k p) i -> p k i", p=128),
                  f"wd{sl}")
            p.wait("gpsimd", f"wd{sl}")
            if ("mwb", sl) in guard:
                p.wait("gpsimd", "pg", guard[("mwb", sl)])
            p.op("gpsimd", lambda h, sl=sl: h.tensor_copy(out=wslab_b[sl],
                                                          in_=wf_sl[sl]), inc="cv")
            guard[("mws", sl)] = p.semv["cv"]
            p.wait("tensor", "cv")
            p.wait("tensor", "dv")
            for th in range(2):
                bk = 2 + 2 * sl + th
                bankw("tensor", bk)
                for k in range(NCH):
                    p.op("tensor", lambda h, k=k, sl=sl, bk=bk, th=th,
                         st=(k == 0), sp=(k == NCH - 1): h.matmul(
                             P[bk][:, 0:512], wslab_b[sl][:, k * 128:(k + 1) * 128],
                             s.xn[:, k * TT + th * 512: k * TT + th * 512 + 512],
                             start=st, stop=sp), inc="pg" if k == NCH - 1 else None)
            guard[("mwb", sl)] = p.semv["pg"]
            if oc < 32:
                p.wait("vector", "pg")
                if ("hpad",) in guard:
                    p.wait("vector", "sc", guard[("hpad",)])
                for th in range(2):
                    bk = 2 + 2 * sl + th
                    p.op("vector", lambda h, bk=bk, th=th: h.tensor_copy(
                        out=hpad[:, 3 + th * 512: 3 + th * 512 + 512],
                        in_=P[bk][:, 0:512]), inc="ev")
                    guard[("evac", bk)] = p.semv["ev"]
                if oc == 0:
                    p.op("vector", lambda h: h.memset(hpad[:, 0:3], 0.0))
                p.op("vector", lambda h, oc=oc: h.tensor_copy(
                    out=tailb[:, oc * 4:(oc + 1) * 4], in_=hpad[:, 1023:1027]))
                p.op("vector", lambda h, oc=oc: h.tensor_scalar(
                    out=acc[:], in0=hpad[:, 0:1024], scalar1=convwc[:, 4 * oc:4 * oc + 1],
                    scalar2=None, op0=ALU.mult))
                for kk in range(1, 4):
                    p.op("vector", lambda h, oc=oc, kk=kk: h.scalar_tensor_tensor(
                        out=acc[:], in0=hpad[:, kk:kk + 1024],
                        scalar=convwc[:, 4 * oc + kk:4 * oc + kk + 1], in1=acc[:],
                        op0=ALU.mult, op1=ALU.add))
                p.op("vector", lambda h, oc=oc: h.tensor_copy(
                    out=accsave[:, oc * 3:(oc + 1) * 3], in_=acc[:, 0:3]), inc="dv")
                p.wait("scalar", "dv")
                p.wait("scalar", f"yo{sl}")
                p.op("scalar", lambda h, oc=oc, sl=sl: h.activation(
                    out=obf[sl], in_=acc[:], func=AF.Silu,
                    bias=convbc[:, oc:oc + 1]), inc="sc")
                guard[("hpad",)] = p.semv["sc"]
                p.wait("gpsimd", "sc")
                p.dma("gpsimd", u_sp[oc * 128:(oc + 1) * 128, :], obf[sl], f"yo{sl}")
            else:
                p.wait("scalar", "pg")
                p.wait("scalar", f"yo{sl}")
                for th in range(2):
                    bk = 2 + 2 * sl + th
                    p.op("scalar", lambda h, bk=bk, th=th, sl=sl: h.activation(
                        out=obf[sl][:, th * 512:(th + 1) * 512], in_=P[bk][:, 0:512],
                        func=AF.Silu), inc="sc")
                    guard[("evac", bk)] = p.semv["sc"]
                p.wait("gpsimd", "sc")
                p.dma("gpsimd", sg_sp[(oc - 32) * 128:(oc - 31) * 128, :], obf[sl],
                      f"yo{sl}")
        for bkx in range(2, 6):
            guard.pop(("evac", bkx), None)
        p.wait("tensor", "sc")
        p.wait("gpsimd", "dv")
        p.dma("gpsimd", t_cc_in.rearrange("(k p) f -> p k f", p=128),
              tailb.rearrange("p (k f) -> p k f", k=32), "st0")
        p.wait("gpsimd", "st0")
        p.op("gpsimd", lambda h: h.collective_compute(
            "AllGather", ALU.bypass, replica_groups=[list(range(8))],
            ins=[t_cc_in[:]], outs=[t_cc_out[:]]), inc="cc")
        p.wait("gpsimd", "cc", p.semv["cc"])
        p.op("vector", lambda h: h.memset(ntf[:], 0.0), inc="dv")
        p.wait("sync", "cc", p.semv["cc"])
        for r in range(8):
            p.wait("sync", "dv", p.semv["dv"])
            p.dma("sync", ntb.rearrange("p (k f) -> p k f", k=32),
                  t_cc_out[r * I4:(r + 1) * I4, :].rearrange("(k p) f -> p k f", p=128),
                  "ldc")
            p.wait("vector", "ldc")
            p.op("vector", lambda h, r=r: h.scalar_tensor_tensor(
                out=ntf[:], in0=ntb, scalar=nbrmc[:, r:r + 1], in1=ntf[:],
                op0=ALU.mult, op1=ALU.add), inc="dv")
        for oc in range(32):
            sl = oc % 2
            for (t, kk) in ((0, 0), (0, 1), (0, 2), (1, 0), (1, 1), (2, 0)):
                src = 4 * oc + 1 + t + kk
                p.op("vector", lambda h, oc=oc, t=t, kk=kk, src=src:
                     h.scalar_tensor_tensor(
                         out=accsave[:, oc * 3 + t: oc * 3 + t + 1],
                         in0=ntf[:, src:src + 1],
                         scalar=convwc[:, 4 * oc + kk:4 * oc + kk + 1],
                         in1=accsave[:, oc * 3 + t: oc * 3 + t + 1],
                         op0=ALU.mult, op1=ALU.add))
            p.op("vector", lambda h: h.nop(), inc="dv")
            p.wait("scalar", "dv")
            p.wait("scalar", f"yo{sl}")
            p.op("scalar", lambda h, oc=oc, sl=sl: h.activation(
                out=obf[sl][:, 0:3], in_=accsave[:, oc * 3:(oc + 1) * 3],
                func=AF.Silu, bias=convbc[:, oc:oc + 1]), inc="sc")
            p.wait("gpsimd", "sc")
            p.dma("gpsimd", u_sp[oc * 128:(oc + 1) * 128, 0:3], obf[sl][:, 0:3],
                  f"yo{sl}")
        p.wait("sync", "yo0")
        p.wait("sync", "yo1")
        for th in range(2):
            bts, bB, bC = 2 + th, 4 + th, (6 if th == 0 else 0)
            bankw("tensor", bts, bB, bC)
            for k in range(32):
                sl = k % 2
                if ("uxs", sl) in guard:
                    p.wait("sync", "pg", guard[("uxs", sl)])
                p.dma("sync", u_sl[sl][:, 0:512],
                      u_sp[k * 128:(k + 1) * 128, th * 512:(th + 1) * 512], f"ld{sl}")
                p.wait("tensor", f"ld{sl}")
                xv = xpb.rearrange("p (k r) -> p k r", k=32)
                p.op("tensor", lambda h, k=k, sl=sl, bts=bts, xv=xv, st=(k == 0),
                     sp=(k == 31): h.matmul(
                         P[bts][:, 0:512], xv[:, k, 0:128],
                         u_sl[sl][:, 0:512], start=st, stop=sp))
                p.op("tensor", lambda h, k=k, sl=sl, bB=bB, xv=xv, st=(k == 0),
                     sp=(k == 31): h.matmul(
                         P[bB][0:16, 0:512], xv[:, k, 128:144],
                         u_sl[sl][:, 0:512], start=st, stop=sp))
                p.op("tensor", lambda h, k=k, sl=sl, bC=bC, xv=xv, st=(k == 0),
                     sp=(k == 31): h.matmul(
                         P[bC][0:16, 0:512], xv[:, k, 144:160],
                         u_sl[sl][:, 0:512], start=st, stop=sp), inc="pg")
                guard[("uxs", sl)] = p.semv["pg"]
            p.wait("vector", "pg")
            p.op("vector", lambda h, th=th, bts=bts: h.tensor_copy(
                out=tsb[:, th * 512:(th + 1) * 512], in_=P[bts][:, 0:512]))
            p.op("vector", lambda h, th=th, bB=bB: h.tensor_copy(
                out=bcrow_b[0:16, th * 512:(th + 1) * 512], in_=P[bB][0:16, 0:512]))
            p.op("vector", lambda h, th=th, bC=bC: h.tensor_copy(
                out=bcrow_b[32:48, th * 512:(th + 1) * 512], in_=P[bC][0:16, 0:512]),
                inc="dv")
            guard[("evacdv", bts)] = p.semv["dv"]
            guard[("evacdv", bB)] = p.semv["dv"]
            guard[("evacdv", bC)] = p.semv["dv"]
        guard[("us1pg",)] = p.semv["pg"]
        p.wait("tensor", "dv")
        bankw("tensor", 0, 1)
        for n in range(16):
            for th in range(2):
                p.op("tensor", lambda h, n=n, th=th: h.matmul(
                    P[0][:, 0:512], selt[0:16, n * 128:(n + 1) * 128],
                    bcrow_b[0:16, th * 512:(th + 1) * 512], start=True, stop=True))
                p.op("tensor", lambda h, n=n, th=th: h.matmul(
                    P[1][:, 0:512], selt[32:48, n * 128:(n + 1) * 128],
                    bcrow_b[32:48, th * 512:(th + 1) * 512], start=True, stop=True),
                    inc="pe")
                p.wait("vector", "pe")
                p.op("vector", lambda h, n=n, th=th: h.tensor_copy(
                    out=Bb_all[:, n * 1024 + th * 512: n * 1024 + th * 512 + 512],
                    in_=P[0][:, 0:512]))
                p.op("vector", lambda h, n=n, th=th: h.tensor_copy(
                    out=Cb_all[:, n * 1024 + th * 512: n * 1024 + th * 512 + 512],
                    in_=P[1][:, 0:512]), inc="dv")
                p.wait("tensor", "dv")
        def emit_dt(ic):
            sl = ic % 2
            if ("dts", sl) in guard:
                p.wait("sync", "cv", guard[("dts", sl)])
            p.dma("sync", dtpf[sl], dtpT[:, ic * 128:(ic + 1) * 128], f"wd{sl}")
            p.wait("gpsimd", f"wd{sl}")
            if ("dtb", sl) in guard:
                p.wait("gpsimd", "pg", guard[("dtb", sl)])
            p.op("gpsimd", lambda h, sl=sl: h.tensor_copy(out=dtsl_b[sl],
                                                          in_=dtpf[sl]), inc="cv")
            guard[("dts", sl)] = p.semv["cv"]
            p.wait("tensor", "cv")
            for th in range(2):
                bk = 5 + th
                bankw("tensor", bk)
                p.op("tensor", lambda h, sl=sl, th=th, bk=bk: h.matmul(
                    P[bk][:, 0:512], dtsl_b[sl],
                    tsb[:, th * 512:(th + 1) * 512], start=True, stop=True), inc="pg")
            guard[("dtb", sl)] = p.semv["pg"]
            v = p.semv["pg"]
            p.wait("vector", "pg", v)
            p.wait("scalar", "pg", v)
            for th in range(2):
                bk = 5 + th
                ts_ = slice(th * 512, (th + 1) * 512)
                p.op("vector", lambda h, bk=bk, ic=ic, ts_=ts_: h.tensor_scalar(
                    out=dtu[:, ts_], in0=P[bk][:, 0:512], scalar1=dtbc[:, ic:ic + 1],
                    scalar2=0.0, op0=ALU.add, op1=ALU.max), inc="dv")
                p.wait("scalar", "dv")
                p.op("scalar", lambda h, bk=bk, ic=ic, ts_=ts_: h.activation(
                    out=dtq[:, ts_], in_=P[bk][:, 0:512], func=AF.Abs,
                    bias=dtbc[:, ic:ic + 1]), inc="ev")
                guard[("evacdv", bk)] = p.semv["dv"]
                guard[("evac", bk)] = p.semv["ev"]
                p.op("scalar", lambda h, ts_=ts_: h.activation(
                    out=dtq[:, ts_], in_=dtq[:, ts_], func=AF.Exp, scale=-1.0))
                p.op("scalar", lambda h, ts_=ts_: h.activation(
                    out=dtq[:, ts_], in_=dtq[:, ts_], func=AF.Ln, bias=1.0), inc="sc")
                p.wait("vector", "sc")
                p.op("vector", lambda h, ts_=ts_: h.tensor_tensor(
                    out=dtq[:, ts_], in0=dtq[:, ts_], in1=dtu[:, ts_], op=ALU.add),
                    inc="dv")
        p.wait("vector", "dv")
        p.op("vector", lambda h: h.memset(zrt[:], 0.0), inc="dv")
        for ic in range(32):
            sl = ic % 2
            emit_dt(ic)
            p.wait("sync", "pg", guard[("us1pg",)])
            if ("us1", sl) in guard:
                p.wait("sync", "dv", guard[("us1", sl)])
            p.dma("sync", u_sl[sl], u_sp[ic * 128:(ic + 1) * 128, :], f"ld{sl}")
            p.wait("vector", f"ld{sl}")
            p.op("vector", lambda h, sl=sl: h.tensor_tensor(
                out=dtu[:], in0=dtq[:], in1=u_sl[sl], op=ALU.mult))
            p.op("vector", lambda h, sl=sl, ic=ic: h.tensor_scalar(
                out=yvt[:], in0=u_sl[sl], scalar1=Dcol[:, ic:ic + 1], scalar2=None,
                op0=ALU.mult), inc="dv")
            guard[("us1", sl)] = p.semv["dv"]
            p.op("vector", lambda h: h.tensor_reduce(
                out=sdtc[:, 0:1], in_=dtq[:], axis=AX.X, op=ALU.add))
            p.op("vector", lambda h, ic=ic: h.tensor_scalar(
                out=summ[:, ic * 32: ic * 32 + 16], in0=Acol[:, ic * 16:(ic + 1) * 16],
                scalar1=sdtc[:, 0:1], scalar2=None, op0=ALU.mult), inc="dv")
            p.wait("scalar", "dv")
            p.op("scalar", lambda h, ic=ic: h.activation(
                out=summ[:, ic * 32: ic * 32 + 16], in_=summ[:, ic * 32: ic * 32 + 16],
                func=AF.Exp), inc="sc")
            for n in range(16):
                p.wait("scalar", "dv", p.semv["dv"])
                p.op("scalar", lambda h, ic=ic, n=n: h.activation(
                    out=dAt[:], in_=dtq[:], func=AF.Exp,
                    scale=Acol[:, ic * 16 + n: ic * 16 + n + 1]), inc="sc")
                p.wait("vector", "sc")
                p.op("vector", lambda h, n=n: h.tensor_tensor(
                    out=dBxt[:], in0=dtu[:], in1=Bb_all[:, n * 1024:(n + 1) * 1024],
                    op=ALU.mult))
                p.op("vector", lambda h: h.tensor_tensor_scan(
                    out=hsct[:], data0=dAt[:], data1=dBxt[:], initial=0.0,
                    op0=ALU.mult, op1=ALU.add))
                p.op("vector", lambda h, ic=ic, n=n: h.tensor_copy(
                    out=summ[:, ic * 32 + 16 + n: ic * 32 + 17 + n],
                    in_=hsct[:, 1023:1024]))
                p.op("vector", lambda h, n=n: h.tensor_tensor(
                    out=hsct[:], in0=hsct[:], in1=Cb_all[:, n * 1024:(n + 1) * 1024],
                    op=ALU.mult))
                p.op("vector", lambda h: h.tensor_tensor(
                    out=yvt[:], in0=yvt[:], in1=hsct[:], op=ALU.add), inc="dv")
            sl2 = ic % 2
            p.wait("gpsimd", "dv")
            if ic >= 2:
                pass
            p.dma("gpsimd", y1_sp[ic * 128:(ic + 1) * 128, :], yvt[:], f"so{sl2}")
            if ic < 31:
                p.wait("vector", f"so{sl2}")
        p.wait("gpsimd", "dv")
        p.dma("gpsimd", s_cc_in.rearrange("(k p) f -> p k f", p=128),
              summ.rearrange("p (k f) -> p k f", k=32), "st1")
        p.wait("gpsimd", "st1")
        p.op("gpsimd", lambda h: h.collective_compute(
            "AllGather", ALU.bypass, replica_groups=[list(range(8))],
            ins=[s_cc_in[:]], outs=[s_cc_out[:]]), inc="cc")
        p.wait("gpsimd", "cc", p.semv["cc"])
        p.op("vector", lambda h: h.memset(hinb[:], 0.0), inc="dv")
        p.wait("sync", "cc", p.semv["cc"])
        for r in range(8):
            p.wait("sync", "dv", p.semv["dv"])
            p.dma("sync", dAt.rearrange("p (k f) -> p k f", k=32),
                  s_cc_out[r * I4:(r + 1) * I4, :].rearrange("(k p) f -> p k f", p=128),
                  "ldc")
            p.wait("vector", "ldc")
            Pv = dAt.rearrange("p (k f) -> p k f", k=32)[:, :, 0:16]
            Hv = dAt.rearrange("p (k f) -> p k f", k=32)[:, :, 16:32]
            p.op("vector", lambda h, Pv=Pv: h.tensor_tensor(
                out=dBxt[:, 0:512], in0=hinb[:].rearrange("p (k f) -> p k f", k=32)
                if False else hinb[:], in1=Pv, op=ALU.mult))
            p.op("vector", lambda h, Hv=Hv: h.tensor_tensor(
                out=dBxt[:, 0:512], in0=dBxt[:, 0:512], in1=Hv, op=ALU.add))
            p.op("vector", lambda h: h.tensor_tensor(
                out=dBxt[:, 0:512], in0=dBxt[:, 0:512], in1=hinb[:], op=ALU.subtract))
            p.op("vector", lambda h, r=r: h.tensor_scalar(
                out=dBxt[:, 0:512], in0=dBxt[:, 0:512], scalar1=rmaskc[:, r:r + 1],
                scalar2=None, op0=ALU.mult))
            p.op("vector", lambda h: h.tensor_tensor(
                out=hinb[:], in0=hinb[:], in1=dBxt[:, 0:512], op=ALU.add), inc="dv")
        p.wait("sync", "so0")
        p.wait("sync", "so1")
        guard[("ygdone",)] = p.semv["dv"]
        p.wait("vector", "dv")
        p.op("vector", lambda h: h.memset(zrt[:], 0.0), inc="dv")
        for ic in range(32):
            sl = ic % 2
            emit_dt(ic)
            if ("ygdone",) in guard:
                p.wait("sync", "dv", guard[("ygdone",)])
            if ("sg2", sl) in guard:
                p.wait("sync", "dv", guard[("sg2", sl)])
            p.dma("sync", yvt[:], y1_sp[ic * 128:(ic + 1) * 128, :], f"ld{sl}")
            p.dma("sync", u_sl[sl], sg_sp[ic * 128:(ic + 1) * 128, :], f"ld{sl}")
            p.wait("vector", f"ld{sl}")
            for n in range(16):
                p.wait("scalar", "dv", p.semv["dv"])
                p.op("scalar", lambda h, ic=ic, n=n: h.activation(
                    out=dAt[:], in_=dtq[:], func=AF.Exp,
                    scale=Acol[:, ic * 16 + n: ic * 16 + n + 1]), inc="sc")
                p.wait("vector", "sc")
                p.op("vector", lambda h, ic=ic, n=n: h.tensor_tensor_scan(
                    out=hsct[:], data0=dAt[:], data1=zrt[:],
                    initial=hinb[:, ic * 16 + n: ic * 16 + n + 1],
                    op0=ALU.mult, op1=ALU.add))
                p.op("vector", lambda h, n=n: h.tensor_tensor(
                    out=hsct[:], in0=hsct[:], in1=Cb_all[:, n * 1024:(n + 1) * 1024],
                    op=ALU.mult))
                p.op("vector", lambda h: h.tensor_tensor(
                    out=yvt[:], in0=yvt[:], in1=hsct[:], op=ALU.add), inc="dv")
            p.wait("vector", f"so{sl}")
            p.op("vector", lambda h, sl=sl: h.tensor_tensor(
                out=obf[sl], in0=yvt[:], in1=u_sl[sl], op=ALU.mult), inc="dv")
            guard[("ygdone",)] = p.semv["dv"]
            guard[("sg2", sl)] = p.semv["dv"]
            p.wait("gpsimd", "dv")
            p.dma("gpsimd", yg_sp[ic * 128:(ic + 1) * 128, :], obf[sl], f"so{sl}")
        if dbg == "mamba":
            p.wait("gpsimd", "so0")
            p.wait("gpsimd", "so1")
            p.wait("gpsimd", "yo0")
            p.wait("gpsimd", "yo1")
            p.dma("gpsimd", dbg_u, u_sp, "st0")
            p.dma("gpsimd", dbg_sg, sg_sp, "st0")
            p.dma("gpsimd", dbg_yg, yg_sp, "st0")
            p.wait("gpsimd", "dv")
            p.op("gpsimd", lambda h: h.tensor_copy(out=A[0:64, 45312:46336],
                                                   in_=bcrow_b[:, :]), inc="cv")
            p.op("gpsimd", lambda h: h.tensor_copy(out=A[:, 46336:47360],
                                                   in_=tsb[:]), inc="cv")
            p.wait("gpsimd", "cv")
            p.dma("gpsimd", dbg_xp, xpb, "st0")
            p.dma("gpsimd", dbg_bc, A[0:64, 45312:46336], "st0")
            p.dma("gpsimd", dbg_ts, A[:, 46336:47360], "st0")
            p.wait("gpsimd", "st0")
        p.wait("sync", "so0")
        p.wait("sync", "so1")
        for oc in range(16):
            osl = oc % 2
            bks = (2 + 2 * osl, 3 + 2 * osl)
            for half in range(2):
                hsl = half
                if ("ow", osl, hsl) in guard:
                    p.wait("sync", "cv", guard[("ow", osl, hsl)])
                p.dma("sync", wf_sl[hsl].rearrange("p (k i) -> p k i", k=16),
                      wopT[half * 2048:(half + 1) * 2048, oc * 128:(oc + 1) * 128]
                      .rearrange("(k p) i -> p k i", p=128), f"wd{hsl}")
                p.wait("gpsimd", f"wd{hsl}")
                if ("owb", osl, hsl) in guard:
                    p.wait("gpsimd", "pg", guard[("owb", osl, hsl)])
                p.op("gpsimd", lambda h, hsl=hsl: h.tensor_copy(
                    out=wslab_b[hsl], in_=wf_sl[hsl]), inc="cv")
                guard[("ow", osl, hsl)] = p.semv["cv"]
                p.wait("tensor", "cv")
                if half == 0:
                    bankw("tensor", *bks)
                for k in range(16):
                    kk = half * 16 + k
                    sl = kk % 2
                    if ("ygs", sl) in guard:
                        p.wait("sync", "pg", guard[("ygs", sl)])
                    p.dma("sync", u_sl[sl], yg_sp[kk * 128:(kk + 1) * 128, :],
                          f"ld{sl}")
                    p.wait("tensor", f"ld{sl}")
                    for th in range(2):
                        p.op("tensor", lambda h, k=k, kk=kk, sl=sl, bk=bks[th],
                             th=th, hsl=hsl: h.matmul(
                                 P[bk][:, 0:512], wslab_b[hsl][:, k * 128:(k + 1) * 128],
                                 u_sl[sl][:, th * 512:(th + 1) * 512],
                                 start=(kk == 0), stop=(kk == 31)),
                             inc="pg" if th == 1 else None)
                    guard[("ygs", sl)] = p.semv["pg"]
                guard[("owb", osl, hsl)] = p.semv["pg"]
            chains_done = p.semv["pg"]
            p.wait("vector", "pg", chains_done)
            for th in range(2):
                residual_from_psum(bks[th], oc, th * 512, imm=1.0)

    if "g2" in phases:
        ghost_sublayer()

    l1st = {}

    def l1ffn_sublayer():
        dma_barrier()
        if not l1st:
            l1st["gcol"] = p.sbuf("gcol", [128, 64], F32)
            l1st["ucol"] = p.sbuf("ucol", [128, 64], F32)
            l1st["dcolh"] = p.sbuf("dcolh", [128, 16], F32)
            p.dma("sync", l1st["gcol"][:], gsc.rearrange("(k p) f -> p (k f)", p=128),
                  "ldc", slow=True)
            p.dma("sync", l1st["ucol"][:], usc.rearrange("(k p) f -> p (k f)", p=128),
                  "ldc", slow=True)
            p.dma("sync", l1st["dcolh"][:], dsc.rearrange("(k p) f -> p (k f)", p=128),
                  "ldc", slow=True)
            p.wait("vector", "ldc")
            p.op("vector", lambda h: h.tensor_scalar(
                out=l1st["dcolh"][:], in0=l1st["dcolh"][:], scalar1=0.5, scalar2=None,
                op0=ALU.mult), inc="dv")
        gcol, ucol, dcolh = l1st["gcol"], l1st["ucol"], l1st["dcolh"]
        g8 = idx_u8[:, 0:2048]
        u8_ = idx_u8[:, 2048:4096]
        d8 = [idx_u8[:, 4096 + k * 512: 4096 + (k + 1) * 512] for k in range(2)]
        wgb = A[:, 32768:34816]
        wub = A[:, 34816:36864]
        wdb = [A[:, 40960 + k * 512: 40960 + (k + 1) * 512] for k in range(2)]
        xsb = A[0:1, 38912:39936]
        hsb = A[0:1, 39936:40960]
        ln_into_xn()
        p.wait("sync", "dv")
        for th in range(2):
            t0 = th * 512
            bankw("tensor", 0, 1)
            p.wait("tensor", "dv")
            for c in range(NCH):
                p.op("tensor", lambda h, c=c, t0=t0, st=(c == 0),
                     sp=(c == NCH - 1): h.matmul(
                         pa[0:1, 0:512], s.ones_l[:, 0:1],
                         s.xn[:, c * TT + t0: c * TT + t0 + 512], start=st, stop=sp),
                     inc="pe" if c == NCH - 1 else None)
            p.wait("vector", "pe")
            p.op("vector", lambda h, th=th: h.tensor_copy(out=xsb[0:1, 0:512],
                 in_=pa[0:1, 0:512]), inc="dv")
            guard[("evacdv", 0)] = p.semv["dv"]
            for ic in range(64):
                sl = ic % 2
                if ("l1w",) in guard:
                    p.wait("sync", "cv", guard[("l1w",)])
                p.dma("sync", g8.rearrange("p (k i) -> p k i", k=16),
                      gwT[:, ic * 128:(ic + 1) * 128].rearrange("(k p) i -> p k i",
                                                                p=128), "wd0")
                p.dma("sync", u8_.rearrange("p (k i) -> p k i", k=16),
                      uwT[:, ic * 128:(ic + 1) * 128].rearrange("(k p) i -> p k i",
                                                                p=128), "wd1")
                p.wait("gpsimd", "wd0")
                if ("l1wb",) in guard:
                    p.wait("gpsimd", "pg", guard[("l1wb",)])
                p.op("gpsimd", lambda h: h.tensor_copy(out=wgb, in_=g8))
                p.wait("gpsimd", "wd1")
                p.op("gpsimd", lambda h: h.tensor_copy(out=wub, in_=u8_), inc="cv")
                guard[("l1w",)] = p.semv["cv"]
                bg, bu = 2 + sl, 4 + sl
                p.wait("tensor", "cv")
                p.wait("tensor", "dv")
                bankw("tensor", bg, bu)
                for (bk, wb) in ((bg, wgb), (bu, wub)):
                    for k in range(NCH):
                        p.op("tensor", lambda h, k=k, bk=bk, wb=wb, t0=t0,
                             st=(k == 0): h.matmul(
                                 P[bk][:, 0:512], wb[:, k * 128:(k + 1) * 128],
                                 s.xn[:, k * TT + t0: k * TT + t0 + 512],
                                 start=st, stop=False))
                    p.op("tensor", lambda h, bk=bk, t0=t0: h.matmul(
                        P[bk][:, 0:512], m128[0:1, :], xsb[0:1, 0:512],
                        start=False, stop=True), inc="pg")
                guard[("l1wb",)] = p.semv["pg"]
                p.wait("scalar", "pg")
                if "gsil" in guard:
                    p.wait("scalar", "ev", guard["gsil"])
                p.op("scalar", lambda h, bg=bg, ic=ic: h.activation(
                    out=gsil[:, 0:512], in_=P[bg][:, 0:512], func=AF.Silu,
                    scale=gcol[:, ic:ic + 1]), inc="sc")
                p.wait("vector", "sc")
                p.wait("vector", f"yo{sl}")
                if ("hsum", sl) in guard:
                    p.wait("vector", "pg", guard[("hsum", sl)])
                p.op("vector", lambda h, bu=bu, ic=ic, sl=sl: h.scalar_tensor_tensor(
                    out=obf[sl][:, 0:512], in0=P[bu][:, 0:512],
                    scalar=ucol[:, ic:ic + 1], in1=gsil[:, 0:512],
                    op0=ALU.mult, op1=ALU.mult), inc="ev")
                guard[("evac", bg)] = p.semv["ev"]
                guard[("evac", bu)] = p.semv["ev"]
                guard["gsil"] = p.semv["ev"]
                p.wait("tensor", "ev", p.semv["ev"])
                p.op("tensor", lambda h, ic=ic, st=(ic == 0), sp=(ic == 63): h.matmul(
                    pb[0:1, 0:512], s.ones_l[:, 0:1], obf[ic % 2][:, 0:512],
                    start=st, stop=sp), inc="pg")
                guard[("hsum", sl)] = p.semv["pg"]
                p.wait("gpsimd", "ev", p.semv["ev"])
                p.dma("gpsimd", h_sp[ic * 128:(ic + 1) * 128, t0:t0 + 512],
                      obf[sl][:, 0:512], f"yo{sl}")
            p.wait("vector", "pg")
            p.op("vector", lambda h: h.tensor_copy(out=hsb[0:1, 0:512],
                 in_=pb[0:1, 0:512]), inc="dv")
            guard[("evacdv", 1)] = p.semv["dv"]
            if dbg == "l1" and th == 1:
                p.wait("gpsimd", "yo0")
                p.wait("gpsimd", "yo1")
                p.dma("gpsimd", dbg_h1, h_sp, "st0")
                p.wait("gpsimd", "dv")
                p.dma("gpsimd", dbg_xs[0:1, 0:512], hsb[0:1, 0:512], "st0")
                p.wait("gpsimd", "st0")
            p.wait("sync", "yo0")
            p.wait("sync", "yo1")
            banks = [6, 0, 1, 2]
            for og in range(4):
                bankw("tensor", *banks)
                p.wait("tensor", "dv", p.semv["dv"])
                for ic in range(64):
                    sl = ic % 2
                    if ("l1d", sl) in guard:
                        p.wait("sync", "cv", guard[("l1d", sl)])
                    p.wait("sync", f"wde{sl}")
                    p.dma("sync", d8[sl],
                          dwT[ic * 128:(ic + 1) * 128, og * 512:(og + 1) * 512],
                          f"wde{sl}")
                    if ("l1h", sl) in guard:
                        p.wait("sync", "pg", guard[("l1h", sl)])
                    p.dma("sync", u_sl[sl][:, 0:512],
                          h_sp[ic * 128:(ic + 1) * 128, t0:t0 + 512], f"ld{sl}")
                    p.wait("gpsimd", f"wde{sl}")
                    if ("l1db", sl) in guard:
                        p.wait("gpsimd", "pg", guard[("l1db", sl)])
                    p.op("gpsimd", lambda h, sl=sl: h.tensor_copy(out=wdb[sl],
                                                                  in_=d8[sl]),
                         inc="cv")
                    guard[("l1d", sl)] = p.semv["cv"]
                    p.wait("tensor", "cv")
                    p.wait("tensor", f"ld{sl}")
                    for j in range(4):
                        p.op("tensor", lambda h, b=banks[j], j=j, sl=sl, ic=ic:
                             h.matmul(P[b][:, 0:512],
                                      wdb[sl][:, j * 128:(j + 1) * 128],
                                      u_sl[sl][:, 0:512],
                                      start=(ic == 0), stop=False),
                             inc="pg" if j == 3 else None)
                    guard[("l1db", sl)] = p.semv["pg"]
                    guard[("l1h", sl)] = p.semv["pg"]
                for j in range(4):
                    p.op("tensor", lambda h, b=banks[j]: h.matmul(
                        P[b][:, 0:512], m128[0:1, :], hsb[0:1, 0:512],
                        start=False, stop=True), inc="pg" if j == 3 else None)
                chains_done = p.semv["pg"]
                p.wait("vector", "pg", chains_done)
                for j in range(4):
                    oc = og * 4 + j
                    residual_from_psum(banks[j], oc, t0, scale_col=dcolh[:, oc:oc + 1])

    if "f1" in phases:
        l1ffn_sublayer()

    if "attn" in phases:
        dma_barrier()
        nwcol = p.sbuf("nwcol", [128, 32], F32)
        hflagc = p.sbuf("hflagc", [128, 1], F32)
        identt = p.sbuf("identt", [128, 128], BF16)
        permb = p.sbuf("permb", [128, 128], BF16)
        scol = p.sbuf("scol", [128, 1], F32)
        nbs = p.sbuf("nbs", [128, 32], mybir.dt.uint32)

        att_o = A[:, 0:16384]
        maskt = A[:, 16384:25600]
        kbuf = A[:, 25600:27648]
        vbuf = A[:, 27648:29696]
        VTt = A[:, 29696:31744]
        qh_t = A[:, 31744:32768]
        qnb = A[:, 32768:33792]
        Pb = A[:, 33792:35072]
        PT = A[:, 35072:36224]
        wob = [A[:, 36864 + k * 2048: 36864 + (k + 1) * 2048] for k in range(2)]
        cosA = Fa[:, 6144:7168]
        sinA = Fa[:, 7168:8192]
        t1A = Fa[:, 0:1024]
        t2A = Fa[:, 2048:3072]
        qnf = Fa[:, 3072:4096]
        qswf = Fa[:, 4096:5120]
        SA = Fa[:, 0:1280]
        mxA = Fa[:, 1280:1288]
        wof = [Fa[:, 0 + k * 2048: (k + 1) * 2048] for k in range(2)]
        p.dma("sync", nwcol[:, 0:16], qnw.rearrange("(k p) f -> p (k f)", p=128),
              "ldc", slow=True)
        p.dma("sync", nwcol[:, 16:32], knw.rearrange("(k p) f -> p (k f)", p=128),
              "ldc", slow=True)
        p.dma("sync", hflagc[:], hflag, "ldc")
        p.dma("sync", identt[:], identb_i, "ldc")
        p.dma("sync", permb[:], permT, "ldc")
        p.dma("sync", nbs[:], nbro, "ldc")
        p.op("vector", lambda h: h.memset(scol[:], 2 ** -3.5), inc="dv")
        ln_into_xn()
        p.wait("sync", "dv")
        p.dma("sync", maskt.rearrange("p (q r) -> p q r", q=8),
              maskb.rearrange("p (q r) -> p q r", q=8), "ldc")
        for oc in range(48):
            sl = oc % 2
            if ("aw", sl) in guard:
                p.wait("sync", "cv", guard[("aw", sl)])
            p.dma("sync", wof[sl].rearrange("p (k i) -> p k i", k=16),
                  wqkvT[:, oc * 128:(oc + 1) * 128].rearrange("(k p) i -> p k i",
                                                             p=128), f"wd{sl}")
            p.wait("gpsimd", f"wd{sl}")
            if ("awb", sl) in guard:
                p.wait("gpsimd", "pg", guard[("awb", sl)])
            p.op("gpsimd", lambda h, sl=sl: h.tensor_copy(out=wob[sl], in_=wof[sl]),
                 inc="cv")
            guard[("aw", sl)] = p.semv["cv"]
            p.wait("tensor", "cv")
            p.wait("tensor", "dv")
            for th in range(2):
                bk = 2 + 2 * sl + th
                bankw("tensor", bk)
                for k in range(NCH):
                    p.op("tensor", lambda h, k=k, sl=sl, bk=bk, th=th,
                         st=(k == 0), sp=(k == NCH - 1): h.matmul(
                             P[bk][:, 0:512], wob[sl][:, k * 128:(k + 1) * 128],
                             s.xn[:, k * TT + th * 512: k * TT + th * 512 + 512],
                             start=st, stop=sp), inc="pg" if k == NCH - 1 else None)
            guard[("awb", sl)] = p.semv["pg"]
            p.wait("vector", "pg")
            p.wait("vector", f"yo{sl}")
            for th in range(2):
                bk = 2 + 2 * sl + th
                p.op("vector", lambda h, bk=bk, th=th, sl=sl: h.tensor_copy(
                    out=obf[sl][:, th * 512:(th + 1) * 512], in_=P[bk][:, 0:512]),
                    inc="ev")
                guard[("evac", bk)] = p.semv["ev"]
            p.wait("gpsimd", "ev")
            if oc < 16:
                p.dma("gpsimd", qraw_sp[oc * 128:(oc + 1) * 128, :], obf[sl], f"yo{sl}")
            elif oc < 32:
                p.dma("gpsimd", kraw_sp[(oc - 16) * 128:(oc - 15) * 128, :], obf[sl],
                      f"yo{sl}")
            else:
                p.dma("gpsimd", kv_cc_in[D + (oc - 32) * 128: D + (oc - 31) * 128, :],
                      obf[sl], f"yo{sl}")
        if dbg == "xr":
            p.wait("gpsimd", "ev")
            p.wait("gpsimd", "dv")
            p.dma("gpsimd", dbg_x1, XR[:], "st1")
        p.wait("sync", "yo0")
        p.wait("sync", "yo1")
        for which, raw_sp, wofs in (("q", qraw_sp, 0), ("k", kraw_sp, 16)):
            p.wait("sync", "dv")
            for th in range(2):
                bankw("tensor", 0)
                p.wait("tensor", "dv")
                for c in range(NCH):
                    sl = c % 2
                    if ("aq", sl) in guard:
                        p.wait("sync", "dv", guard[("aq", sl)])
                    p.dma("sync", u_sl[sl][:, 0:512],
                          raw_sp[c * 128:(c + 1) * 128, th * 512:(th + 1) * 512],
                          f"ld{sl}")
                    p.wait("vector", f"ld{sl}")
                    if ("aq2", sl) in guard:
                        p.wait("vector", "pe", guard[("aq2", sl)])
                    p.op("vector", lambda h, sl=sl: h.tensor_tensor(
                        out=qnb[:, sl * 512:(sl + 1) * 512], in0=u_sl[sl][:, 0:512],
                        in1=u_sl[sl][:, 0:512], op=ALU.mult), inc="dv")
                    guard[("aq", sl)] = p.semv["dv"]
                    p.wait("tensor", "dv")
                    p.op("tensor", lambda h, sl=sl, st=(c == 0),
                         sp=(c == NCH - 1): h.matmul(
                             pa[0:1, 0:512], s.ones_l[:, 0:1],
                             qnb[:, sl * 512:(sl + 1) * 512], start=st, stop=sp),
                         inc="pe")
                    guard[("aq2", sl)] = p.semv["pe"]
                p.wait("vector", "pe")
                p.op("vector", lambda h, th=th: h.tensor_scalar(
                    out=s.r1[0:1, th * 512:(th + 1) * 512], in0=pa[0:1, 0:512],
                    scalar1=1.0 / D, scalar2=None, op0=ALU.mult), inc="dv")
                guard[("evacdv", 0)] = p.semv["dv"]
            p.wait("scalar", "dv")
            p.op("scalar", lambda h: h.activation(out=s.r2[0:1, 0:TT],
                                                  in_=s.r1[0:1, 0:TT], func=AF.Ln,
                                                  bias=s.eps6[0:1, 0:1]))
            p.op("scalar", lambda h: h.activation(out=s.r3[0:1, 0:TT],
                                                  in_=s.r2[0:1, 0:TT], func=AF.Exp,
                                                  scale=-0.5), inc="sc")
            for th in range(2):
                p.wait("tensor", "sc")
                p.wait("tensor", "dv")
                bankw("tensor", 1)
                p.op("tensor", lambda h, th=th: h.matmul(
                    pb[:, 0:512], s.ones_c[0:1, :], s.r3[0:1, th * 512:(th + 1) * 512],
                    start=True, stop=True), inc="pe")
                p.wait("vector", "pe")
                p.op("vector", lambda h, th=th: h.tensor_copy(
                    out=s.rb[:, th * 512:(th + 1) * 512], in_=pb[:, 0:512]), inc="dv")
                guard[("evacdv", 1)] = p.semv["dv"]
            if which == "q":
                p.dma("sync", cosA[:], cosb, "ldc")
                p.dma("sync", sinA[:], sinb, "ldc")
            p.wait("sync", "dv")
            for c in range(NCH):
                sl = c % 2
                if ("ar", sl) in guard:
                    p.wait("sync", "dv", guard[("ar", sl)])
                p.dma("sync", u_sl[sl], raw_sp[c * 128:(c + 1) * 128, :], f"ld{sl}")
                p.wait("vector", f"ld{sl}")
                p.wait("vector", "ldc")
                p.op("vector", lambda h, sl=sl, c=c, wofs=wofs: h.scalar_tensor_tensor(
                    out=qnf[:], in0=u_sl[sl], scalar=nwcol[:, wofs + c: wofs + c + 1],
                    in1=s.rb[:, 0:TT], op0=ALU.mult, op1=ALU.mult))
                guard[("ar", sl)] = p.semv["dv"] + 1
                p.op("vector", lambda h: h.tensor_copy(out=qnb, in_=qnf[:]), inc="dv")
                for th in range(2):
                    p.wait("tensor", "dv")
                    bankw("tensor", 6)
                    p.op("tensor", lambda h, th=th: h.matmul(
                        P[6][:, 0:512], permb[:], qnb[:, th * 512:(th + 1) * 512],
                        start=True, stop=True), inc="pe")
                    p.wait("vector", "pe")
                    p.op("vector", lambda h, th=th: h.tensor_copy(
                        out=qswf[:, th * 512:(th + 1) * 512], in_=P[6][:, 0:512]),
                        inc="ev")
                    guard[("evac", 6)] = p.semv["ev"]
                p.op("vector", lambda h: h.tensor_tensor(
                    out=t1A[:], in0=qnf[:], in1=cosA[:], op=ALU.mult))
                p.op("vector", lambda h: h.tensor_tensor(
                    out=t2A[:], in0=qswf[:], in1=sinA[:], op=ALU.mult))
                p.wait("vector", f"yo{sl}")
                p.op("vector", lambda h, sl=sl: h.tensor_tensor(
                    out=obf[sl], in0=t1A[:], in1=t2A[:], op=ALU.add), inc="dv")
                p.wait("gpsimd", "dv")
                dst = (q_sp if which == "q" else kv_cc_in)
                p.dma("gpsimd", dst[c * 128:(c + 1) * 128, :], obf[sl], f"yo{sl}")
        if dbg == "xr":
            p.wait("gpsimd", "dv")
            p.dma("gpsimd", dbg_x2, XR[:], "st1")
        p.wait("gpsimd", "yo0")
        p.wait("gpsimd", "yo1")
        p.op("gpsimd", lambda h: h.collective_compute(
            "AllGather", ALU.bypass, replica_groups=[list(range(8))],
            ins=[kv_cc_in[:]], outs=[kv_cc_out[:]]), inc="cc")
        p.wait("gpsimd", "cc", p.semv["cc"])

        p.wait("sync", "ldc")
        for hh in range(16):
            if hh > 0:
                p.wait("sync", "pg", guard[("heads",)])
                p.wait("gpsimd", "pg", guard[("heads",)])
            p.dma("sync", qh_t, q_sp[hh * 128:(hh + 1) * 128, :], "ld0")
            p.dma("sync", kbuf[:, 1024:2048], kv_cc_in[hh * 128:(hh + 1) * 128, :],
                  "ld0")
            p.dma("sync", vbuf[:, 1024:2048],
                  kv_cc_in[D + hh * 128: D + (hh + 1) * 128, :], "ld0")
            p.op("gpsimd", lambda h, hh=hh: h.indirect_dma_start(
                out=kbuf[:, 0:1024], out_offset=None, in_=kv_cc_out,
                in_offset=bass.IndirectOffsetOnAxis(ap=nbs[:, hh:hh + 1], axis=0)),
                inc="st0", k=16)
            p.op("gpsimd", lambda h, hh=hh: h.indirect_dma_start(
                out=vbuf[:, 0:1024], out_offset=None, in_=kv_cc_out,
                in_offset=bass.IndirectOffsetOnAxis(ap=nbs[:, 16 + hh:17 + hh],
                                                    axis=0)), inc="st0", k=16)
            p.wait("vector", "st0")
            p.wait("vector", "ld0")
            p.op("vector", lambda h: h.tensor_scalar(
                out=kbuf[:, 0:1024], in0=kbuf[:, 0:1024], scalar1=hflagc[:, 0:1],
                scalar2=None, op0=ALU.mult))
            p.op("vector", lambda h: h.tensor_scalar(
                out=vbuf[:, 0:1024], in0=vbuf[:, 0:1024], scalar1=hflagc[:, 0:1],
                scalar2=None, op0=ALU.mult), inc="dv")
            for j in range(16):
                p.wait("tensor", "dv")
                if ("vt",) in guard:
                    p.wait("tensor", "ev", guard[("vt",)])
                p.op("tensor", lambda h, j=j: h.transpose(
                    pvt[:, 0:128], vbuf[:, j * 128:(j + 1) * 128], identt[:]),
                    inc="pe")
                p.wait("vector", "pe")
                p.op("vector", lambda h, j=j: h.tensor_copy(
                    out=VTt[:, j * 128:(j + 1) * 128], in_=pvt[:, 0:128]), inc="ev")
                guard[("vt",)] = p.semv["ev"]
            for qb in range(8):
                segs = ((0, 512, 2), (512, 512, 3), (1024, 128, 4))
                for (o, n, bk) in segs:
                    bankw("tensor", bk)
                    p.op("tensor", lambda h, o=o, n=n, bk=bk, qb=qb: h.matmul(
                        P[bk][:, 0:n], qh_t[:, qb * 128:(qb + 1) * 128],
                        kbuf[:, qb * 128 + o: qb * 128 + o + n],
                        start=True, stop=True), inc="pg" if bk == 4 else None)
                p.wait("vector", "pg")
                for (o, n, bk) in segs:
                    p.op("vector", lambda h, o=o, n=n, bk=bk, qb=qb:
                         h.scalar_tensor_tensor(
                             out=SA[:, o:o + n], in0=P[bk][:, 0:n],
                             scalar=scol[:, 0:1],
                             in1=maskt[:, qb * 1152 + o: qb * 1152 + o + n],
                             op0=ALU.mult, op1=ALU.add),
                         inc="ev" if bk == 4 else None)
                guard[("evac", 2)] = guard[("evac", 3)] = guard[("evac", 4)] = \
                    p.semv["ev"]
                p.op("vector", lambda h: h.reduce_max(out=mxA[:, 0:1],
                                                      in_=SA[:, 0:1152], axis=AX.X))
                p.op("vector", lambda h: h.tensor_scalar(
                    out=mxA[:, 1:2], in0=mxA[:, 0:1], scalar1=-1.0, scalar2=None,
                    op0=ALU.mult), inc="dv")
                p.wait("scalar", "dv")
                p.op("scalar", lambda h: h.activation(
                    out=Pb[:, 0:1152], in_=SA[:, 0:1152], func=AF.Exp,
                    bias=mxA[:, 1:2]), inc="sc")
                p.wait("vector", "sc")
                p.op("vector", lambda h: h.reduce_sum(out=mxA[:, 2:3],
                                                      in_=Pb[:, 0:1152], axis=AX.X))
                p.op("vector", lambda h: h.reciprocal(out=mxA[:, 3:4],
                                                      in_=mxA[:, 2:3]))
                p.op("vector", lambda h: h.tensor_scalar(
                    out=Pb[:, 0:1152], in0=Pb[:, 0:1152], scalar1=mxA[:, 3:4],
                    scalar2=None, op0=ALU.mult), inc="dv")
                for kc in range(9):
                    p.wait("tensor", "dv")
                    if ("pt",) in guard:
                        p.wait("tensor", "ev", guard[("pt",)])
                    p.op("tensor", lambda h, kc=kc: h.transpose(
                        ptr[:, 0:128], Pb[:, kc * 128:(kc + 1) * 128], identt[:]),
                        inc="pe")
                    p.wait("vector", "pe")
                    p.op("vector", lambda h, kc=kc: h.tensor_copy(
                        out=PT[:, kc * 128:(kc + 1) * 128], in_=ptr[:, 0:128]),
                        inc="ev")
                    guard[("pt",)] = p.semv["ev"]
                bankw("tensor", 5)
                p.wait("tensor", "ev", guard[("pt",)])
                for kc in range(9):
                    p.op("tensor", lambda h, kc=kc, qb=qb: h.matmul(
                        P[5][0:128, 0:128], VTt[:, (qb + kc) * 128:(qb + kc + 1) * 128],
                        PT[:, kc * 128:(kc + 1) * 128],
                        start=(kc == 0), stop=(kc == 8)), inc="pg" if kc == 8 else None)
                p.wait("vector", "pg")
                p.op("vector", lambda h, hh=hh, qb=qb: h.tensor_copy(
                    out=att_o[:, hh * 1024 + qb * 128: hh * 1024 + (qb + 1) * 128],
                    in_=P[5][0:128, 0:128]), inc="ev")
                guard[("evac", 5)] = p.semv["ev"]
            guard[("heads",)] = p.semv["pg"]
        if dbg == "at":
            p.wait("gpsimd", "ev")
            p.dma("gpsimd", dbg_q, q_sp, "st1")
            p.dma("gpsimd", dbg_k, kv_cc_in[0:D, :], "st1")
            p.dma("gpsimd", dbg_v, kv_cc_in[D:2 * D, :], "st1")
            p.dma("gpsimd", dbg_ao, att_o, "st1")
            p.wait("gpsimd", "st1")
        if dbg == "xr":
            p.wait("gpsimd", "ev")
            p.wait("gpsimd", "dv")
            p.dma("gpsimd", dbg_x3, XR[:], "st1")
            p.wait("gpsimd", "st1")
        p.wait("sync", "ev")
        for oc in range(16):
            sl = oc % 2
            if ("wo", sl) in guard:
                p.wait("sync", "cv", guard[("wo", sl)])
            p.dma("sync", wof[sl].rearrange("p (k i) -> p k i", k=16),
                  woT[:, oc * 128:(oc + 1) * 128].rearrange("(k p) i -> p k i", p=128),
                  f"wd{sl}")
            p.wait("gpsimd", f"wd{sl}")
            if ("wob", sl) in guard:
                p.wait("gpsimd", "pg", guard[("wob", sl)])
            p.op("gpsimd", lambda h, sl=sl: h.tensor_copy(out=wob[sl], in_=wof[sl]),
                 inc="cv")
            guard[("wo", sl)] = p.semv["cv"]
            p.wait("tensor", "cv")
            p.wait("tensor", "ev")
            for th in range(2):
                bk = 2 + 2 * sl + th
                bankw("tensor", bk)
                for k in range(NCH):
                    p.op("tensor", lambda h, k=k, sl=sl, bk=bk, th=th,
                         st=(k == 0), sp=(k == NCH - 1): h.matmul(
                             P[bk][:, 0:512], wob[sl][:, k * 128:(k + 1) * 128],
                             att_o[:, k * 1024 + th * 512: k * 1024 + th * 512 + 512],
                             start=st, stop=sp), inc="pg" if k == NCH - 1 else None)
            guard[("wob", sl)] = p.semv["pg"]
            p.wait("vector", "pg")
            for th in range(2):
                bk = 2 + 2 * sl + th
                residual_from_psum(bk, oc, th * 512, imm=1.0)

    if "f2" in phases:
        l1ffn_sublayer()

    def finalize():
        p.wait("gpsimd", "ev")
        p.wait("gpsimd", "dv")
        for c in range(NCH):
            p.dma("gpsimd", yT[c * 128:(c + 1) * 128, :],
                  XR[:, c * TT:(c + 1) * TT], f"st{c % 2}")
        p.wait("gpsimd", "st0")
        p.wait("gpsimd", "st1")

    return p, locals(), finalize



def make_attn_masks8(core):
    m = np.full((8, 128, 1152), -1e30, dtype=np.float32)
    for qb in range(8):
        v = min(core * 8 + qb, 8)
        for ql in range(128):
            lo = ql + 1
            if v < 8:
                lo = max(lo, 1024 - v * 128)
            m[qb, ql, lo:ql + 1025] = 0.0
    return np_bf16(m.transpose(1, 0, 2).reshape(128, 8 * 1152))


def make_rope_tables(t0, T):
    pos = np.arange(t0, t0 + T, dtype=np.float64)
    inv = 1.0 / (10000.0 ** (np.arange(0, 128, 2, dtype=np.float64) / 128.0))
    fr = pos[None, :] * inv[:, None]
    cos = np.cos(fr).astype(np.float32)
    sin = np.sin(fr).astype(np.float32)
    return (np.ascontiguousarray(np.concatenate([cos, cos], axis=0)),
            np.ascontiguousarray(np.concatenate([sin, sin], axis=0)))


def make_perm():
    Pm = np.zeros((128, 128), dtype=np.float32)
    for d in range(64):
        Pm[d, 64 + d] = -1.0
        Pm[64 + d, d] = 1.0
    return np_bf16(Pm.T)


def make_sel16():
    selv = np.zeros((64, 16 * 128), dtype=np.float32)
    for n in range(16):
        selv[n, n * 128:(n + 1) * 128] = 1.0
        selv[32 + n, n * 128:(n + 1) * 128] = 1.0
    return np_bf16(selv)


def stage_mega(core, ins):
    L = ins
    c = core
    xT = np.ascontiguousarray(np.asarray(L["L0_attn_x"])[0].T[:, c * TT:(c + 1) * TT]
                              .astype(np.float32))
    gT = np.asarray(L["L0_ffn_gate_idx"]).T
    uT = np.asarray(L["L0_ffn_up_idx"]).T
    dT = np.asarray(L["L0_ffn_down_idx"]).T
    rm = np.zeros((128, 8), np.float32)
    rm[:, 0:c] = 1.0
    nb = np.zeros((128, 8), np.float32)
    if c > 0:
        nb[:, c - 1] = 1.0
    cosq, sinq = make_rope_tables(c * TT, TT)
    return {
        "xT": xT,
        "gidxT_s": np.ascontiguousarray(gT[c * 256:(c + 1) * 256]).astype(np.uint8),
        "uidxT_s": np.ascontiguousarray(uT[c * 256:(c + 1) * 256]).astype(np.uint8),
        "didxT_s": np.ascontiguousarray(dT[:, c * 256:(c + 1) * 256]).astype(np.uint8),
        "lut": np.ascontiguousarray(np.asarray(L["L0_ffn_lut"]),
                                    dtype=np.float32).reshape(1, 16),
        "wipT": np.ascontiguousarray(np.asarray(L["L0_attn_in_proj_w"]).T
                                     .astype(np.float32)),
        "convw": np.ascontiguousarray(np.asarray(L["L0_attn_conv_w"])[:, 0, :]
                                      .astype(np.float32)),
        "convb": np.ascontiguousarray(np.asarray(L["L0_attn_conv_b"])
                                      .astype(np.float32).reshape(-1, 1)),
        "xpT": np.ascontiguousarray(np.asarray(L["L0_attn_x_proj_w"]).T
                                    .astype(np.float32)),
        "dtpT": np.ascontiguousarray(np.asarray(L["L0_attn_dt_proj_w"]).T
                                     .astype(np.float32)),
        "dtb": np.ascontiguousarray(np.asarray(L["L0_attn_dt_proj_b"])
                                    .astype(np.float32).reshape(-1, 1)),
        "Alog": np.ascontiguousarray(np.asarray(L["L0_attn_A_log"]).astype(np.float32)),
        "Dvec": np.ascontiguousarray(np.asarray(L["L0_attn_D"])
                                     .astype(np.float32).reshape(-1, 1)),
        "wopT": np.ascontiguousarray(np.asarray(L["L0_attn_out_proj_w"]).T
                                     .astype(np.float32)),
        "sel": make_sel16(),
        "rmask": rm,
        "nbrm": nb,
        "nbro": (np.array([[((c + 7) % 8) * 2 * D + hh * 128 + pp
                            for hh in range(16)] +
                           [((c + 7) % 8) * 2 * D + D + hh * 128 + pp
                            for hh in range(16)]
                           for pp in range(128)], dtype=np.uint32)),
        "gwT": np.ascontiguousarray(np.asarray(L["L1_ffn_gate_w"]).T).astype(np.uint8),
        "uwT": np.ascontiguousarray(np.asarray(L["L1_ffn_up_w"]).T).astype(np.uint8),
        "dwT": np.ascontiguousarray(np.asarray(L["L1_ffn_down_w"]).T).astype(np.uint8),
        "gsc": np.ascontiguousarray(np.asarray(L["L1_ffn_gate_scale"]),
                                    dtype=np.float32).reshape(-1, 1),
        "usc": np.ascontiguousarray(np.asarray(L["L1_ffn_up_scale"]),
                                    dtype=np.float32).reshape(-1, 1),
        "dsc": np.ascontiguousarray(np.asarray(L["L1_ffn_down_scale"]),
                                    dtype=np.float32).reshape(-1, 1),
        "wqkvT": np.ascontiguousarray(np.asarray(L["L1_attn_w_qkv"]).T
                                      .astype(np.float32)),
        "qnw": np.ascontiguousarray(np.asarray(L["L1_attn_q_norm_w"])
                                    .astype(np.float32).reshape(-1, 1)),
        "knw": np.ascontiguousarray(np.asarray(L["L1_attn_k_norm_w"])
                                    .astype(np.float32).reshape(-1, 1)),
        "woT": np.ascontiguousarray(np.asarray(L["L1_attn_w_o"]).T.astype(np.float32)),
        "cosb": cosq,
        "sinb": sinq,
        "permT": make_perm(),
        "maskb": make_attn_masks8(c),
        "identb_i": np_bf16(np.eye(128)),
        "hflag": np.full((128, 1), 1.0 if c > 0 else 0.0, np.float32),
    }




from concourse.bass_utils import run_bass_kernel_spmd

_MEGA = {}


def kernel(**inputs):
    if "nc" not in _MEGA:
        p, env, fin = build_mega()
        fin()
        _MEGA["nc"] = p.build()
    nc = _MEGA["nc"]
    L = {k: np.asarray(v) for k, v in inputs.items()}
    in_maps = [stage_mega(c, L) for c in range(8)]
    rs = run_bass_kernel_spmd(nc, in_maps, core_ids=list(range(8))).results
    yT = np.concatenate([r["yT"] for r in rs], axis=1)
    return np.ascontiguousarray(yT.T).reshape(1, 8192, 2048).astype(np.float32)
